# version 01
# baseline (speedup 1.0000x reference)
import contextlib
import numpy as np
import concourse.bass as bass
import concourse.mybir as mybir
from concourse.bass_utils import run_bass_kernel_spmd

AF = mybir.ActivationFunctionType
ALU = mybir.AluOpType
AX = mybir.AxisListType
F32 = mybir.dt.float32
BF16 = mybir.dt.bfloat16

B_TOT = 16
B_LOC = 2
L = 2048
D = 1024
N_CORES = 8


_CHAN_ENG = {
    "pe": "tensor", "act": "scalar", "dve": "vector", "pool": "gpsimd",
}


class KB:

    def __init__(self, nc, es):
        self.nc = nc
        self.es_root = es
        self.es_stack = [es]
        self.sem = {}
        self.seq = {c: 0 for c in _CHAN_ENG}
        self.incs = {c: [] for c in _CHAN_ENG}
        self.inst = {c: {} for c in _CHAN_ENG}
        self.dma_seq = {}
        self.dma_queue = {}
        self.clock = {e: {} for e in ("tensor", "scalar", "vector", "gpsimd", "sync")}
        self.tstate = {}
        for c in _CHAN_ENG:
            self.sem[c] = es.enter_context(nc.semaphore("S_" + c))

    def eng(self, name):
        return getattr(self.nc, {"tensor": "tensor", "scalar": "scalar",
                                 "vector": "vector", "gpsimd": "gpsimd",
                                 "sync": "sync"}[name])

    def _tname(self, ap):
        t = ap.tensor if hasattr(ap, "tensor") else ap
        return t.name

    def _dep_key(self, ap):
        return self._tname(ap)

    def _edge(self, waits, chan_or_stream, seq):
        cur = waits.get(chan_or_stream, 0)
        if seq > cur:
            waits[chan_or_stream] = seq

    def _gather(self, reads, writes):
        waits = {}
        for ap in reads:
            st = self.tstate.get(self._dep_key(ap))
            if st and st[0] is not None:
                self._edge(waits, st[0][0], st[0][1])
        for ap in writes:
            st = self.tstate.get(self._dep_key(ap))
            if st:
                if st[0] is not None:
                    self._edge(waits, st[0][0], st[0][1])
                for c, s in st[1].items():
                    self._edge(waits, c, s)
        return waits

    def _wait_val(self, chan, seq):
        if chan in _CHAN_ENG:
            incs = self.incs[chan]
            if not incs or seq > incs[-1]:
                ins = self.inst[chan].get(seq)
                assert ins is not None, f"no instr {chan}#{seq}"
                ins.then_inc(self.sem[chan], 1)
                incs.append(seq)
                return len(incs)
            import bisect
            i = bisect.bisect_left(incs, seq)
            return i + 1
        return 16 * self.dma_seq[chan]

    def _emit_waits(self, engine, waits, exclude_self_pe=False):
        e = self.eng(engine)
        for chan, seq in sorted(waits.items()):
            if exclude_self_pe and chan == "pe" and engine == "tensor":
                continue
            val = self._wait_val(chan, seq)
            if self.clock[engine].get(chan, 0) < val:
                e.wait_ge(self.sem[chan], val)
                self.clock[engine][chan] = val

    def _record(self, chan, seq, reads, writes):
        for ap in reads:
            k = self._dep_key(ap)
            st = self.tstate.setdefault(k, [None, {}])
            st[1][chan] = seq
        for ap in writes:
            k = self._dep_key(ap)
            self.tstate[k] = [(chan, seq), {}]

    def op(self, chan, fn, reads, writes):
        if chan == "pool":
            chan = "dve"
        engine = _CHAN_ENG[chan]
        waits = self._gather(reads, writes)
        self._emit_waits(engine, waits, exclude_self_pe=(chan == "pe"))
        ins = fn(self.eng(engine))
        self.seq[chan] += 1
        self.inst[chan][self.seq[chan]] = ins
        self._record(chan, self.seq[chan], reads, writes)
        return ins

    def dma(self, dst, src, stream="ld", queue="sync"):
        if stream not in self.dma_seq:
            self.dma_seq[stream] = 0
            self.dma_queue[stream] = queue
            self.sem[stream] = self.es_root.enter_context(
                self.nc.semaphore("S_" + stream))
        assert self.dma_queue[stream] == queue
        waits = self._gather([src], [dst])
        if self.dma_seq[stream]:
            self._edge(waits, stream, self.dma_seq[stream])
        self._emit_waits(queue, waits)
        ins = self.eng(queue).dma_start(out=dst, in_=src)
        ins.then_inc(self.sem[stream], 16)
        self.dma_seq[stream] += 1
        for ap in [src]:
            self.tstate.setdefault(self._dep_key(ap), [None, {}])[1][stream] = \
                self.dma_seq[stream]
        self.tstate[self._dep_key(dst)] = [(stream, self.dma_seq[stream]), {}]
        return ins

    def barrier(self):
        for stream, n in sorted(self.dma_seq.items()):
            if n:
                val = 16 * n
                if self.clock["sync"].get(stream, 0) < val:
                    self.nc.sync.wait_ge(self.sem[stream], val)
                    self.clock["sync"][stream] = val
        for c in _CHAN_ENG:
            if self.seq[c]:
                val = self._wait_val(c, self.seq[c])
                if self.clock["sync"].get(c, 0) < val:
                    self.nc.sync.wait_ge(self.sem[c], val)
                    self.clock["sync"][c] = val
        self.nc.all_engine_barrier()
        for k in self.tstate:
            self.tstate[k] = [None, {}]
        for e in self.clock:
            for c in list(self.clock[e]):
                pass

    def tt(self, out, a, b, op, chan="dve"):
        return self.op(chan, lambda e: e.tensor_tensor(out, a, b, op),
                       [a, b], [out])

    def tadd(self, out, a, b, chan="dve"):
        return self.tt(out, a, b, ALU.add, chan)

    def tmul(self, out, a, b, chan="dve"):
        return self.tt(out, a, b, ALU.mult, chan)

    def tsub(self, out, a, b, chan="dve"):
        return self.tt(out, a, b, ALU.subtract, chan)

    def ts(self, out, in0, s1, s2, op0, op1=ALU.bypass, chan="dve"):
        reads = [in0] + [s for s in (s1, s2) if hasattr(s, "tensor")]
        return self.op(chan, lambda e: e.tensor_scalar(out, in0, s1, s2, op0, op1),
                       reads, [out])

    def stt(self, out, in0, scalar, in1, op0, op1, chan="dve"):
        reads = [in0, in1] + ([scalar] if hasattr(scalar, "tensor") else [])
        return self.op(chan,
                       lambda e: e.scalar_tensor_tensor(out, in0, scalar, in1, op0, op1),
                       reads, [out])

    def copy(self, out, in_, chan="dve"):
        if chan == "act":
            return self.op(chan, lambda e: e.copy(out, in_), [in_], [out])
        return self.op(chan, lambda e: e.tensor_copy(out, in_), [in_], [out])

    def memset(self, ap, v, chan="dve"):
        return self.op(chan, lambda e: e.memset(ap, v), [], [ap])

    def act(self, out, in_, func, bias=0.0, scale=1.0, accum_out=None):
        reads = [in_] + [b for b in (bias, scale) if hasattr(b, "tensor")]
        writes = [out] + ([accum_out] if accum_out is not None else [])
        return self.op("act",
                       lambda e: e.activation(out, in_, func, bias=bias,
                                              scale=scale, accum_out=accum_out),
                       reads, writes)

    def red(self, out, in_, op, negate=False, chan="dve"):
        return self.op(chan,
                       lambda e: e.tensor_reduce(out, in_, AX.X, op, negate=negate),
                       [in_], [out])

    def recip(self, out, in_):
        return self.op("dve", lambda e: e.reciprocal(out, in_), [in_], [out])

    def scan(self, out, d0, d1, init, op0=ALU.mult, op1=ALU.add, chan="dve"):
        reads = [d0, d1] + ([init] if hasattr(init, "tensor") else [])
        return self.op(chan,
                       lambda e: e.tensor_tensor_scan(out, d0, d1, init, op0, op1),
                       reads, [out])

    def mm(self, out, lhsT, rhs, start, stop):
        return self.op("pe", lambda e: e.matmul(out, lhsT, rhs, start=start, stop=stop),
                       [lhsT, rhs], [out])

    def trans(self, out_psum, in_sb, ident):
        return self.op("pe", lambda e: e.transpose(out_psum, in_sb, ident),
                       [in_sb, ident], [out_psum])

    def select(self, out, mask, on_true, on_false):
        return self.op("dve", lambda e: e.select(out, mask, on_true, on_false),
                       [mask, on_true, on_false], [out])

    def max8(self, out, in_):
        return self.op("dve", lambda e: e.max(out=out, in_=in_), [in_], [out])

    _serial = 0

    def _uniq(self, name):
        KB._serial += 1
        return f"{name}_{KB._serial}"

    def sbuf(self, name, shape, dtype):
        return self.es_stack[-1].enter_context(
            self.nc.sbuf_tensor(self._uniq(name), shape, dtype))

    def psum(self, name, shape, dtype=F32):
        return self.es_stack[-1].enter_context(
            self.nc.psum_tensor(self._uniq(name), shape, dtype))

    @contextlib.contextmanager
    def scope(self):
        es = contextlib.ExitStack()
        self.es_stack.append(es)
        try:
            yield
        finally:
            self.barrier()
            self.es_stack.pop()
            es.close()



def load_vec_chunks(kb, name, v_hbm, n, stream="ld"):
    nch = n // 128
    t = kb.sbuf(name, [128, nch], F32)
    ap = v_hbm.ap()
    for c in range(nch):
        kb.dma(t[:, c:c + 1], ap[c * 128:(c + 1) * 128].unsqueeze(1), stream=stream)
    return t


def load_w_T(kb, name, w_hbm, O, I, ident, stream="wld", stage=None, wT=None):
    OC, IC = O // 128, I // 128
    if wT is None:
        wT = kb.sbuf(name, [128, IC * O], BF16)
    if stage is None:
        stage = (kb.sbuf(name + "_stg", [128, I], F32),
                 kb.sbuf(name + "_stgb", [128, I], BF16),
                 kb.psum(name + "_stgp", [128, 512], BF16))
    stg, stgb, stgp = stage
    ap = w_hbm if isinstance(w_hbm, bass.AP) else w_hbm.ap()
    for oc in range(OC):
        kb.dma(stg[:, 0:I], ap[oc * 128:(oc + 1) * 128, :], stream=stream)
        kb.copy(stgb[:, 0:I], stg[:, 0:I], chan="pool")
        for ic4 in range((IC + 3) // 4):
            ics = range(ic4 * 4, min(ic4 * 4 + 4, IC))
            for j, ic in enumerate(ics):
                kb.trans(stgp[:, j * 128:(j + 1) * 128],
                         stgb[:, ic * 128:(ic + 1) * 128], ident)
            for j, ic in enumerate(ics):
                kb.copy(wT[:, ic * O + oc * 128: ic * O + (oc + 1) * 128],
                        stgp[:, j * 128:(j + 1) * 128])
    return wT


def load_w_direct(kb, name, w_hbm, I, O, stream="wld", wsb=None, stage=None):
    IC = I // 128
    if wsb is None:
        wsb = kb.sbuf(name, [128, IC * O], BF16)
    if stage is None:
        stage = kb.sbuf(name + "_stg", [128, O], F32)
    ap = w_hbm if isinstance(w_hbm, bass.AP) else w_hbm.ap()
    for ic in range(IC):
        kb.dma(stage[:, 0:O], ap[ic * 128:(ic + 1) * 128, :], stream=stream)
        kb.copy(wsb[:, ic * O:(ic + 1) * O], stage[:, 0:O], chan="pool")
    return wsb


def bcast_row(kb, name, row_sb, n, ones1, psum=None, out=None, dtype=F32):
    if out is None:
        out = kb.sbuf(name, [128, n], dtype)
    own = psum is None
    if own:
        psum = kb.psum(name + "_p", [128, 512], F32)
    for j in range(0, n, 512):
        w = min(512, n - j)
        kb.mm(psum[:, 0:w], ones1, row_sb[:, j:j + w], start=True, stop=True)
        kb.copy(out[:, j:j + w], psum[:, 0:w])
    return out


def ln_tiles_to_fm(kb, x_src_ap, hT, tiles, Tcols, ident, g_bcast=None, b_bcast=None,
                   eps=1e-5, xtm_out=None, stream="xld", prefix="ln",
                   out_dtype=BF16, tp=None):
    xt = kb.sbuf(prefix + "_xt", [128, 1024], F32)
    xn = kb.sbuf(prefix + "_xn", [128, 1024], out_dtype)
    mneg = kb.sbuf(prefix + "_mneg", [128, 1], F32)
    ssum = kb.sbuf(prefix + "_ss", [128, 1], F32)
    sq = kb.sbuf(prefix + "_sq", [128, 1024], F32)
    var = kb.sbuf(prefix + "_var", [128, 1], F32)
    rstd = kb.sbuf(prefix + "_rstd", [128, 1], F32)
    if tp is None:
        tp = kb.psum(prefix + "_tp", [128, 512], BF16)
    for j, ti in enumerate(tiles):
        kb.dma(xt[:], x_src_ap[ti * 128:(ti + 1) * 128, :], stream=stream)
        kb.red(mneg[:], xt[:], ALU.add, negate=True)
        kb.ts(mneg[:], mneg[:], 1.0 / 1024, None, ALU.mult)
        kb.act(sq[:], xt[:], AF.Square, bias=mneg[:], accum_out=ssum[:])
        kb.ts(var[:], ssum[:], 1.0 / 1024, eps, ALU.mult, ALU.add)
        kb.act(rstd[:], var[:], AF.Ln)
        kb.act(rstd[:], rstd[:], AF.Exp, scale=-0.5)
        if g_bcast is None:
            kb.ts(xn[:], xt[:], mneg[:], rstd[:], ALU.add, ALU.mult)
        else:
            kb.ts(sq[:], xt[:], mneg[:], rstd[:], ALU.add, ALU.mult)
            kb.tmul(sq[:], sq[:], g_bcast[:])
            kb.tadd(xn[:], sq[:], b_bcast[:])
        if xtm_out is not None:
            kb.copy(xtm_out[:, j * 1024:(j + 1) * 1024], xn[:], chan="pool")
        for c4 in range(2):
            for j2 in range(4):
                c = c4 * 4 + j2
                kb.trans(tp[:, j2 * 128:(j2 + 1) * 128],
                         xn[:, c * 128:(c + 1) * 128], ident)
            for j2 in range(4):
                c = c4 * 4 + j2
                kb.copy(hT[:, c * Tcols + j * 128: c * Tcols + (j + 1) * 128],
                        tp[:, j2 * 128:(j2 + 1) * 128])



def build_l0_attn_a(kb, x_src, xres, consts, W):
    nc = kb.nc
    ident = consts["ident"]
    ones1 = consts["ones1"]
    with kb.scope():
        bq = load_vec_chunks(kb, "aa_bq", W["L0_attn_a_bq"], 1024)
        bk = load_vec_chunks(kb, "aa_bk", W["L0_attn_a_bk"], 1024)
        bv_row = kb.sbuf("aa_bvrow", [1, 1024], BF16)
        bv_stg = kb.sbuf("aa_bvstg", [1, 1024], F32)
        kb.dma(bv_stg[:], W["L0_attn_a_bv"].ap().unsqueeze(0), stream="ld")
        kb.copy(bv_row[:], bv_stg[:])
        bvB = bcast_row(kb, "aa_bvB", bv_row, 1024, ones1)

        wstage = (kb.sbuf("aa_wstg", [128, 1024], F32),
                  kb.sbuf("aa_wstgb", [128, 1024], BF16),
                  kb.psum("aa_wstgp", [128, 512], BF16))
        wT = kb.sbuf("aa_wT", [128, 8 * 1024], BF16)

        hT = kb.sbuf("aa_hT", [128, 8 * L], BF16)
        qT = kb.sbuf("aa_qT", [128, 8 * L], BF16)
        kT = kb.sbuf("aa_kT", [128, 8 * L], BF16)
        vtm = kb.sbuf("aa_vtm", [128, (L // 128) * 1024], BF16)
        pp = kb.psum("aa_pp", [128, 512], F32)
        pp2 = kb.psum("aa_pp2", [128, 512], F32)
        S = kb.sbuf("aa_S", [128, L], F32)
        P = kb.sbuf("aa_P", [128, L], BF16)
        PT = kb.sbuf("aa_PT", [128, L], BF16)
        ptp = kb.psum("aa_ptp", [128, 512], BF16)
        mrow = kb.sbuf("aa_m", [128, 1], F32)
        srow = kb.sbuf("aa_s", [128, 1], F32)
        rrow = kb.sbuf("aa_r", [128, 1], F32)
        pav = kb.psum("aa_pav", [128, 512], F32)
        pav2 = kb.psum("aa_pav2", [128, 512], F32)
        otm = kb.sbuf("aa_otm", [128, 1024], F32)
        xt2 = kb.sbuf("aa_xt2", [128, 1024], F32)

        LT = L // 128
        for b in range(B_LOC):
            tiles = list(range(b * LT, (b + 1) * LT))
            ln_tiles_to_fm(kb, x_src, hT, tiles, L, ident, prefix=f"aa_ln{b}",
                           tp=ptp)
            for wname, bias, dstT in (("L0_attn_a_wq", bq, qT), ("L0_attn_a_wk", bk, kT)):
                load_w_T(kb, "aa_w", W[wname], 1024, 1024, ident, wT=wT, stage=wstage)
                for co in range(8):
                    for ti2, t0 in enumerate(range(0, L, 512)):
                        tw = min(512, L - t0)
                        ps = pp if ti2 % 2 == 0 else pp2
                        for ci in range(8):
                            kb.mm(ps[:, 0:tw], wT[:, ci * 1024 + co * 128: ci * 1024 + (co + 1) * 128],
                                  hT[:, ci * L + t0: ci * L + t0 + tw],
                                  start=(ci == 0), stop=(ci == 7))
                        kb.ts(dstT[:, co * L + t0: co * L + t0 + tw],
                              ps[:, 0:tw], bias[:, co:co + 1], None, ALU.add)
            load_w_T(kb, "aa_w", W["L0_attn_a_wv"], 1024, 1024, ident, wT=wT, stage=wstage)
            for tt_ in range(LT):
                for oh in range(2):
                    ps = pp if oh == 0 else pp2
                    for ci in range(8):
                        kb.mm(ps[:], hT[:, ci * L + tt_ * 128: ci * L + (tt_ + 1) * 128],
                              wT[:, ci * 1024 + oh * 512: ci * 1024 + (oh + 1) * 512],
                              start=(ci == 0), stop=(ci == 7))
                    kb.copy(vtm[:, tt_ * 1024 + oh * 512: tt_ * 1024 + (oh + 1) * 512], ps[:])
            for qt in range(LT):
                for ki2, k0 in enumerate(range(0, L, 512)):
                    kw = min(512, L - k0)
                    ps = pp if ki2 % 2 == 0 else pp2
                    for ci in range(8):
                        kb.mm(ps[:, 0:kw], qT[:, ci * L + qt * 128: ci * L + (qt + 1) * 128],
                              kT[:, ci * L + k0: ci * L + k0 + kw],
                              start=(ci == 0), stop=(ci == 7))
                    kb.ts(S[:, k0:k0 + kw], ps[:, 0:kw], 1.0 / 32.0, None, ALU.mult)
                kb.red(mrow[:], S[:], ALU.max, negate=True)
                kb.act(P[:], S[:], AF.Exp, bias=mrow[:], accum_out=srow[:])
                kb.recip(rrow[:], srow[:])
                for kt4 in range(0, LT, 4):
                    kts = range(kt4, min(kt4 + 4, LT))
                    for j, kt in enumerate(kts):
                        kb.trans(ptp[:, j * 128:(j + 1) * 128],
                                 P[:, kt * 128:(kt + 1) * 128], ident)
                    for j, kt in enumerate(kts):
                        kb.copy(PT[:, kt * 128:(kt + 1) * 128],
                                ptp[:, j * 128:(j + 1) * 128])
                for oh in range(2):
                    ps = pav if oh == 0 else pav2
                    for kt in range(LT):
                        kb.mm(ps[:], PT[:, kt * 128:(kt + 1) * 128],
                              vtm[:, kt * 1024 + oh * 512: kt * 1024 + (oh + 1) * 512],
                              start=(kt == 0), stop=(kt == LT - 1))
                    kb.stt(otm[:, oh * 512:(oh + 1) * 512], ps[:], rrow[:],
                           bvB[:, oh * 512:(oh + 1) * 512], ALU.mult, ALU.add)
                ti = b * LT + qt
                kb.dma(xt2[:], x_src[ti * 128:(ti + 1) * 128, :], stream="xld2")
                kb.tadd(otm[:], otm[:], xt2[:], chan="pool")
                kb.dma(xres[ti * 128:(ti + 1) * 128, :], otm[:], stream="xst",
                       queue="gpsimd")
        kb.barrier()


def build_l0_ffn(kb, x_src, xres, consts, W, tag):
    nc = kb.nc
    ident = consts["ident"]
    T = B_LOC * L
    NT = T // 128
    TC = T // 512
    hT_d = nc.dram_tensor(f"moe_hT{tag}", [8, 128, T], BF16, kind="Internal")
    acc_d = nc.dram_tensor(f"moe_acc{tag}", [T, 1024], F32, kind="Internal")
    with kb.scope():
        with kb.scope():
            hT = kb.sbuf("moe_hT", [128, 8 * T], BF16)
            tp = kb.psum("moe_tp", [128, 512], BF16)
            ln_tiles_to_fm(kb, x_src, hT, list(range(NT)), T, ident,
                           prefix="moe_ln", tp=tp)
            for c in range(8):
                kb.dma(hT_d.ap()[c], hT[:, c * T:(c + 1) * T], stream="hst",
                       queue="gpsimd")
        kb.barrier()
        comb = kb.sbuf("moe_comb", [128, NT * 8], F32)
        with kb.scope():
            gstg = kb.sbuf("moe_gstg", [128, 8], F32)
            gate = kb.sbuf("moe_gate", [128, 8 * 8], BF16)
            for ci in range(8):
                kb.dma(gstg[:], W["L0_ffn_gate_w"].ap()[ci * 128:(ci + 1) * 128, :],
                       stream="wld")
                kb.copy(gate[:, ci * 8:(ci + 1) * 8], gstg[:], chan="pool")
            htc = kb.sbuf("moe_htcB", [128, 8 * 512], BF16)
            pg = kb.psum("moe_pg", [128, 8], F32)
            lg = kb.sbuf("moe_lg", [128, 8], F32)
            top = kb.sbuf("moe_top", [128, 8], F32)
            p1 = kb.sbuf("moe_p1", [128, 1], F32)
            p2 = kb.sbuf("moe_p2", [128, 1], F32)
            m1 = kb.sbuf("moe_m1", [128, 8], F32)
            m2 = kb.sbuf("moe_m2", [128, 8], F32)
            for tc in range(TC):
                for c in range(8):
                    kb.dma(htc[:, c * 512:(c + 1) * 512],
                           hT_d.ap()[c][:, tc * 512:(tc + 1) * 512], stream="hld")
                for tt in range(4):
                    ti = tc * 4 + tt
                    for ci in range(8):
                        kb.mm(pg[:], htc[:, ci * 512 + tt * 128: ci * 512 + (tt + 1) * 128],
                              gate[:, ci * 8:(ci + 1) * 8], start=(ci == 0), stop=(ci == 7))
                    kb.copy(lg[:], pg[:])
                    kb.max8(top[:], lg[:])
                    kb.tsub(p1[:], top[:, 0:1], top[:, 1:2])
                    kb.act(p1[:], p1[:], AF.Sigmoid)
                    kb.ts(p2[:], p1[:], -1.0, 1.0, ALU.mult, ALU.add)
                    kb.ts(m1[:], lg[:], top[:, 0:1], None, ALU.is_equal)
                    kb.ts(m2[:], lg[:], top[:, 1:2], None, ALU.is_equal)
                    kb.ts(m1[:], m1[:], p1[:], None, ALU.mult)
                    kb.stt(comb[:, ti * 8:(ti + 1) * 8], m2[:], p2[:], m1[:],
                           ALU.mult, ALU.add)
        kb.barrier()
        experts = [("S", W["L0_ffn_Sg"], W["L0_ffn_Su"], W["L0_ffn_Sd"], 1408, None)]
        for e in range(8):
            experts.append((f"E{e}", W["L0_ffn_Wg"], W["L0_ffn_Wu"],
                            W["L0_ffn_Wd"], 2816, e))
        wg_sb = kb.sbuf("moe_wg", [128, 8 * 2816], BF16)
        wu_sb = kb.sbuf("moe_wu", [128, 8 * 2816], BF16)
        wd_sb = kb.sbuf("moe_wd", [128, 22 * 1024], BF16)
        wstg = kb.sbuf("moe_wstg", [128, 2816], F32)
        htcs = [kb.sbuf(f"moe_htc{i}", [128, 8 * 512], BF16) for i in range(2)]
        sg = kb.sbuf("moe_sg", [128, 22 * 512], BF16)
        sgg = kb.sbuf("moe_sgg", [128, 512], BF16)
        pga = kb.psum("moe_pga", [128, 512], F32)
        pgb = kb.psum("moe_pgb", [128, 512], F32)
        pda = kb.psum("moe_pda", [128, 512], F32)
        pdb = kb.psum("moe_pdb", [128, 512], F32)
        atile = kb.sbuf("moe_at", [128, 1024], F32)
        otile = kb.sbuf("moe_ot", [128, 1024], F32)
        for name, Wg_h, Wu_h, Wd_h, I, e in experts:
            IC = I // 128
            for w_h, dst, O in ((Wg_h, wg_sb, I), (Wu_h, wu_sb, I)):
                ap = w_h.ap() if e is None else w_h.ap()[e]
                for ci in range(8):
                    kb.dma(wstg[:, 0:O], ap[ci * 128:(ci + 1) * 128, :], stream="wld")
                    kb.copy(dst[:, ci * I:ci * I + O], wstg[:, 0:O], chan="pool")
            ap = Wd_h.ap() if e is None else Wd_h.ap()[e]
            for oc in range(IC):
                kb.dma(wstg[:, 0:1024], ap[oc * 128:(oc + 1) * 128, :], stream="wld")
                kb.copy(wd_sb[:, oc * 1024:(oc + 1) * 1024], wstg[:, 0:1024], chan="pool")
            for tc in range(TC):
                htc = htcs[tc % 2]
                for c in range(8):
                    kb.dma(htc[:, c * 512:(c + 1) * 512],
                           hT_d.ap()[c][:, tc * 512:(tc + 1) * 512], stream="hld")
                for oc in range(IC):
                    for ci in range(8):
                        kb.mm(pga[:], wg_sb[:, ci * I + oc * 128: ci * I + (oc + 1) * 128],
                              htc[:, ci * 512:(ci + 1) * 512], start=(ci == 0), stop=(ci == 7))
                    for ci in range(8):
                        kb.mm(pgb[:], wu_sb[:, ci * I + oc * 128: ci * I + (oc + 1) * 128],
                              htc[:, ci * 512:(ci + 1) * 512], start=(ci == 0), stop=(ci == 7))
                    kb.act(sgg[:], pga[:], AF.Silu)
                    kb.tmul(sg[:, oc * 512:(oc + 1) * 512], sgg[:], pgb[:])
                for tt in range(4):
                    ti = tc * 4 + tt
                    if e is not None:
                        kb.dma(atile[:], acc_d.ap()[ti * 128:(ti + 1) * 128, :],
                               stream="ald")
                    for half in range(2):
                        pd = pda if half == 0 else pdb
                        for oc in range(IC):
                            kb.mm(pd[:], sg[:, oc * 512 + tt * 128: oc * 512 + (tt + 1) * 128],
                                  wd_sb[:, oc * 1024 + half * 512: oc * 1024 + (half + 1) * 512],
                                  start=(oc == 0), stop=(oc == IC - 1))
                        if e is None:
                            kb.copy(otile[:, half * 512:(half + 1) * 512], pd[:])
                        else:
                            kb.stt(otile[:, half * 512:(half + 1) * 512], pd[:],
                                   comb[:, ti * 8 + e: ti * 8 + e + 1],
                                   atile[:, half * 512:(half + 1) * 512],
                                   ALU.mult, ALU.add)
                    kb.dma(acc_d.ap()[ti * 128:(ti + 1) * 128, :], otile[:],
                           stream="ast", queue="gpsimd")
        with kb.scope():
            xt = kb.sbuf("moe_xt", [128, 1024], F32)
            at2 = kb.sbuf("moe_at2", [128, 1024], F32)
            for ti in range(NT):
                kb.dma(xt[:], x_src[ti * 128:(ti + 1) * 128, :], stream="xld2")
                kb.dma(at2[:], acc_d.ap()[ti * 128:(ti + 1) * 128, :], stream="ald")
                kb.tadd(at2[:], at2[:], xt[:], chan="pool")
                kb.dma(xres[ti * 128:(ti + 1) * 128, :], at2[:], stream="xst",
                       queue="gpsimd")
        kb.barrier()


def build_l1_attn_b(kb, x_src, xres, consts, W):
    nc = kb.nc
    ident = consts["ident"]
    ones1 = consts["ones1"]
    tri = consts["tri"]
    LT = L // 128
    with kb.scope():
        bq = load_vec_chunks(kb, "mb_bq", W["L1_attn_b_b_q"], 1024)
        bk = load_vec_chunks(kb, "mb_bk", W["L1_attn_b_b_k"], 1024)
        brow = kb.sbuf("mb_brow", [1, 1024], BF16)
        bstg = kb.sbuf("mb_bstg", [1, 1024], F32)
        bp = kb.psum("mb_bp", [128, 512], F32)
        kb.dma(bstg[:], W["L1_attn_b_b_v"].ap().unsqueeze(0), stream="ld")
        kb.copy(brow[:], bstg[:])
        bvB = bcast_row(kb, "mb_bvB", brow, 1024, ones1, psum=bp)
        kb.dma(bstg[:], W["L1_attn_b_b_o"].ap().unsqueeze(0), stream="ld")
        kb.copy(brow[:], bstg[:])
        boB = bcast_row(kb, "mb_boB", brow, 1024, ones1, psum=bp)
        wsb = kb.sbuf("mb_w", [128, 8 * 1024], BF16)
        wstg = kb.sbuf("mb_wstg", [128, 1024], F32)
        QT = kb.sbuf("mb_QT", [128, 8 * L], BF16)
        KT = kb.sbuf("mb_KT", [128, 8 * L], BF16)
        Vtm = kb.sbuf("mb_V", [128, LT * 1024], BF16)
        pp = kb.psum("mb_pp", [128, 512], F32)
        pp2 = kb.psum("mb_pp2", [128, 512], F32)
        ptp = kb.psum("mb_ptp", [128, 512], BF16)
        pav = kb.psum("mb_pav", [64, 128], F32)
        for b in range(B_LOC):
            tiles = list(range(b * LT, (b + 1) * LT))
            with kb.scope():
                hT = kb.sbuf("mb_hT", [128, 8 * L], BF16)
                ln_tiles_to_fm(kb, x_src, hT, tiles, L, ident,
                               prefix=f"mb_ln{b}", tp=ptp)
                for wname, bias, dstT in (("L1_attn_b_W_q", bq, QT),
                                          ("L1_attn_b_W_k", bk, KT)):
                    load_w_direct(kb, "mb_wq", W[wname], 1024, 1024, wsb=wsb,
                                  stage=wstg)
                    for co in range(8):
                        for ti2, t0 in enumerate(range(0, L, 512)):
                            tw = min(512, L - t0)
                            ps = pp if ti2 % 2 == 0 else pp2
                            for ci in range(8):
                                kb.mm(ps[:, 0:tw],
                                      wsb[:, ci * 1024 + co * 128: ci * 1024 + (co + 1) * 128],
                                      hT[:, ci * L + t0: ci * L + t0 + tw],
                                      start=(ci == 0), stop=(ci == 7))
                            kb.ts(dstT[:, co * L + t0: co * L + t0 + tw],
                                  ps[:, 0:tw], bias[:, co:co + 1], None, ALU.add)
                load_w_direct(kb, "mb_wq", W["L1_attn_b_W_v"], 1024, 1024,
                              wsb=wsb, stage=wstg)
                for tt_ in range(LT):
                    for oh in range(2):
                        ps = pp if oh == 0 else pp2
                        for ci in range(8):
                            kb.mm(ps[:], hT[:, ci * L + tt_ * 128: ci * L + (tt_ + 1) * 128],
                                  wsb[:, ci * 1024 + oh * 512: ci * 1024 + (oh + 1) * 512],
                                  start=(ci == 0), stop=(ci == 7))
                        kb.stt(Vtm[:, tt_ * 1024 + oh * 512: tt_ * 1024 + (oh + 1) * 512],
                               ps[:], 1.0, bvB[:, oh * 512:(oh + 1) * 512],
                               ALU.mult, ALU.add)
            kb.barrier()
            ext = kb.scope()
            ext.__enter__()
            aT = kb.sbuf("mb_aT", [128, 8 * L], BF16)
            S = kb.sbuf("mb_S", [128, L], F32)
            Pf = kb.sbuf("mb_Pf", [128, L], F32)
            Pb = kb.sbuf("mb_Pb", [128, L], BF16)
            PT = kb.sbuf("mb_PT", [128, L], BF16)
            negm = kb.sbuf("mb_negm", [128, 1], F32)
            ssum = kb.sbuf("mb_ss", [128, 1], F32)
            rrow = kb.sbuf("mb_rr", [128, 1], F32)
            otm = kb.sbuf("mb_otm", [128, 1024], F32)
            xt2 = kb.sbuf("mb_xt2", [128, 1024], F32)
            for qt in range(LT):
                kspan = (qt + 1) * 128
                for h in range(16):
                    ci = h // 2
                    pb = (h % 2) * 64
                    for ki2, k0 in enumerate(range(0, kspan, 512)):
                        kw = min(512, kspan - k0)
                        ps = pp if ki2 % 2 == 0 else pp2
                        kb.mm(ps[:, 0:kw],
                              QT[:, ci * L + qt * 128: ci * L + (qt + 1) * 128][pb:pb + 64, :],
                              KT[:, ci * L + k0: ci * L + k0 + kw][pb:pb + 64, :],
                              start=True, stop=True)
                        kb.copy(S[:, k0:k0 + kw], ps[:, 0:kw])
                    kb.tadd(S[:, qt * 128:kspan], S[:, qt * 128:kspan], tri)
                    kb.red(negm[:], S[:, 0:kspan], ALU.max, negate=True)
                    kb.ts(negm[:], negm[:], 0.125, None, ALU.mult)
                    kb.act(Pf[:, 0:kspan], S[:, 0:kspan], AF.Exp, bias=negm[:],
                           scale=0.125, accum_out=ssum[:])
                    kb.recip(rrow[:], ssum[:])
                    kb.ts(Pb[:, 0:kspan], Pf[:, 0:kspan], rrow[:], None, ALU.mult)
                    for kt4 in range(0, qt + 1, 4):
                        kts = range(kt4, min(kt4 + 4, qt + 1))
                        for j, kt in enumerate(kts):
                            kb.trans(ptp[:, j * 128:(j + 1) * 128],
                                     Pb[:, kt * 128:(kt + 1) * 128], ident)
                        for j, kt in enumerate(kts):
                            kb.copy(PT[:, kt * 128:(kt + 1) * 128],
                                    ptp[:, j * 128:(j + 1) * 128])
                    for kt in range(qt + 1):
                        kb.mm(pav[:], Vtm[:, kt * 1024 + h * 64: kt * 1024 + (h + 1) * 64],
                              PT[:, kt * 128:(kt + 1) * 128],
                              start=(kt == 0), stop=(kt == qt))
                    kb.copy(aT[:, ci * L + qt * 128: ci * L + (qt + 1) * 128][pb:pb + 64, :],
                            pav[:])
            load_w_direct(kb, "mb_wq", W["L1_attn_b_W_o"], 1024, 1024,
                          wsb=wsb, stage=wstg)
            for qt in range(LT):
                for oh in range(2):
                    ps = pp if oh == 0 else pp2
                    for ci in range(8):
                        kb.mm(ps[:], aT[:, ci * L + qt * 128: ci * L + (qt + 1) * 128],
                              wsb[:, ci * 1024 + oh * 512: ci * 1024 + (oh + 1) * 512],
                              start=(ci == 0), stop=(ci == 7))
                    kb.stt(otm[:, oh * 512:(oh + 1) * 512], ps[:], 1.0,
                           boB[:, oh * 512:(oh + 1) * 512], ALU.mult, ALU.add)
                ti = b * LT + qt
                kb.dma(xt2[:], x_src[ti * 128:(ti + 1) * 128, :], stream="xld2")
                kb.tadd(otm[:], otm[:], xt2[:], chan="pool")
                kb.dma(xres[ti * 128:(ti + 1) * 128, :], otm[:], stream="xst",
                       queue="gpsimd")
            ext.__exit__(None, None, None)
        kb.barrier()


def _row_bcast_from_hbm(kb, name, v_hbm, n, ones1, psum, src_ap=None, repeat=1,
                        dtype=F32):
    stg = kb.sbuf(name + "_rs", [1, n], F32)
    rowb = kb.sbuf(name + "_rb", [1, n], BF16)
    ap = src_ap if src_ap is not None else v_hbm.ap()
    m = n // repeat
    for r in range(repeat):
        kb.dma(stg[:, r * m:(r + 1) * m], ap.unsqueeze(0), stream="ld")
    kb.copy(rowb[:], stg[:])
    return bcast_row(kb, name, rowb, n, ones1, psum=psum, dtype=dtype)


def _fm_mix(kb, dT, hT, mixc, outT, c, first_zero=True):
    kb.tsub(dT[:, 1:L], hT[:, c * L + 1: c * L + L], hT[:, c * L: c * L + L - 1])
    kb.stt(outT[:, c * L + 1: c * L + L], dT[:, 1:L], mixc,
           hT[:, c * L: c * L + L - 1], ALU.mult, ALU.add)
    kb.ts(outT[:, c * L: c * L + 1], hT[:, c * L: c * L + 1], mixc, None, ALU.mult)


def ln_plain_to_dram(kb, x_src, y_ap, NT, prefix):
    with kb.scope():
        xt = kb.sbuf(prefix + "_xt", [128, 1024], F32)
        yt = kb.sbuf(prefix + "_yt", [128, 1024], F32)
        mneg = kb.sbuf(prefix + "_mneg", [128, 1], F32)
        ssum = kb.sbuf(prefix + "_ss", [128, 1], F32)
        sq = kb.sbuf(prefix + "_sq", [128, 1024], F32)
        var = kb.sbuf(prefix + "_var", [128, 1], F32)
        rstd = kb.sbuf(prefix + "_rstd", [128, 1], F32)
        for ti in range(NT):
            kb.dma(xt[:], x_src[ti * 128:(ti + 1) * 128, :], stream="xld")
            kb.red(mneg[:], xt[:], ALU.add, negate=True)
            kb.ts(mneg[:], mneg[:], 1.0 / 1024, None, ALU.mult)
            kb.act(sq[:], xt[:], AF.Square, bias=mneg[:], accum_out=ssum[:])
            kb.ts(var[:], ssum[:], 1.0 / 1024, 1e-5, ALU.mult, ALU.add)
            kb.act(rstd[:], var[:], AF.Ln)
            kb.act(rstd[:], rstd[:], AF.Exp, scale=-0.5)
            kb.ts(yt[:], xt[:], mneg[:], rstd[:], ALU.add, ALU.mult)
            kb.dma(y_ap[ti * 128:(ti + 1) * 128, :], yt[:], stream="yst",
                   queue="gpsimd")


def final_add_xy(kb, x_src, y_ap, xres, NT, prefix):
    with kb.scope():
        xt = kb.sbuf(prefix + "_fx", [128, 1024], F32)
        yt = kb.sbuf(prefix + "_fy", [128, 1024], F32)
        for ti in range(NT):
            kb.dma(xt[:], x_src[ti * 128:(ti + 1) * 128, :], stream="xld")
            kb.dma(yt[:], y_ap[ti * 128:(ti + 1) * 128, :], stream="yld")
            kb.tadd(yt[:], yt[:], xt[:])
            kb.dma(xres[ti * 128:(ti + 1) * 128, :], yt[:], stream="xst",
                   queue="gpsimd")


def build_l0_attn_b(kb, x_orig, xres, consts, W):
    nc = kb.nc
    ident = consts["ident"]
    ones1 = consts["ones1"]
    LT = L // 128
    NT = B_LOC * LT
    y_d = nc.dram_tensor("rw_y", [B_LOC * L, D], F32, kind="Internal")
    ln_plain_to_dram(kb, x_orig, y_d.ap(), NT, "rwp")
    x_src = y_d.ap()
    with kb.scope():
        bp = kb.psum("rw_bp", [128, 512], F32)
        gA = _row_bcast_from_hbm(kb, "rw_gA", W["L0_attn_b_ln_att_g"], 1024, ones1, bp)
        bA = _row_bcast_from_hbm(kb, "rw_bA", W["L0_attn_b_ln_att_b"], 1024, ones1, bp)
        gF = _row_bcast_from_hbm(kb, "rw_gF", W["L0_attn_b_ln_ffn_g"], 1024, ones1, bp)
        bF = _row_bcast_from_hbm(kb, "rw_bF", W["L0_attn_b_ln_ffn_b"], 1024, ones1, bp)
        mixk = load_vec_chunks(kb, "rw_mk", W["L0_attn_b_att_mix_k"], 1024)
        mixv = load_vec_chunks(kb, "rw_mv", W["L0_attn_b_att_mix_v"], 1024)
        mixr = load_vec_chunks(kb, "rw_mr", W["L0_attn_b_att_mix_r"], 1024)
        fmixk = load_vec_chunks(kb, "rw_fmk", W["L0_attn_b_ffn_mix_k"], 1024)
        fmixr = load_vec_chunks(kb, "rw_fmr", W["L0_attn_b_ffn_mix_r"], 1024)
        td = load_vec_chunks(kb, "rw_td", W["L0_attn_b_att_time_decay"], 1024)
        tf = load_vec_chunks(kb, "rw_tf", W["L0_attn_b_att_time_first"], 1024)
        ew = kb.sbuf("rw_ew", [128, 8], F32)
        eu = kb.sbuf("rw_eu", [128, 8], F32)
        kb.act(ew[:], td[:], AF.Exp)
        kb.act(ew[:], ew[:], AF.Exp, scale=-1.0)
        kb.act(eu[:], tf[:], AF.Exp)
        wstg = kb.sbuf("rw_wstg", [128, 1024], F32)
        ptp = kb.psum("rw_ptp", [128, 512], BF16)
        pp = kb.psum("rw_pp", [128, 512], F32)
        pp2 = kb.psum("rw_pp2", [128, 512], F32)
        kx_d = nc.dram_tensor("rw_kx_d", [8, 128, L], BF16, kind="Internal")
        vx_d = nc.dram_tensor("rw_vx_d", [8, 128, L], BF16, kind="Internal")
        rx_d = nc.dram_tensor("rw_rx_d", [8, 128, L], BF16, kind="Internal")
        for b in range(B_LOC):
          with kb.scope():
            tiles = list(range(b * LT, (b + 1) * LT))
            with kb.scope():
                kxT = kb.sbuf("rw_kxT", [128, 8 * L], BF16)
                vxT = kb.sbuf("rw_vxT", [128, 8 * L], BF16)
                rxT = kb.sbuf("rw_rxT", [128, 8 * L], BF16)
                with kb.scope():
                    hT = kb.sbuf("rw_hT", [128, 8 * L], BF16)
                    dT = kb.sbuf("rw_dT", [128, L], BF16)
                    ln_tiles_to_fm(kb, x_src, hT, tiles, L, ident,
                                   g_bcast=gA, b_bcast=bA, prefix=f"rw_lnA{b}",
                                   tp=ptp)
                    for c in range(8):
                        _fm_mix(kb, dT, hT, mixk[:, c:c + 1], kxT, c)
                        _fm_mix(kb, dT, hT, mixv[:, c:c + 1], vxT, c)
                        _fm_mix(kb, dT, hT, mixr[:, c:c + 1], rxT, c)
                for c in range(8):
                    kb.dma(kx_d.ap()[c], kxT[:, c * L:(c + 1) * L], stream="rwst",
                           queue="gpsimd")
                    kb.dma(vx_d.ap()[c], vxT[:, c * L:(c + 1) * L], stream="rwst",
                           queue="gpsimd")
                    kb.dma(rx_d.ap()[c], rxT[:, c * L:(c + 1) * L], stream="rwst",
                           queue="gpsimd")
            with kb.scope():
              ekT = kb.sbuf("rw_ekT", [128, 8 * L], BF16)
              ekvT = kb.sbuf("rw_ekvT", [128, 8 * L], BF16)
              with kb.scope():
                  wsb = kb.sbuf("rw_wk", [128, 8 * 1024], BF16)
                  xin = kb.sbuf("rw_xin", [128, 8 * L], BF16)
                  for c in range(8):
                      kb.dma(xin[:, c * L:(c + 1) * L], kx_d.ap()[c], stream="rwld")
                  load_w_direct(kb, "rw_w", W["L0_attn_b_att_wk"], 1024, 1024,
                                wsb=wsb, stage=wstg)
                  for co in range(8):
                      for ti2, t0 in enumerate(range(0, L, 512)):
                          tw = min(512, L - t0)
                          ps = pp if ti2 % 2 == 0 else pp2
                          for ci in range(8):
                              kb.mm(ps[:, 0:tw],
                                    wsb[:, ci * 1024 + co * 128: ci * 1024 + (co + 1) * 128],
                                    xin[:, ci * L + t0: ci * L + t0 + tw],
                                    start=(ci == 0), stop=(ci == 7))
                          kb.act(ekT[:, co * L + t0: co * L + t0 + tw],
                                 ps[:, 0:tw], AF.Exp)
              with kb.scope():
                  wsb = kb.sbuf("rw_wv", [128, 8 * 1024], BF16)
                  xin = kb.sbuf("rw_xin2", [128, 8 * L], BF16)
                  vtmp = kb.sbuf("rw_vtmp", [128, L], BF16)
                  for c in range(8):
                      kb.dma(xin[:, c * L:(c + 1) * L], vx_d.ap()[c], stream="rwld")
                  load_w_direct(kb, "rw_w", W["L0_attn_b_att_wv"], 1024, 1024,
                                wsb=wsb, stage=wstg)
                  for co in range(8):
                      for ti2, t0 in enumerate(range(0, L, 512)):
                          tw = min(512, L - t0)
                          ps = pp if ti2 % 2 == 0 else pp2
                          for ci in range(8):
                              kb.mm(ps[:, 0:tw],
                                    wsb[:, ci * 1024 + co * 128: ci * 1024 + (co + 1) * 128],
                                    xin[:, ci * L + t0: ci * L + t0 + tw],
                                    start=(ci == 0), stop=(ci == 7))
                          kb.copy(vtmp[:, 0:tw], ps[:, 0:tw])
                          kb.tmul(ekvT[:, co * L + t0: co * L + t0 + tw],
                                  vtmp[:, 0:tw], ekT[:, co * L + t0: co * L + t0 + tw])
              rw = kb.sbuf("rw_rw", [128, 8 * L], BF16)
              with kb.scope():
                  ewb = kb.sbuf("rw_ewb", [128, L], F32)
                  pscan = kb.sbuf("rw_ps", [128, L], F32)
                  qscan = kb.sbuf("rw_qs", [128, L], F32)
                  num = kb.sbuf("rw_num", [128, L], F32)
                  den = kb.sbuf("rw_den", [128, L], F32)
                  tmp = kb.sbuf("rw_tmp", [128, L], F32)
                  for c in range(8):
                      kb.memset(ewb[:], 1.0)
                      kb.ts(ewb[:], ewb[:], ew[:, c:c + 1], None, ALU.mult)
                      kb.scan(pscan[:], ewb[:], ekvT[:, c * L:(c + 1) * L], 0.0)
                      kb.scan(qscan[:], ewb[:], ekT[:, c * L:(c + 1) * L], 0.0,
                              chan="pool")
                      kb.ts(tmp[:], ekvT[:, c * L:(c + 1) * L], eu[:, c:c + 1],
                            None, ALU.mult)
                      kb.copy(num[:, 0:1], tmp[:, 0:1])
                      kb.tadd(num[:, 1:L], pscan[:, 0:L - 1], tmp[:, 1:L])
                      kb.ts(tmp[:], ekT[:, c * L:(c + 1) * L], eu[:, c:c + 1],
                            None, ALU.mult, chan="pool")
                      kb.copy(den[:, 0:1], tmp[:, 0:1], chan="pool")
                      kb.tadd(den[:, 1:L], qscan[:, 0:L - 1], tmp[:, 1:L], chan="pool")
                      kb.recip(den[:], den[:])
                      kb.tmul(rw[:, c * L:(c + 1) * L], num[:], den[:])
              with kb.scope():
                  wsb = kb.sbuf("rw_wr", [128, 8 * 1024], BF16)
                  xin = kb.sbuf("rw_xin3", [128, 8 * L], BF16)
                  srf = kb.sbuf("rw_srf", [128, 512], BF16)
                  for c in range(8):
                      kb.dma(xin[:, c * L:(c + 1) * L], rx_d.ap()[c], stream="rwld")
                  load_w_direct(kb, "rw_w", W["L0_attn_b_att_wr"], 1024, 1024,
                                wsb=wsb, stage=wstg)
                  for co in range(8):
                      for ti2, t0 in enumerate(range(0, L, 512)):
                          tw = min(512, L - t0)
                          ps = pp if ti2 % 2 == 0 else pp2
                          for ci in range(8):
                              kb.mm(ps[:, 0:tw],
                                    wsb[:, ci * 1024 + co * 128: ci * 1024 + (co + 1) * 128],
                                    xin[:, ci * L + t0: ci * L + t0 + tw],
                                    start=(ci == 0), stop=(ci == 7))
                          kb.act(srf[:, 0:tw], ps[:, 0:tw], AF.Sigmoid)
                          kb.tmul(rw[:, co * L + t0: co * L + t0 + tw],
                                  rw[:, co * L + t0: co * L + t0 + tw], srf[:, 0:tw])
              with kb.scope():
                  wsb = kb.sbuf("rw_wo", [128, 8 * 1024], BF16)
                  otm = kb.sbuf("rw_otm", [128, 1024], F32)
                  xt2 = kb.sbuf("rw_xt2", [128, 1024], F32)
                  load_w_direct(kb, "rw_w", W["L0_attn_b_att_wo"], 1024, 1024,
                                wsb=wsb, stage=wstg)
                  for qt in range(LT):
                      for oh in range(2):
                          ps = pp if oh == 0 else pp2
                          for ci in range(8):
                              kb.mm(ps[:], rw[:, ci * L + qt * 128: ci * L + (qt + 1) * 128],
                                    wsb[:, ci * 1024 + oh * 512: ci * 1024 + (oh + 1) * 512],
                                    start=(ci == 0), stop=(ci == 7))
                          kb.copy(otm[:, oh * 512:(oh + 1) * 512], ps[:])
                      ti = b * LT + qt
                      kb.dma(xt2[:], x_src[ti * 128:(ti + 1) * 128, :], stream="xld2")
                      kb.tadd(otm[:], otm[:], xt2[:], chan="pool")
                      kb.dma(x_src[ti * 128:(ti + 1) * 128, :], otm[:], stream="xst",
                             queue="gpsimd")
              kb.barrier()

            with kb.scope():
                kxf = kb.sbuf("rw_kxf", [128, 8 * L], BF16)
                rxf = kb.sbuf("rw_rxf", [128, 8 * L], BF16)
                outT = kb.sbuf("rw_outT", [128, 8 * L], BF16)
                with kb.scope():
                    hT = kb.sbuf("rw_hT2", [128, 8 * L], BF16)
                    dT = kb.sbuf("rw_dT2", [128, L], BF16)
                    ln_tiles_to_fm(kb, x_src, hT, tiles, L, ident,
                                   g_bcast=gF, b_bcast=bF, prefix=f"rw_lnF{b}",
                                   tp=ptp)
                    for c in range(8):
                        _fm_mix(kb, dT, hT, fmixk[:, c:c + 1], kxf, c)
                        _fm_mix(kb, dT, hT, fmixr[:, c:c + 1], rxf, c)
                with kb.scope():
                    wkg = kb.sbuf("rw_wkg", [128, 8 * 1024], BF16)
                    wvg = kb.sbuf("rw_wvg", [128, 8 * 1024], BF16)
                    kfg = kb.sbuf("rw_kfg", [128, 8 * L], BF16)
                    rtmp = kb.sbuf("rw_rtmp", [128, 512], F32)
                    for g in range(4):
                        for ci in range(8):
                            kb.dma(wstg[:],
                                   W["L0_attn_b_ffn_wk"].ap()[ci * 128:(ci + 1) * 128,
                                                              g * 1024:(g + 1) * 1024],
                                   stream="wld")
                            kb.copy(wkg[:, ci * 1024:(ci + 1) * 1024], wstg[:],
                                    chan="pool")
                        load_w_direct(kb, "rw_w", W["L0_attn_b_ffn_wv"].ap()[g * 1024:(g + 1) * 1024, :],
                                      1024, 1024, wsb=wvg, stage=wstg)
                        for co in range(8):
                            for ti2, t0 in enumerate(range(0, L, 512)):
                                tw = min(512, L - t0)
                                ps = pp if ti2 % 2 == 0 else pp2
                                for ci in range(8):
                                    kb.mm(ps[:, 0:tw],
                                          wkg[:, ci * 1024 + co * 128: ci * 1024 + (co + 1) * 128],
                                          kxf[:, ci * L + t0: ci * L + t0 + tw],
                                          start=(ci == 0), stop=(ci == 7))
                                kb.ts(rtmp[:, 0:tw], ps[:, 0:tw], 0.0, None,
                                      ALU.max)
                                kb.act(kfg[:, co * L + t0: co * L + t0 + tw],
                                       rtmp[:, 0:tw], AF.Square)
                        for co in range(8):
                            for ti2, t0 in enumerate(range(0, L, 512)):
                                tw = min(512, L - t0)
                                ps = pp if ti2 % 2 == 0 else pp2
                                for ci in range(8):
                                    kb.mm(ps[:, 0:tw],
                                          wvg[:, ci * 1024 + co * 128: ci * 1024 + (co + 1) * 128],
                                          kfg[:, ci * L + t0: ci * L + t0 + tw],
                                          start=(ci == 0), stop=(ci == 7))
                                if g == 0:
                                    kb.copy(outT[:, co * L + t0: co * L + t0 + tw],
                                            ps[:, 0:tw])
                                else:
                                    kb.tadd(outT[:, co * L + t0: co * L + t0 + tw],
                                            outT[:, co * L + t0: co * L + t0 + tw],
                                            ps[:, 0:tw])
                with kb.scope():
                    wsb = kb.sbuf("rw_wr2", [128, 8 * 1024], BF16)
                    srf = kb.sbuf("rw_srf", [128, L], BF16)
                    bfprod = kb.sbuf("rw_bfp", [128, 8 * L], BF16)
                    otm = kb.sbuf("rw_otm2", [128, 1024], F32)
                    xt2 = kb.sbuf("rw_xt3", [128, 1024], F32)
                    load_w_direct(kb, "rw_w", W["L0_attn_b_ffn_wr"], 1024, 1024,
                                  wsb=wsb, stage=wstg)
                    for co in range(8):
                        for ti2, t0 in enumerate(range(0, L, 512)):
                            tw = min(512, L - t0)
                            ps = pp if ti2 % 2 == 0 else pp2
                            for ci in range(8):
                                kb.mm(ps[:, 0:tw],
                                      wsb[:, ci * 1024 + co * 128: ci * 1024 + (co + 1) * 128],
                                      rxf[:, ci * L + t0: ci * L + t0 + tw],
                                      start=(ci == 0), stop=(ci == 7))
                            kb.act(srf[:, 0:tw], ps[:, 0:tw], AF.Sigmoid)
                            kb.tmul(bfprod[:, co * L + t0: co * L + t0 + tw],
                                    srf[:, 0:tw], outT[:, co * L + t0: co * L + t0 + tw])
                    for qt in range(LT):
                        for c4 in range(2):
                            cs = range(c4 * 4, c4 * 4 + 4)
                            for j, c in enumerate(cs):
                                kb.trans(ptp[:, j * 128:(j + 1) * 128],
                                         bfprod[:, c * L + qt * 128: c * L + (qt + 1) * 128],
                                         ident)
                            for j, c in enumerate(cs):
                                kb.copy(otm[:, c * 128:(c + 1) * 128],
                                        ptp[:, j * 128:(j + 1) * 128])
                        ti = b * LT + qt
                        kb.dma(xt2[:], x_src[ti * 128:(ti + 1) * 128, :], stream="xld2")
                        kb.tadd(otm[:], otm[:], xt2[:], chan="pool")
                        kb.dma(x_src[ti * 128:(ti + 1) * 128, :], otm[:], stream="xst",
                               queue="gpsimd")
            kb.barrier()
        kb.barrier()
    final_add_xy(kb, x_orig, y_d.ap(), xres, NT, "rwf")
    kb.barrier()


def build_l1_attn_a(kb, x_orig, xres, consts, W):
    nc = kb.nc
    ident = consts["ident"]
    ones1 = consts["ones1"]
    LT = L // 128
    y_d = nc.dram_tensor("ma_y", [B_LOC * L, D], F32, kind="Internal")
    ln_plain_to_dram(kb, x_orig, y_d.ap(), B_LOC * LT, "map")
    x_src = y_d.ap()
    ED = 2048
    EC = ED // 128
    NS = 16
    z_d = nc.dram_tensor("ma_z", [EC, 128, L], BF16, kind="Internal")
    dl_d = nc.dram_tensor("ma_dl", [EC, 128, L], BF16, kind="Internal")
    dx_d = nc.dram_tensor("ma_dx", [EC, 128, L], BF16, kind="Internal")
    xc_d = nc.dram_tensor("ma_xc", [EC, 128, L], BF16, kind="Internal")
    pr_d = nc.dram_tensor("ma_pr", [EC, 128, L], BF16, kind="Internal")
    with kb.scope():
        bp = kb.psum("ma_bp", [128, 512], F32)
        gB = _row_bcast_from_hbm(kb, "ma_g", W["L1_attn_a_gamma"], 1024, ones1, bp)
        bB = _row_bcast_from_hbm(kb, "ma_b", W["L1_attn_a_beta"], 1024, ones1, bp)
        boB = _row_bcast_from_hbm(kb, "ma_bo", W["L1_attn_a_b_out"], 1024, ones1, bp)
        b_in = load_vec_chunks(kb, "ma_bin", W["L1_attn_a_b_in"], 4096)
        b_dt = load_vec_chunks(kb, "ma_bdt", W["L1_attn_a_b_dt"], 2048)
        Dp = load_vec_chunks(kb, "ma_D", W["L1_attn_a_D_param"], 2048)
        cw = kb.sbuf("ma_cw", [128, EC * 15], F32)
        for c in range(EC):
            kb.dma(cw[:, c * 15:(c + 1) * 15],
                   W["L1_attn_a_conv_w"].ap()[c * 128:(c + 1) * 128, :], stream="ld")
        cb = load_vec_chunks(kb, "ma_cb", W["L1_attn_a_conv_b"], 2048)
        A_sb = kb.sbuf("ma_A", [128, EC * NS], F32)
        astg = kb.sbuf("ma_astg", [128, NS], F32)
        for c in range(EC):
            kb.dma(astg[:], W["L1_attn_a_A_log"].ap()[c * 128:(c + 1) * 128, :],
                   stream="ld")
            kb.act(A_sb[:, c * NS:(c + 1) * NS], astg[:], AF.Exp)
            kb.ts(A_sb[:, c * NS:(c + 1) * NS], A_sb[:, c * NS:(c + 1) * NS],
                  -1.0, None, ALU.mult)
        wstg = kb.sbuf("ma_wstg", [128, 2048], F32)
        wstgb = kb.sbuf("ma_wstgb", [128, 2048], BF16)
        ptp = kb.psum("ma_ptp", [128, 512], BF16)
        pp = kb.psum("ma_pp", [128, 512], F32)
        pp2 = kb.psum("ma_pp2", [128, 512], F32)
        wstage = (wstg, wstgb, ptp)
        for b in range(B_LOC):
          with kb.scope():
            tiles = list(range(b * LT, (b + 1) * LT))
            Bt = kb.sbuf("ma_Bt", [16, L], F32)
            Ct = kb.sbuf("ma_Ct", [16, L], F32)
            with kb.scope():
              xcT = kb.sbuf("ma_xcT", [128, EC * L], BF16)
              with kb.scope():
                xnT = kb.sbuf("ma_xnT", [128, 8 * L], BF16)
                ln_tiles_to_fm(kb, x_src, xnT, tiles, L, ident, g_bcast=gB,
                               b_bcast=bB, prefix=f"ma_ln{b}", tp=ptp)
                ztmp = kb.sbuf("ma_ztmp", [128, 512], BF16)
                for half in range(2):
                  with kb.scope():
                    wT = kb.sbuf(f"ma_winT{half}", [128, 8 * 2048], BF16)
                    load_w_T(kb, "ma_win", W["L1_attn_a_W_in"].ap()
                             [half * 2048:(half + 1) * 2048, :],
                             2048, 1024, ident, wT=wT, stage=wstage)
                    for co in range(EC):
                        for ti2, t0 in enumerate(range(0, L, 512)):
                            tw = min(512, L - t0)
                            ps = pp if ti2 % 2 == 0 else pp2
                            for ci in range(8):
                                kb.mm(ps[:, 0:tw],
                                      wT[:, ci * 2048 + co * 128: ci * 2048 + (co + 1) * 128],
                                      xnT[:, ci * L + t0: ci * L + t0 + tw],
                                      start=(ci == 0), stop=(ci == 7))
                            if half == 0:
                                kb.ts(xcT[:, co * L + t0: co * L + t0 + tw],
                                      ps[:, 0:tw], b_in[:, co:co + 1], None, ALU.add)
                            else:
                                kb.ts(ztmp[:, 0:tw], ps[:, 0:tw],
                                      b_in[:, 16 + co:17 + co], None, ALU.add)
                                kb.dma(z_d.ap()[co][:, t0:t0 + tw], ztmp[:, 0:tw],
                                       stream="mzst", queue="gpsimd")
              with kb.scope():
                xpad = kb.sbuf("ma_xpad", [128, L + 14], BF16)
                cacc = kb.sbuf("ma_cacc", [128, L], F32)
                for c in range(EC):
                    kb.memset(xpad[:, 0:14], 0.0)
                    kb.copy(xpad[:, 14:L + 14], xcT[:, c * L:(c + 1) * L],
                            chan="pool")
                    kb.ts(cacc[:], xpad[:, 0:L], cw[:, c * 15:c * 15 + 1],
                          None, ALU.mult)
                    for j in range(1, 15):
                        kb.stt(cacc[:], xpad[:, j:j + L], cw[:, c * 15 + j:c * 15 + j + 1],
                               cacc[:], ALU.mult, ALU.add,
                               chan=("dve" if j % 2 else "pool"))
                    kb.act(xcT[:, c * L:(c + 1) * L], cacc[:], AF.Silu,
                           bias=cb[:, c:c + 1])
                    kb.dma(xc_d.ap()[c], xcT[:, c * L:(c + 1) * L],
                           stream="mxst", queue="gpsimd")
              with kb.scope():
                wxT = kb.sbuf("ma_wxT", [128, EC * 96], BF16)
                kb.dma(wstg[:96, 0:2048], W["L1_attn_a_W_x"].ap(), stream="wld")
                kb.copy(wstgb[:96, 0:2048], wstg[:96, 0:2048], chan="pool")
                for c in range(EC):
                    kb.trans(ptp[:, 0:96], wstgb[0:96, c * 128:(c + 1) * 128],
                             ident[0:96, 0:96])
                    kb.copy(wxT[:, c * 96:(c + 1) * 96], ptp[:, 0:96])
                dtl = kb.sbuf("ma_dtl", [64, L], F32)
                for ti2, t0 in enumerate(range(0, L, 512)):
                    tw = min(512, L - t0)
                    for lo, hi, dst in ((0, 64, dtl), (64, 80, Bt), (80, 96, Ct)):
                        ps = pp
                        for ci in range(EC):
                            kb.mm(ps[0:hi - lo, 0:tw],
                                  wxT[:, ci * 96 + lo: ci * 96 + hi],
                                  xcT[:, ci * L + t0: ci * L + t0 + tw],
                                  start=(ci == 0), stop=(ci == EC - 1))
                        kb.copy(dst[:, t0:t0 + tw], ps[0:hi - lo, 0:tw])
                wdtT = kb.sbuf("ma_wdtT", [64, EC * 128], BF16)
                for c4 in range(EC // 4):
                    for j in range(4):
                        c = c4 * 4 + j
                        kb.dma(wstg[:, 0:64],
                               W["L1_attn_a_W_dt"].ap()[c * 128:(c + 1) * 128, :],
                               stream="wld")
                        kb.copy(wstgb[:, j * 64:(j + 1) * 64], wstg[:, 0:64],
                                chan="pool")
                    for j in range(4):
                        c = c4 * 4 + j
                        kb.trans(ptp[:, 0:128][0:64, :] if False else ptp[0:64, j * 128:(j + 1) * 128],
                                 wstgb[:, j * 64:(j + 1) * 64], ident)
                    for j in range(4):
                        c = c4 * 4 + j
                        kb.copy(wdtT[:, c * 128:(c + 1) * 128],
                                ptp[0:64, j * 128:(j + 1) * 128])
                dtlb = kb.sbuf("ma_dtlb", [64, L], BF16)
                kb.copy(dtlb[:], dtl[:])
                spx = kb.sbuf("ma_spx", [128, 512], F32)
                spa = kb.sbuf("ma_spa", [128, 512], F32)
                dchunk = kb.sbuf("ma_dch", [128, L], BF16)
                dxchunk = kb.sbuf("ma_dxch", [128, L], BF16)
                for c in range(EC):
                    for ti2, t0 in enumerate(range(0, L, 512)):
                        tw = min(512, L - t0)
                        ps = pp if ti2 % 2 == 0 else pp2
                        kb.mm(ps[:, 0:tw], wdtT[:, c * 128:(c + 1) * 128],
                              dtlb[:, t0:t0 + tw], start=True, stop=True)
                        kb.ts(spx[:, 0:tw], ps[:, 0:tw], b_dt[:, c:c + 1],
                              None, ALU.add)
                        kb.act(spa[:, 0:tw], spx[:, 0:tw], AF.Abs)
                        kb.act(spa[:, 0:tw], spa[:, 0:tw], AF.Exp, scale=-1.0)
                        kb.act(spa[:, 0:tw], spa[:, 0:tw], AF.Ln, bias=1.0)
                        kb.ts(spx[:, 0:tw], spx[:, 0:tw], 0.0, None, ALU.max)
                        kb.tadd(dchunk[:, t0:t0 + tw], spa[:, 0:tw], spx[:, 0:tw])
                    kb.tmul(dxchunk[:], dchunk[:], xcT[:, c * L:(c + 1) * L],
                            chan="pool")
                    kb.dma(dl_d.ap()[c], dchunk[:], stream="mdst", queue="gpsimd")
                    kb.dma(dx_d.ap()[c], dxchunk[:], stream="mdst", queue="gpsimd")
            with kb.scope():
                yacc = kb.sbuf("ma_yacc", [128, EC * L], BF16)
                Brow = kb.sbuf("ma_Brow", [1, L], BF16)
                Crow = kb.sbuf("ma_Crow", [1, L], BF16)
                Bb = kb.sbuf("ma_Bb", [128, L], BF16)
                Cb = kb.sbuf("ma_Cb", [128, L], BF16)
                dl_c = [kb.sbuf(f"ma_dlc{i}", [128, L], BF16) for i in range(2)]
                dx_c = [kb.sbuf(f"ma_dxc{i}", [128, L], BF16) for i in range(2)]
                dA = kb.sbuf("ma_dA", [128, L], BF16)
                dxB = kb.sbuf("ma_dxB", [128, L], BF16)
                hsc = kb.sbuf("ma_h", [128, L], BF16)
                hC = kb.sbuf("ma_hC", [128, L], BF16)
                for n in range(NS):
                    kb.dma(Brow[:], Bt[n:n + 1, :], stream="mrow", queue="gpsimd")
                    kb.dma(Crow[:], Ct[n:n + 1, :], stream="mrow", queue="gpsimd")
                    for j, t0 in enumerate(range(0, L, 512)):
                        tw = min(512, L - t0)
                        kb.mm(pp[:, 0:tw], ones1, Brow[:, t0:t0 + tw],
                              start=True, stop=True)
                        kb.mm(pp2[:, 0:tw], ones1, Crow[:, t0:t0 + tw],
                              start=True, stop=True)
                        kb.copy(Bb[:, t0:t0 + tw], pp[:, 0:tw])
                        kb.copy(Cb[:, t0:t0 + tw], pp2[:, 0:tw], chan="act")
                    for c in range(EC):
                        dlc, dxc = dl_c[c % 2], dx_c[c % 2]
                        kb.dma(dlc[:], dl_d.ap()[c], stream="mld")
                        kb.dma(dxc[:], dx_d.ap()[c], stream="mld2", queue="gpsimd")
                        kb.act(dA[:], dlc[:], AF.Exp, scale=A_sb[:, c * NS + n: c * NS + n + 1])
                        kb.tmul(dxB[:], dxc[:], Bb[:])
                        kb.scan(hsc[:], dA[:], dxB[:], 0.0)
                        kb.tmul(hC[:], hsc[:], Cb[:], chan="pool")
                        if n == 0:
                            kb.copy(yacc[:, c * L:(c + 1) * L], hC[:], chan="pool")
                        else:
                            kb.tadd(yacc[:, c * L:(c + 1) * L],
                                    yacc[:, c * L:(c + 1) * L], hC[:], chan="pool")
                zc = kb.sbuf("ma_zc", [128, L], BF16)
                prc = kb.sbuf("ma_prc", [128, L], BF16)
                for c in range(EC):
                    kb.dma(dl_c[0][:], xc_d.ap()[c], stream="mld")
                    kb.stt(yacc[:, c * L:(c + 1) * L], dl_c[0][:], Dp[:, c:c + 1],
                           yacc[:, c * L:(c + 1) * L], ALU.mult, ALU.add)
                    kb.dma(zc[:], z_d.ap()[c], stream="mld")
                    kb.act(zc[:], zc[:], AF.Silu)
                    kb.tmul(prc[:], yacc[:, c * L:(c + 1) * L], zc[:])
                    kb.dma(pr_d.ap()[c], prc[:], stream="mpst", queue="gpsimd")
            with kb.scope():
                woT = kb.sbuf("ma_woT", [128, EC * 1024], BF16)
                load_w_T(kb, "ma_wo", W["L1_attn_a_W_out"], 1024, 2048, ident,
                         wT=woT, stage=wstage)
                prod = kb.sbuf("ma_prod", [128, EC * L], BF16)
                for c in range(EC):
                    kb.dma(prod[:, c * L:(c + 1) * L], pr_d.ap()[c], stream="mld")
                otm = kb.sbuf("ma_otm", [128, 1024], F32)
                xt2 = kb.sbuf("ma_xt2", [128, 1024], F32)
                for qt in range(LT):
                    for oh in range(2):
                        ps = pp if oh == 0 else pp2
                        for ci in range(EC):
                            kb.mm(ps[:], prod[:, ci * L + qt * 128: ci * L + (qt + 1) * 128],
                                  woT[:, ci * 1024 + oh * 512: ci * 1024 + (oh + 1) * 512],
                                  start=(ci == 0), stop=(ci == EC - 1))
                        kb.stt(otm[:, oh * 512:(oh + 1) * 512], ps[:], 1.0,
                               boB[:, oh * 512:(oh + 1) * 512], ALU.mult, ALU.add)
                    ti = b * LT + qt
                    kb.dma(xt2[:], x_src[ti * 128:(ti + 1) * 128, :], stream="xld2")
                    kb.tadd(otm[:], otm[:], xt2[:])
                    kb.dma(xt2[:], x_orig[ti * 128:(ti + 1) * 128, :], stream="xld2")
                    kb.tadd(otm[:], otm[:], xt2[:])
                    kb.dma(xres[ti * 128:(ti + 1) * 128, :], otm[:], stream="xst",
                           queue="gpsimd")
        kb.barrier()


def load_vec_chunks_pad(kb, name, src_ap, n, stream="ld"):
    nch = (n + 127) // 128
    t = kb.sbuf(name, [128, nch], F32)
    for c in range(nch):
        r = min(128, n - c * 128)
        kb.dma(t[0:r, c:c + 1], src_ap[c * 128:c * 128 + r].unsqueeze(1),
               stream=stream)
    return t


def shifted_tile_loader(kb, y_ap, xt, ti, first_of_batch, stream="xld"):
    t0 = ti * 128
    kb.dma(xt[:, 0:512], y_ap[t0:t0 + 128, 0:512], stream=stream)
    if first_of_batch:
        kb.memset(xt[0:1, 512:1024], 0.0, chan="pool")
        kb.dma(xt[1:128, 512:1024], y_ap[t0:t0 + 127, 512:1024], stream=stream)
    else:
        kb.dma(xt[:, 512:1024], y_ap[t0 - 1:t0 + 127, 512:1024], stream=stream)


def tiles_to_fm(kb, hT, tiles, Tcols, ident, loader, ln_affine=None,
                prefix="tf", tp=None, eps=1e-5):
    xt = kb.sbuf(prefix + "_xt", [128, 1024], F32)
    xn = kb.sbuf(prefix + "_xn", [128, 1024], BF16)
    if tp is None:
        tp = kb.psum(prefix + "_tp", [128, 512], BF16)
    if ln_affine is not None:
        mneg = kb.sbuf(prefix + "_mneg", [128, 1], F32)
        ssum = kb.sbuf(prefix + "_ss", [128, 1], F32)
        sq = kb.sbuf(prefix + "_sq", [128, 1024], F32)
        var = kb.sbuf(prefix + "_var", [128, 1], F32)
        rstd = kb.sbuf(prefix + "_rstd", [128, 1], F32)
        g_bcast, b_bcast = ln_affine
    for j, ti in enumerate(tiles):
        loader(kb, xt, ti, j)
        if ln_affine is not None:
            kb.red(mneg[:], xt[:], ALU.add, negate=True)
            kb.ts(mneg[:], mneg[:], 1.0 / 1024, None, ALU.mult)
            kb.act(sq[:], xt[:], AF.Square, bias=mneg[:], accum_out=ssum[:])
            kb.ts(var[:], ssum[:], 1.0 / 1024, eps, ALU.mult, ALU.add)
            kb.act(rstd[:], var[:], AF.Ln)
            kb.act(rstd[:], rstd[:], AF.Exp, scale=-0.5)
            kb.ts(sq[:], xt[:], mneg[:], rstd[:], ALU.add, ALU.mult)
            kb.tmul(sq[:], sq[:], g_bcast[:])
            kb.tadd(xn[:], sq[:], b_bcast[:])
        else:
            kb.copy(xn[:], xt[:])
        for c4 in range(2):
            for j2 in range(4):
                c = c4 * 4 + j2
                kb.trans(tp[:, j2 * 128:(j2 + 1) * 128],
                         xn[:, c * 128:(c + 1) * 128], ident)
            for j2 in range(4):
                c = c4 * 4 + j2
                kb.copy(hT[:, c * Tcols + j * 128: c * Tcols + (j + 1) * 128],
                        tp[:, j2 * 128:(j2 + 1) * 128])


def build_l1_ffn(kb, x_src, xres, consts, W, tag):
    nc = kb.nc
    ident = consts["ident"]
    ones1 = consts["ones1"]
    tri = consts["tri"]
    LT = L // 128
    T = B_LOC * L
    NT = T // 128
    y_d = nc.dram_tensor(f"lf_y{tag}", [T, D], F32, kind="Internal")
    qk_d = nc.dram_tensor(f"lf_qk{tag}", [3, L, D], BF16, kind="Internal")
    with kb.scope():
        pp = kb.psum("lf_pp", [128, 512], F32)
        pp2 = kb.psum("lf_pp2", [128, 512], F32)
        ptp = kb.psum("lf_ptp", [128, 512], BF16)
        bp = kb.psum("lf_bp", [128, 512], F32)
        with kb.scope():
            xt = kb.sbuf("lf_xt", [128, 1024], F32)
            yt = kb.sbuf("lf_yt", [128, 1024], F32)
            mneg = kb.sbuf("lf_mneg", [128, 1], F32)
            ssum = kb.sbuf("lf_ss", [128, 1], F32)
            sq = kb.sbuf("lf_sq", [128, 1024], F32)
            var = kb.sbuf("lf_var", [128, 1], F32)
            rstd = kb.sbuf("lf_rstd", [128, 1], F32)
            for ti in range(NT):
                kb.dma(xt[:], x_src[ti * 128:(ti + 1) * 128, :], stream="xld")
                kb.red(mneg[:], xt[:], ALU.add, negate=True)
                kb.ts(mneg[:], mneg[:], 1.0 / 1024, None, ALU.mult)
                kb.act(sq[:], xt[:], AF.Square, bias=mneg[:], accum_out=ssum[:])
                kb.ts(var[:], ssum[:], 1.0 / 1024, 1e-5, ALU.mult, ALU.add)
                kb.act(rstd[:], var[:], AF.Ln)
                kb.act(rstd[:], rstd[:], AF.Exp, scale=-0.5)
                kb.ts(yt[:], xt[:], mneg[:], rstd[:], ALU.add, ALU.mult)
                kb.dma(y_d.ap()[ti * 128:(ti + 1) * 128, :], yt[:], stream="yst",
                       queue="gpsimd")
        for li in range(6):
            with kb.scope():
                qsB = _row_bcast_from_hbm(kb, "lf_qs", None, 1024, ones1, bp, dtype=BF16,
                                          src_ap=W["L1_ffn_q_scale"].ap()[li],
                                          repeat=16)
                ksB = _row_bcast_from_hbm(kb, "lf_ks", None, 1024, ones1, bp, dtype=BF16,
                                          src_ap=W["L1_ffn_k_scale"].ap()[li],
                                          repeat=16)
                gB = _row_bcast_from_hbm(kb, "lf_g", None, 1024, ones1, bp, dtype=BF16,
                                         src_ap=W["L1_ffn_ln_g"].ap()[li])
                bB = _row_bcast_from_hbm(kb, "lf_b", None, 1024, ones1, bp, dtype=BF16,
                                         src_ap=W["L1_ffn_ln_b"].ap()[li])
                b2B = _row_bcast_from_hbm(kb, "lf_b2", None, 1024, ones1, bp, dtype=BF16,
                                          src_ap=W["L1_ffn_b2"].ap()[li])
                b1c = (load_vec_chunks_pad(kb, "lf_b1a",
                                           W["L1_ffn_b1"].ap()[li][0:2730], 2730),
                       load_vec_chunks_pad(kb, "lf_b1g",
                                           W["L1_ffn_b1"].ap()[li][2730:5460], 2730))
                for b in range(B_LOC):
                    tiles = list(range(b * LT, (b + 1) * LT))
                    _l1ffn_attn(kb, y_d, qk_d, consts, W, li, b, tiles, qsB,
                                ksB, pp, pp2, ptp)
                    _l1ffn_ffn(kb, y_d, consts, W, li, b, tiles, gB, bB, b1c,
                               b2B, pp, pp2, ptp)
        with kb.scope():
            xt = kb.sbuf("lf_ext", [128, 1024], F32)
            yt = kb.sbuf("lf_eyt", [128, 1024], F32)
            for ti in range(NT):
                kb.dma(xt[:], x_src[ti * 128:(ti + 1) * 128, :], stream="xld")
                kb.dma(yt[:], y_d.ap()[ti * 128:(ti + 1) * 128, :], stream="yld")
                kb.tadd(yt[:], yt[:], xt[:], chan="pool")
                kb.dma(xres[ti * 128:(ti + 1) * 128, :], yt[:], stream="xst",
                       queue="gpsimd")
        kb.barrier()


def _l1ffn_attn(kb, y_d, qk_d, consts, W, li, b, tiles, qsB, ksB, pp, pp2, ptp):
    ident = consts["ident"]
    tri = consts["tri"]
    LT = L // 128
    with kb.scope():
        QhT = kb.sbuf("lfa_QhT", [128, 8 * L], BF16)
        KhT = kb.sbuf("lfa_KhT", [128, 8 * L], BF16)
        with kb.scope():
            with kb.scope():
                xsT = kb.sbuf("lfa_xsT", [128, 8 * L], BF16)
                ld = lambda kb_, xt, ti, j: shifted_tile_loader(
                    kb_, y_d.ap(), xt, ti, j == 0)
                tiles_to_fm(kb, xsT, tiles, L, ident, ld, prefix=f"lfa_tf{li}{b}",
                            tp=ptp)
                wq = kb.sbuf("lfa_wq", [128, 8 * 1024], BF16)
                wstg = kb.sbuf("lfa_wstg", [128, 1024], F32)
                qtile = kb.sbuf("lfa_qtile", [128, 1024], BF16)
                for part in range(3):
                    load_w_direct(kb, "lfa_w",
                                  W["L1_ffn_Wqkv"].ap()[li][:, part * 1024:(part + 1) * 1024],
                                  1024, 1024, wsb=wq, stage=wstg)
                    for tt_ in range(LT):
                        for oh in range(2):
                            ps = pp if oh == 0 else pp2
                            for ci in range(8):
                                kb.mm(ps[:], xsT[:, ci * L + tt_ * 128: ci * L + (tt_ + 1) * 128],
                                      wq[:, ci * 1024 + oh * 512: ci * 1024 + (oh + 1) * 512],
                                      start=(ci == 0), stop=(ci == 7))
                            kb.copy(qtile[:, oh * 512:(oh + 1) * 512], ps[:])
                        kb.dma(qk_d.ap()[part][tt_ * 128:(tt_ + 1) * 128, :],
                               qtile[:], stream="qkst", queue="gpsimd")
            with kb.scope():
                cs = kb.sbuf("lfa_cs", [128, 1024], BF16)
                ss = kb.sbuf("lfa_ss", [128, 1024], BF16)
                sq = kb.sbuf("lfa_sq", [128, 1024], F32)
                ssq = kb.sbuf("lfa_ssq", [128, 16], F32)
                rn = kb.sbuf("lfa_rn", [128, 16], F32)
                sw = kb.sbuf("lfa_sw", [128, 1024], F32)
                t1 = kb.sbuf("lfa_t1", [128, 1024], F32)
                qb = kb.sbuf("lfa_qb", [128, 1024], BF16)
                qld = kb.sbuf("lfa_qld", [128, 1024], BF16)
                for j, ti in enumerate(tiles):
                    for which, sc, tabc, tabs, hdst in (
                            (0, qsB, "c_qcs", "c_qss", QhT),
                            (1, ksB, "c_kcs", "c_kss", KhT)):
                        kb.dma(qld[:], qk_d.ap()[which][j * 128:(j + 1) * 128, :],
                               stream="qkld")
                        q = qld[:, :]
                        kb.dma(cs[:], consts[tabc][j * 128:(j + 1) * 128, :],
                               stream="rtl")
                        kb.dma(ss[:], consts[tabs][j * 128:(j + 1) * 128, :],
                               stream="rtl")
                        kb.act(sq[:], q, AF.Square)
                        kb.red(ssq[:], sq[:, :].rearrange("p (h e) -> p h e", h=16),
                               ALU.add)
                        kb.act(rn[:], ssq[:], AF.Sqrt)
                        kb.ts(rn[:], rn[:], 1e-12, None, ALU.max)
                        kb.recip(rn[:], rn[:])
                        rnb = rn[:, :].unsqueeze(2).broadcast_to((128, 16, 64))
                        kb.op("dve", lambda e: e.tensor_tensor(
                            t1[:, :].rearrange("p (h e) -> p h e", h=16),
                            q.rearrange("p (h e) -> p h e", h=16), rnb, ALU.mult),
                            [q, rn[:]], [t1[:]])
                        kb.tmul(t1[:], t1[:], sc[:])
                        swv = sw[:, :].rearrange("p (h v e) -> p h v e", h=16, v=2)
                        t1v = t1[:, :].rearrange("p (h v e) -> p h v e", h=16, v=2)
                        kb.op("dve", lambda e: e.tensor_copy(swv[:, :, 0, :], t1v[:, :, 1, :]),
                              [t1[:]], [sw[:]])
                        kb.op("dve", lambda e: e.tensor_copy(swv[:, :, 1, :], t1v[:, :, 0, :]),
                              [t1[:]], [sw[:]])
                        kb.tmul(sw[:], sw[:], ss[:])
                        kb.tt(qb[:], t1[:], cs[:], ALU.mult)
                        kb.op("dve", lambda e: e.scalar_tensor_tensor(
                            qb[:], sw[:], 1.0, qb[:], ALU.mult, ALU.add),
                            [sw[:], qb[:]], [qb[:]])
                        for c4 in range(2):
                            for j2 in range(4):
                                c = c4 * 4 + j2
                                kb.trans(ptp[:, j2 * 128:(j2 + 1) * 128],
                                         qb[:, c * 128:(c + 1) * 128], ident)
                            for j2 in range(4):
                                c = c4 * 4 + j2
                                kb.copy(hdst[:, c * L + j * 128: c * L + (j + 1) * 128],
                                        ptp[:, j2 * 128:(j2 + 1) * 128])
        aT = kb.sbuf("lfa_aT", [128, 8 * L], BF16)
        with kb.scope():
            vwin = kb.sbuf("lfa_vwin", [128, 4 * 1024], BF16)
            Sall = kb.sbuf("lfa_S", [128, 16 * 512], F32)
            Pall = kb.sbuf("lfa_P", [128, 16 * 512], BF16)
            negm = kb.sbuf("lfa_negm", [128, 16], F32)
            rs = kb.sbuf("lfa_rs", [128, 16], F32)
            PT = kb.sbuf("lfa_PT", [128, 512], BF16)
            pav = kb.psum("lfa_pav", [64, 128], F32)
            vloaded = set()
            for qt in range(LT):
                w = qt // 2
                kt0 = 0 if w == 0 else (w - 1) * 2
                nkt = qt - kt0 + 1
                kw = nkt * 128
                for kt in range(kt0, qt + 1):
                    if kt not in vloaded:
                        kb.dma(vwin[:, (kt % 4) * 1024:(kt % 4 + 1) * 1024],
                               qk_d.ap()[2][kt * 128:(kt + 1) * 128, :],
                               stream="vld")
                        vloaded.add(kt)
                for h in range(16):
                    ci, pb = h // 2, (h % 2) * 64
                    kb.mm(pp[:, 0:kw],
                          QhT[:, ci * L + qt * 128: ci * L + (qt + 1) * 128][pb:pb + 64, :],
                          KhT[:, ci * L + kt0 * 128: ci * L + kt0 * 128 + kw][pb:pb + 64, :],
                          start=True, stop=True)
                    kb.copy(Sall[:, h * 512:h * 512 + kw], pp[:, 0:kw])
                sv = Sall[:, :].rearrange("p (h k) -> p h k", h=16)[:, :, 0:kw]
                dv = Sall[:, :].rearrange("p (h k) -> p h k", h=16)[:, :, kw - 128:kw]
                trib = tri.unsqueeze(1).broadcast_to((128, 16, 128))
                kb.op("dve", lambda e: e.tensor_tensor(dv, dv, trib, ALU.add),
                      [Sall[:], tri], [Sall[:]])
                kb.red(negm[:], sv, ALU.max, negate=True)
                nmb = negm[:, :].unsqueeze(2).broadcast_to((128, 16, kw))
                kb.op("dve", lambda e: e.tensor_tensor(sv, sv, nmb, ALU.add),
                      [Sall[:], negm[:]], [Sall[:]])
                pv = Pall[:, :].rearrange("p (h k) -> p h k", h=16)[:, :, 0:kw]
                kb.act(pv, sv, AF.Exp, scale=8.0)
                kb.red(rs[:], pv, ALU.add)
                kb.recip(rs[:], rs[:])
                rsb = rs[:, :].unsqueeze(2).broadcast_to((128, 16, kw))
                kb.op("dve", lambda e: e.tensor_tensor(pv, pv, rsb, ALU.mult),
                      [Pall[:], rs[:]], [Pall[:]])
                for h in range(16):
                    ci, pb = h // 2, (h % 2) * 64
                    for j2 in range(nkt):
                        kb.trans(ptp[:, j2 * 128:(j2 + 1) * 128],
                                 Pall[:, h * 512 + j2 * 128: h * 512 + (j2 + 1) * 128],
                                 ident)
                    for j2 in range(nkt):
                        kb.copy(PT[:, j2 * 128:(j2 + 1) * 128],
                                ptp[:, j2 * 128:(j2 + 1) * 128])
                    for j2 in range(nkt):
                        kt = kt0 + j2
                        kb.mm(pav[:], vwin[:, (kt % 4) * 1024 + h * 64: (kt % 4) * 1024 + (h + 1) * 64],
                              PT[:, j2 * 128:(j2 + 1) * 128],
                              start=(j2 == 0), stop=(j2 == nkt - 1))
                    kb.copy(aT[:, ci * L + qt * 128: ci * L + (qt + 1) * 128][pb:pb + 64, :],
                            pav[:])
        with kb.scope():
            wo = kb.sbuf("lfa_wo", [128, 8 * 1024], BF16)
            wstg = kb.sbuf("lfa_wstg2", [128, 1024], F32)
            otm = kb.sbuf("lfa_otm", [128, 1024], F32)
            yt2 = kb.sbuf("lfa_yt2", [128, 1024], F32)
            load_w_direct(kb, "lfa_w2", W["L1_ffn_Wo"].ap()[li], 1024, 1024,
                          wsb=wo, stage=wstg)
            for j, ti in enumerate(tiles):
                for oh in range(2):
                    ps = pp if oh == 0 else pp2
                    for ci in range(8):
                        kb.mm(ps[:], aT[:, ci * L + j * 128: ci * L + (j + 1) * 128],
                              wo[:, ci * 1024 + oh * 512: ci * 1024 + (oh + 1) * 512],
                              start=(ci == 0), stop=(ci == 7))
                    kb.copy(otm[:, oh * 512:(oh + 1) * 512], ps[:])
                kb.dma(yt2[:], y_d.ap()[ti * 128:(ti + 1) * 128, :], stream="yld")
                kb.tadd(otm[:], otm[:], yt2[:], chan="pool")
                kb.dma(y_d.ap()[ti * 128:(ti + 1) * 128, :], otm[:], stream="yst",
                       queue="gpsimd")
    kb.barrier()


def _l1ffn_ffn(kb, y_d, consts, W, li, b, tiles, gB, bB, b1c, b2B, pp, pp2, ptp):
    ident = consts["ident"]
    LT = L // 128
    NI = 2730
    OC = 22
    b1a, b1g = b1c
    with kb.scope():
        xnT = kb.sbuf("lff_xnT", [128, 8 * L], BF16)
        ld = lambda kb_, xt, ti, j: shifted_tile_loader(kb_, y_d.ap(), xt, ti,
                                                        j == 0)
        with kb.scope():
            tiles_to_fm(kb, xnT, tiles, L, ident, ld, ln_affine=(gB, bB),
                        prefix=f"lff_tf{li}{b}", tp=ptp)
        otacc = kb.sbuf("lff_otacc", [128, (L // 128) * 1024], BF16)
        otm = kb.sbuf("lff_otm", [128, 1024], F32)
        yt2 = kb.sbuf("lff_yt2", [128, 1024], F32)
        pd = kb.psum("lff_pd", [128, 512], F32)
        for hf in range(2):
            oc0 = hf * 11
            nmm = sum(1 for oc in range(11) if NI - (oc0 + oc) * 128 > 0)
            with kb.scope():
                w1h = kb.sbuf("lff_w1h", [128, 8 * 2816], BF16)
                w2h = kb.sbuf("lff_w2h", [128, 11 * 1024], BF16)
                with kb.scope():
                    wstg = kb.sbuf("lff_wstg", [128, 1408], F32)
                    wid = min(1408, NI - oc0 * 128)
                    for ci in range(8):
                        for gpart, off in ((0, oc0 * 128), (1, NI + oc0 * 128)):
                            kb.dma(wstg[:, 0:wid],
                                   W["L1_ffn_W1"].ap()[li][ci * 128:(ci + 1) * 128,
                                                           off:off + wid],
                                   stream="wld")
                            kb.copy(w1h[:, ci * 2816 + gpart * 1408:
                                        ci * 2816 + gpart * 1408 + wid],
                                    wstg[:, 0:wid], chan="pool")
                    for oc in range(nmm):
                        g_oc = oc0 + oc
                        r = min(128, NI - g_oc * 128)
                        kb.dma(wstg[0:r, 0:1024],
                               W["L1_ffn_W2"].ap()[li][g_oc * 128:g_oc * 128 + r, :],
                               stream="wld")
                        kb.copy(w2h[0:r, oc * 1024:(oc + 1) * 1024],
                                wstg[0:r, 0:1024], chan="pool")
                prodT = kb.sbuf("lff_prod", [128, 11 * 512], BF16)
                ga = kb.sbuf("lff_ga", [128, 512], BF16)
                for t0 in range(0, L, 512):
                    tc = t0 // 512
                    tw = min(512, L - t0)
                    for oc in range(nmm):
                        g_oc = oc0 + oc
                        r = min(128, NI - g_oc * 128)
                        for ci in range(8):
                            kb.mm(pp[0:r, 0:tw],
                                  w1h[:, ci * 2816 + 1408 + oc * 128: ci * 2816 + 1408 + oc * 128 + r],
                                  xnT[:, ci * L + t0: ci * L + t0 + tw],
                                  start=(ci == 0), stop=(ci == 7))
                        for ci in range(8):
                            kb.mm(pp2[0:r, 0:tw],
                                  w1h[:, ci * 2816 + oc * 128: ci * 2816 + oc * 128 + r],
                                  xnT[:, ci * L + t0: ci * L + t0 + tw],
                                  start=(ci == 0), stop=(ci == 7))
                        kb.op("act", lambda e: e.activation(
                            ga[0:r, 0:tw], pp[0:r, 0:tw], AF.Gelu,
                            bias=b1g[0:r, g_oc:g_oc + 1]), [pp[:], b1g[:]], [ga[:]])
                        kb.op("dve", lambda e: e.scalar_tensor_tensor(
                            prodT[0:r, oc * 512:oc * 512 + tw], pp2[0:r, 0:tw],
                            b1a[0:r, g_oc:g_oc + 1], ga[0:r, 0:tw], ALU.add, ALU.mult),
                            [pp2[:], ga[:], b1a[:]], [prodT[:]])
                    for tt_ in range(tw // 128):
                        gt = tc * 4 + tt_
                        for oh in range(2):
                            for oc in range(nmm):
                                g_oc = oc0 + oc
                                r = min(128, NI - g_oc * 128)
                                kb.mm(pd[:],
                                      prodT[0:r, oc * 512 + tt_ * 128: oc * 512 + (tt_ + 1) * 128],
                                      w2h[0:r, oc * 1024 + oh * 512: oc * 1024 + (oh + 1) * 512],
                                      start=(oc == 0), stop=(oc == nmm - 1))
                            if hf == 0:
                                kb.copy(otacc[:, gt * 1024 + oh * 512: gt * 1024 + (oh + 1) * 512],
                                        pd[:])
                            else:
                                kb.tadd(otacc[:, gt * 1024 + oh * 512: gt * 1024 + (oh + 1) * 512],
                                        otacc[:, gt * 1024 + oh * 512: gt * 1024 + (oh + 1) * 512],
                                        pd[:])
        for gt in range(L // 128):
            ti = tiles[gt]
            kb.dma(yt2[:], y_d.ap()[ti * 128:(ti + 1) * 128, :], stream="yld")
            kb.tadd(otm[:], otacc[:, gt * 1024:(gt + 1) * 1024], b2B[:])
            kb.tadd(otm[:], otm[:], yt2[:], chan="pool")
            kb.dma(y_d.ap()[ti * 128:(ti + 1) * 128, :], otm[:], stream="yst",
                   queue="gpsimd")
    kb.barrier()



def input_specs():
    return [
        ("L0_attn_a_x", [B_LOC, L, D]), ("L0_attn_a_wq", [D, D]), ("L0_attn_a_bq", [D]),
        ("L0_attn_a_wk", [D, D]), ("L0_attn_a_bk", [D]), ("L0_attn_a_wv", [D, D]),
        ("L0_attn_a_bv", [D]),
        ("L0_attn_b_ln_att_g", [D]), ("L0_attn_b_ln_att_b", [D]),
        ("L0_attn_b_ln_ffn_g", [D]), ("L0_attn_b_ln_ffn_b", [D]),
        ("L0_attn_b_att_time_decay", [D]), ("L0_attn_b_att_time_first", [D]),
        ("L0_attn_b_att_mix_k", [D]), ("L0_attn_b_att_mix_v", [D]),
        ("L0_attn_b_att_mix_r", [D]),
        ("L0_attn_b_att_wk", [D, D]), ("L0_attn_b_att_wv", [D, D]),
        ("L0_attn_b_att_wr", [D, D]), ("L0_attn_b_att_wo", [D, D]),
        ("L0_attn_b_ffn_mix_k", [D]), ("L0_attn_b_ffn_mix_r", [D]),
        ("L0_attn_b_ffn_wk", [D, 4096]), ("L0_attn_b_ffn_wv", [4096, D]),
        ("L0_attn_b_ffn_wr", [D, D]),
        ("L0_ffn_gate_w", [D, 8]), ("L0_ffn_Wg", [8, D, 2816]),
        ("L0_ffn_Wu", [8, D, 2816]), ("L0_ffn_Wd", [8, 2816, D]),
        ("L0_ffn_Sg", [D, 1408]), ("L0_ffn_Su", [D, 1408]), ("L0_ffn_Sd", [1408, D]),
        ("L1_attn_a_gamma", [D]), ("L1_attn_a_beta", [D]),
        ("L1_attn_a_W_in", [4096, D]), ("L1_attn_a_b_in", [4096]),
        ("L1_attn_a_conv_w", [2048, 15]), ("L1_attn_a_conv_b", [2048]),
        ("L1_attn_a_W_x", [96, 2048]), ("L1_attn_a_W_dt", [2048, 64]),
        ("L1_attn_a_b_dt", [2048]), ("L1_attn_a_A_log", [2048, 16]),
        ("L1_attn_a_D_param", [2048]), ("L1_attn_a_W_out", [D, 2048]),
        ("L1_attn_a_b_out", [D]),
        ("L1_attn_b_W_q", [D, D]), ("L1_attn_b_b_q", [D]),
        ("L1_attn_b_W_k", [D, D]), ("L1_attn_b_b_k", [D]),
        ("L1_attn_b_W_v", [D, D]), ("L1_attn_b_b_v", [D]),
        ("L1_attn_b_W_o", [D, D]), ("L1_attn_b_b_o", [D]),
        ("L1_ffn_Wqkv", [6, D, 3072]), ("L1_ffn_q_scale", [6, 64]),
        ("L1_ffn_k_scale", [6, 64]), ("L1_ffn_Wo", [6, 1024, D]),
        ("L1_ffn_ln_g", [6, D]), ("L1_ffn_ln_b", [6, D]),
        ("L1_ffn_W1", [6, D, 5460]), ("L1_ffn_b1", [6, 5460]),
        ("L1_ffn_W2", [6, 2730, D]), ("L1_ffn_b2", [6, D]),
    ]


def declare_inputs(nc):
    W = {}
    for name, shape in input_specs():
        W[name] = nc.dram_tensor(name, shape, F32, kind="ExternalInput")
    return W


def build_consts(kb):
    nc = kb.nc
    c = {}
    ident_h = nc.dram_tensor("c_ident", [128, 128], BF16, kind="ExternalInput")
    ident = kb.sbuf("ident", [128, 128], BF16)
    kb.dma(ident[:], ident_h.ap(), stream="ld")
    ones1 = kb.sbuf("ones1", [1, 128], BF16)
    kb.memset(ones1[:], 1.0)
    tri_h = nc.dram_tensor("c_tri", [128, 128], F32, kind="ExternalInput")
    tri = kb.sbuf("tri", [128, 128], F32)
    kb.dma(tri[:], tri_h.ap(), stream="ld")
    c["ident"] = ident[:]
    c["ones1"] = ones1[:]
    c["tri"] = tri[:]
    for name in ("c_qcs", "c_qss", "c_kcs", "c_kss"):
        c[name] = nc.dram_tensor(name, [L, 1024], BF16, kind="ExternalInput").ap()
    return c


def all_stages():
    return [
        ("l0_attn_a", lambda kb, s, d, c, W: build_l0_attn_a(kb, s, d, c, W)),
        ("l0_ffn_0", lambda kb, s, d, c, W: build_l0_ffn(kb, s, d, c, W, "0")),
        ("l0_attn_b", lambda kb, s, d, c, W: build_l0_attn_b(kb, s, d, c, W)),
        ("l0_ffn_1", lambda kb, s, d, c, W: build_l0_ffn(kb, s, d, c, W, "1")),
        ("l1_attn_a", lambda kb, s, d, c, W: build_l1_attn_a(kb, s, d, c, W)),
        ("l1_ffn_0", lambda kb, s, d, c, W: build_l1_ffn(kb, s, d, c, W, "0")),
        ("l1_attn_b", lambda kb, s, d, c, W: build_l1_attn_b(kb, s, d, c, W)),
        ("l1_ffn_1", lambda kb, s, d, c, W: build_l1_ffn(kb, s, d, c, W, "1")),
    ]


def build_program(stages=None):
    nc = bass.Bass("TRN2", target_bir_lowering=False)
    es = contextlib.ExitStack()
    kb = KB(nc, es)
    W = declare_inputs(nc)
    T = B_LOC * L
    out = nc.dram_tensor("out", [T, D], F32, kind="ExternalOutput")
    xres = nc.dram_tensor("xres", [T, D], F32, kind="Internal")
    consts = build_consts(kb)
    x_in = W["L0_attn_a_x"].ap().rearrange("b l d -> (b l) d")
    stg = all_stages()
    if stages is not None:
        stg = [s for s in stg if s[0] in stages]
    src = x_in
    for i, (name, fn) in enumerate(stg):
        dst = out.ap() if i == len(stg) - 1 else xres.ap()
        fn(kb, src, dst, consts, W)
        src = dst
    es.close()
    return nc


def host_consts():
    import ml_dtypes
    c = {}
    c["c_ident"] = np.eye(128, dtype=np.float32).astype(ml_dtypes.bfloat16)
    ii = np.arange(128)
    c["c_tri"] = np.where(ii[None, :] <= ii[:, None], 0.0, -1e30).astype(np.float32)
    dh = 64
    half = np.arange(0, dh, 2, dtype=np.float64)
    inv_freq = 1.0 / 10000.0 ** (half / dh)
    t = np.arange(L, dtype=np.float64)
    freqs = np.concatenate([t[:, None] * inv_freq] * 2, -1)
    base_scale = (half + 0.4 * dh) / (1.4 * dh)
    power = (t - L // 2) / 128.0
    scale = np.concatenate([base_scale[None, :] ** power[:, None]] * 2, -1)
    sign = np.where(np.arange(dh) < 32, -1.0, 1.0)
    co, si = np.cos(freqs), np.sin(freqs)
    tile16 = lambda a: np.tile(a.astype(np.float32), (1, 16)).astype(ml_dtypes.bfloat16)
    c["c_qcs"] = tile16(co * scale)
    c["c_qss"] = tile16(si * scale * sign)
    c["c_kcs"] = tile16(co / scale)
    c["c_kss"] = tile16(si / scale * sign)
    return c


def kernel(**inputs):
    nc = build_program()
    consts = host_consts()
    in_maps = []
    for core in range(N_CORES):
        m = {}
        for k, v in inputs.items():
            if k == "L0_attn_a_x":
                m[k] = np.ascontiguousarray(v[core * B_LOC:(core + 1) * B_LOC])
            else:
                m[k] = np.ascontiguousarray(v)
        m.update(consts)
        in_maps.append(m)
    res = run_bass_kernel_spmd(nc, in_maps, core_ids=list(range(N_CORES)))
    out = np.concatenate([np.asarray(r["out"]).reshape(B_LOC, L, D)
                          for r in res.results], axis=0)
    return out


# version 02 (final)
# speedup vs baseline: 1.0374x; 1.0374x over previous
import contextlib
import numpy as np
import concourse.bass as bass
import concourse.mybir as mybir
from concourse.bass_utils import run_bass_kernel_spmd

AF = mybir.ActivationFunctionType
ALU = mybir.AluOpType
AX = mybir.AxisListType
F32 = mybir.dt.float32
BF16 = mybir.dt.bfloat16

B_TOT = 16
B_LOC = 2
L = 2048
D = 1024
N_CORES = 8


_CHAN_ENG = {
    "pe": "tensor", "act": "scalar", "dve": "vector", "pool": "gpsimd",
}


class KB:

    def __init__(self, nc, es):
        self.nc = nc
        self.es_root = es
        self.es_stack = [es]
        self.sem = {}
        self.seq = {c: 0 for c in _CHAN_ENG}
        self.incs = {c: [] for c in _CHAN_ENG}
        self.inst = {c: {} for c in _CHAN_ENG}
        self.dma_seq = {}
        self.dma_queue = {}
        self.clock = {e: {} for e in ("tensor", "scalar", "vector", "gpsimd", "sync")}
        self.tstate = {}
        for c in _CHAN_ENG:
            self.sem[c] = es.enter_context(nc.semaphore("S_" + c))

    def eng(self, name):
        return getattr(self.nc, {"tensor": "tensor", "scalar": "scalar",
                                 "vector": "vector", "gpsimd": "gpsimd",
                                 "sync": "sync"}[name])

    def _tname(self, ap):
        t = ap.tensor if hasattr(ap, "tensor") else ap
        return t.name

    def _dep_key(self, ap):
        return self._tname(ap)

    def _edge(self, waits, chan_or_stream, seq):
        cur = waits.get(chan_or_stream, 0)
        if seq > cur:
            waits[chan_or_stream] = seq

    def _gather(self, reads, writes):
        waits = {}
        for ap in reads:
            st = self.tstate.get(self._dep_key(ap))
            if st and st[0] is not None:
                self._edge(waits, st[0][0], st[0][1])
        for ap in writes:
            st = self.tstate.get(self._dep_key(ap))
            if st:
                if st[0] is not None:
                    self._edge(waits, st[0][0], st[0][1])
                for c, s in st[1].items():
                    self._edge(waits, c, s)
        return waits

    def _wait_val(self, chan, seq):
        if chan in _CHAN_ENG:
            incs = self.incs[chan]
            if not incs or seq > incs[-1]:
                ins = self.inst[chan].get(seq)
                assert ins is not None, f"no instr {chan}#{seq}"
                ins.then_inc(self.sem[chan], 1)
                incs.append(seq)
                return len(incs)
            import bisect
            i = bisect.bisect_left(incs, seq)
            return i + 1
        return 16 * self.dma_seq[chan]

    def _emit_waits(self, engine, waits, exclude_self_pe=False):
        e = self.eng(engine)
        for chan, seq in sorted(waits.items()):
            if exclude_self_pe and chan == "pe" and engine == "tensor":
                continue
            val = self._wait_val(chan, seq)
            if self.clock[engine].get(chan, 0) < val:
                e.wait_ge(self.sem[chan], val)
                self.clock[engine][chan] = val

    def _record(self, chan, seq, reads, writes):
        for ap in reads:
            k = self._dep_key(ap)
            st = self.tstate.setdefault(k, [None, {}])
            st[1][chan] = seq
        for ap in writes:
            k = self._dep_key(ap)
            self.tstate[k] = [(chan, seq), {}]

    def op(self, chan, fn, reads, writes):
        if chan == "pool":
            chan = "dve"
        engine = _CHAN_ENG[chan]
        waits = self._gather(reads, writes)
        self._emit_waits(engine, waits, exclude_self_pe=(chan == "pe"))
        ins = fn(self.eng(engine))
        self.seq[chan] += 1
        self.inst[chan][self.seq[chan]] = ins
        self._record(chan, self.seq[chan], reads, writes)
        return ins

    def dma(self, dst, src, stream="ld", queue="sync"):
        if stream not in self.dma_seq:
            self.dma_seq[stream] = 0
            self.dma_queue[stream] = queue
            self.sem[stream] = self.es_root.enter_context(
                self.nc.semaphore("S_" + stream))
        assert self.dma_queue[stream] == queue
        waits = self._gather([src], [dst])
        if self.dma_seq[stream]:
            self._edge(waits, stream, self.dma_seq[stream])
        self._emit_waits(queue, waits)
        ins = self.eng(queue).dma_start(out=dst, in_=src)
        ins.then_inc(self.sem[stream], 16)
        self.dma_seq[stream] += 1
        for ap in [src]:
            self.tstate.setdefault(self._dep_key(ap), [None, {}])[1][stream] = \
                self.dma_seq[stream]
        self.tstate[self._dep_key(dst)] = [(stream, self.dma_seq[stream]), {}]
        return ins

    def barrier(self):
        for stream, n in sorted(self.dma_seq.items()):
            if n:
                val = 16 * n
                if self.clock["sync"].get(stream, 0) < val:
                    self.nc.sync.wait_ge(self.sem[stream], val)
                    self.clock["sync"][stream] = val
        for c in _CHAN_ENG:
            if self.seq[c]:
                val = self._wait_val(c, self.seq[c])
                if self.clock["sync"].get(c, 0) < val:
                    self.nc.sync.wait_ge(self.sem[c], val)
                    self.clock["sync"][c] = val
        self.nc.all_engine_barrier()
        for k in self.tstate:
            self.tstate[k] = [None, {}]
        for e in self.clock:
            for c in list(self.clock[e]):
                pass

    def tt(self, out, a, b, op, chan="dve"):
        return self.op(chan, lambda e: e.tensor_tensor(out, a, b, op),
                       [a, b], [out])

    def tadd(self, out, a, b, chan="dve"):
        return self.tt(out, a, b, ALU.add, chan)

    def tmul(self, out, a, b, chan="dve"):
        return self.tt(out, a, b, ALU.mult, chan)

    def tsub(self, out, a, b, chan="dve"):
        return self.tt(out, a, b, ALU.subtract, chan)

    def ts(self, out, in0, s1, s2, op0, op1=ALU.bypass, chan="dve"):
        reads = [in0] + [s for s in (s1, s2) if hasattr(s, "tensor")]
        return self.op(chan, lambda e: e.tensor_scalar(out, in0, s1, s2, op0, op1),
                       reads, [out])

    def stt(self, out, in0, scalar, in1, op0, op1, chan="dve"):
        reads = [in0, in1] + ([scalar] if hasattr(scalar, "tensor") else [])
        return self.op(chan,
                       lambda e: e.scalar_tensor_tensor(out, in0, scalar, in1, op0, op1),
                       reads, [out])

    def copy(self, out, in_, chan="dve"):
        if chan == "act":
            return self.op(chan, lambda e: e.copy(out, in_), [in_], [out])
        return self.op(chan, lambda e: e.tensor_copy(out, in_), [in_], [out])

    def memset(self, ap, v, chan="dve"):
        return self.op(chan, lambda e: e.memset(ap, v), [], [ap])

    def act(self, out, in_, func, bias=0.0, scale=1.0, accum_out=None):
        reads = [in_] + [b for b in (bias, scale) if hasattr(b, "tensor")]
        writes = [out] + ([accum_out] if accum_out is not None else [])
        return self.op("act",
                       lambda e: e.activation(out, in_, func, bias=bias,
                                              scale=scale, accum_out=accum_out),
                       reads, writes)

    def red(self, out, in_, op, negate=False, chan="dve"):
        return self.op(chan,
                       lambda e: e.tensor_reduce(out, in_, AX.X, op, negate=negate),
                       [in_], [out])

    def recip(self, out, in_):
        return self.op("dve", lambda e: e.reciprocal(out, in_), [in_], [out])

    def scan(self, out, d0, d1, init, op0=ALU.mult, op1=ALU.add, chan="dve"):
        reads = [d0, d1] + ([init] if hasattr(init, "tensor") else [])
        return self.op(chan,
                       lambda e: e.tensor_tensor_scan(out, d0, d1, init, op0, op1),
                       reads, [out])

    def mm(self, out, lhsT, rhs, start, stop):
        return self.op("pe", lambda e: e.matmul(out, lhsT, rhs, start=start, stop=stop),
                       [lhsT, rhs], [out])

    def trans(self, out_psum, in_sb, ident):
        return self.op("pe", lambda e: e.transpose(out_psum, in_sb, ident),
                       [in_sb, ident], [out_psum])

    def select(self, out, mask, on_true, on_false):
        return self.op("dve", lambda e: e.select(out, mask, on_true, on_false),
                       [mask, on_true, on_false], [out])

    def max8(self, out, in_):
        return self.op("dve", lambda e: e.max(out=out, in_=in_), [in_], [out])

    _serial = 0

    def _uniq(self, name):
        KB._serial += 1
        return f"{name}_{KB._serial}"

    def sbuf(self, name, shape, dtype):
        return self.es_stack[-1].enter_context(
            self.nc.sbuf_tensor(self._uniq(name), shape, dtype))

    def psum(self, name, shape, dtype=F32):
        return self.es_stack[-1].enter_context(
            self.nc.psum_tensor(self._uniq(name), shape, dtype))

    @contextlib.contextmanager
    def scope(self):
        es = contextlib.ExitStack()
        self.es_stack.append(es)
        try:
            yield
        finally:
            self.barrier()
            self.es_stack.pop()
            es.close()



def load_vec_chunks(kb, name, v_hbm, n, stream="ld"):
    nch = n // 128
    t = kb.sbuf(name, [128, nch], F32)
    ap = v_hbm.ap()
    for c in range(nch):
        kb.dma(t[:, c:c + 1], ap[c * 128:(c + 1) * 128].unsqueeze(1), stream=stream)
    return t


def load_w_T(kb, name, w_hbm, O, I, ident, stream="wld", stage=None, wT=None):
    OC, IC = O // 128, I // 128
    if wT is None:
        wT = kb.sbuf(name, [128, IC * O], BF16)
    if stage is None:
        stage = (kb.sbuf(name + "_stg", [128, I], F32),
                 kb.sbuf(name + "_stgb", [128, I], BF16),
                 kb.psum(name + "_stgp", [128, 512], BF16))
    stg, stgb, stgp = stage
    ap = w_hbm if isinstance(w_hbm, bass.AP) else w_hbm.ap()
    for oc in range(OC):
        kb.dma(stg[:, 0:I], ap[oc * 128:(oc + 1) * 128, :],
               stream=f"{stream}{oc % 4}")
        kb.copy(stgb[:, 0:I], stg[:, 0:I], chan="pool")
        for ic4 in range((IC + 3) // 4):
            ics = range(ic4 * 4, min(ic4 * 4 + 4, IC))
            for j, ic in enumerate(ics):
                kb.trans(stgp[:, j * 128:(j + 1) * 128],
                         stgb[:, ic * 128:(ic + 1) * 128], ident)
            for j, ic in enumerate(ics):
                kb.copy(wT[:, ic * O + oc * 128: ic * O + (oc + 1) * 128],
                        stgp[:, j * 128:(j + 1) * 128])
    return wT


def load_w_direct(kb, name, w_hbm, I, O, stream="wld", wsb=None, stage=None):
    IC = I // 128
    if wsb is None:
        wsb = kb.sbuf(name, [128, IC * O], BF16)
    if stage is None:
        stage = kb.sbuf(name + "_stg", [128, O], F32)
    ap = w_hbm if isinstance(w_hbm, bass.AP) else w_hbm.ap()
    stages = [stage] if not isinstance(stage, list) else stage
    for ic in range(IC):
        stg = stages[ic % len(stages)]
        kb.dma(stg[:, 0:O], ap[ic * 128:(ic + 1) * 128, :],
               stream=f"{stream}{ic % 4}")
        kb.copy(wsb[:, ic * O:(ic + 1) * O], stg[:, 0:O], chan="pool")
    return wsb


def bcast_row(kb, name, row_sb, n, ones1, psum=None, out=None, dtype=F32):
    if out is None:
        out = kb.sbuf(name, [128, n], dtype)
    own = psum is None
    if own:
        psum = kb.psum(name + "_p", [128, 512], F32)
    for j in range(0, n, 512):
        w = min(512, n - j)
        kb.mm(psum[:, 0:w], ones1, row_sb[:, j:j + w], start=True, stop=True)
        kb.copy(out[:, j:j + w], psum[:, 0:w])
    return out


def ln_tiles_to_fm(kb, x_src_ap, hT, tiles, Tcols, ident, g_bcast=None, b_bcast=None,
                   eps=1e-5, xtm_out=None, stream="xld", prefix="ln",
                   out_dtype=BF16, tp=None):
    with kb.scope():
        bufs = []
        for _i in range(2):
            bufs.append([kb.sbuf(prefix + "_xt", [128, 1024], F32),
                         kb.sbuf(prefix + "_xn", [128, 1024], out_dtype),
                         kb.sbuf(prefix + "_mneg", [128, 1], F32),
                         kb.sbuf(prefix + "_ss", [128, 1], F32),
                         kb.sbuf(prefix + "_sq", [128, 1024], F32),
                         kb.sbuf(prefix + "_var", [128, 1], F32),
                         kb.sbuf(prefix + "_rstd", [128, 1], F32)])
        if tp is None:
            tps = [kb.psum(prefix + "_tp", [128, 512], BF16) for _ in range(2)]
        elif isinstance(tp, list):
            tps = tp
        else:
            tps = [tp, tp]
        for j, ti in enumerate(tiles):
            xt, xn, mneg, ssum, sq, var, rstd = bufs[j % 2]
            tp = tps[j % 2]
            kb.dma(xt[:], x_src_ap[ti * 128:(ti + 1) * 128, :],
                   stream=stream + str(j % 2))
            kb.red(mneg[:], xt[:], ALU.add, negate=True)
            kb.ts(mneg[:], mneg[:], 1.0 / 1024, None, ALU.mult)
            kb.act(sq[:], xt[:], AF.Square, bias=mneg[:], accum_out=ssum[:])
            kb.ts(var[:], ssum[:], 1.0 / 1024, eps, ALU.mult, ALU.add)
            kb.act(rstd[:], var[:], AF.Ln)
            kb.act(rstd[:], rstd[:], AF.Exp, scale=-0.5)
            if g_bcast is None:
                kb.ts(xn[:], xt[:], mneg[:], rstd[:], ALU.add, ALU.mult)
            else:
                kb.ts(sq[:], xt[:], mneg[:], rstd[:], ALU.add, ALU.mult)
                kb.tmul(sq[:], sq[:], g_bcast[:])
                kb.tadd(xn[:], sq[:], b_bcast[:])
            if xtm_out is not None:
                kb.copy(xtm_out[:, j * 1024:(j + 1) * 1024], xn[:], chan="pool")
            for c4 in range(2):
                for j2 in range(4):
                    c = c4 * 4 + j2
                    kb.trans(tp[:, j2 * 128:(j2 + 1) * 128],
                             xn[:, c * 128:(c + 1) * 128], ident)
                for j2 in range(4):
                    c = c4 * 4 + j2
                    kb.copy(hT[:, c * Tcols + j * 128: c * Tcols + (j + 1) * 128],
                            tp[:, j2 * 128:(j2 + 1) * 128])



def build_l0_attn_a(kb, x_src, xres, consts, W):
    nc = kb.nc
    ident = consts["ident"]
    ones1 = consts["ones1"]
    with kb.scope():
        bq = load_vec_chunks(kb, "aa_bq", W["L0_attn_a_bq"], 1024)
        bk = load_vec_chunks(kb, "aa_bk", W["L0_attn_a_bk"], 1024)
        bv_row = kb.sbuf("aa_bvrow", [1, 1024], BF16)
        bv_stg = kb.sbuf("aa_bvstg", [1, 1024], F32)
        kb.dma(bv_stg[:], W["L0_attn_a_bv"].ap().unsqueeze(0), stream="ld")
        kb.copy(bv_row[:], bv_stg[:])
        bvB = bcast_row(kb, "aa_bvB", bv_row, 1024, ones1)

        wstage = (kb.sbuf("aa_wstg", [128, 1024], F32),
                  kb.sbuf("aa_wstgb", [128, 1024], BF16),
                  kb.psum("aa_wstgp", [128, 512], BF16))
        wT = kb.sbuf("aa_wT", [128, 8 * 1024], BF16)

        hT = kb.sbuf("aa_hT", [128, 8 * L], BF16)
        qT = kb.sbuf("aa_qT", [128, 8 * L], BF16)
        kT = kb.sbuf("aa_kT", [128, 8 * L], BF16)
        vtm = kb.sbuf("aa_vtm", [128, (L // 128) * 1024], BF16)
        pp = kb.psum("aa_pp", [128, 512], F32)
        pp2 = kb.psum("aa_pp2", [128, 512], F32)
        S = kb.sbuf("aa_S", [128, L], F32)
        P = kb.sbuf("aa_P", [128, L], BF16)
        PT = kb.sbuf("aa_PT", [128, L], BF16)
        ptp = kb.psum("aa_ptp", [128, 512], BF16)
        mrow = kb.sbuf("aa_m", [128, 1], F32)
        srow = kb.sbuf("aa_s", [128, 1], F32)
        rrow = kb.sbuf("aa_r", [128, 1], F32)
        pav = kb.psum("aa_pav", [128, 512], F32)
        pav2 = kb.psum("aa_pav2", [128, 512], F32)
        otm = kb.sbuf("aa_otm", [128, 1024], F32)
        xt2 = kb.sbuf("aa_xt2", [128, 1024], F32)

        LT = L // 128
        for b in range(B_LOC):
            tiles = list(range(b * LT, (b + 1) * LT))
            ln_tiles_to_fm(kb, x_src, hT, tiles, L, ident, prefix=f"aa_ln{b}",
                           tp=ptp)
            for wname, bias, dstT in (("L0_attn_a_wq", bq, qT), ("L0_attn_a_wk", bk, kT)):
                load_w_T(kb, "aa_w", W[wname], 1024, 1024, ident, wT=wT, stage=wstage)
                for co in range(8):
                    for ti2, t0 in enumerate(range(0, L, 512)):
                        tw = min(512, L - t0)
                        ps = pp if ti2 % 2 == 0 else pp2
                        for ci in range(8):
                            kb.mm(ps[:, 0:tw], wT[:, ci * 1024 + co * 128: ci * 1024 + (co + 1) * 128],
                                  hT[:, ci * L + t0: ci * L + t0 + tw],
                                  start=(ci == 0), stop=(ci == 7))
                        kb.ts(dstT[:, co * L + t0: co * L + t0 + tw],
                              ps[:, 0:tw], bias[:, co:co + 1], None, ALU.add)
            load_w_T(kb, "aa_w", W["L0_attn_a_wv"], 1024, 1024, ident, wT=wT, stage=wstage)
            for tt_ in range(LT):
                for oh in range(2):
                    ps = pp if oh == 0 else pp2
                    for ci in range(8):
                        kb.mm(ps[:], hT[:, ci * L + tt_ * 128: ci * L + (tt_ + 1) * 128],
                              wT[:, ci * 1024 + oh * 512: ci * 1024 + (oh + 1) * 512],
                              start=(ci == 0), stop=(ci == 7))
                    kb.copy(vtm[:, tt_ * 1024 + oh * 512: tt_ * 1024 + (oh + 1) * 512], ps[:])
            for qt in range(LT):
                for ki2, k0 in enumerate(range(0, L, 512)):
                    kw = min(512, L - k0)
                    ps = pp if ki2 % 2 == 0 else pp2
                    for ci in range(8):
                        kb.mm(ps[:, 0:kw], qT[:, ci * L + qt * 128: ci * L + (qt + 1) * 128],
                              kT[:, ci * L + k0: ci * L + k0 + kw],
                              start=(ci == 0), stop=(ci == 7))
                    kb.ts(S[:, k0:k0 + kw], ps[:, 0:kw], 1.0 / 32.0, None, ALU.mult)
                kb.red(mrow[:], S[:], ALU.max, negate=True)
                kb.act(P[:], S[:], AF.Exp, bias=mrow[:], accum_out=srow[:])
                kb.recip(rrow[:], srow[:])
                for kt4 in range(0, LT, 4):
                    kts = range(kt4, min(kt4 + 4, LT))
                    for j, kt in enumerate(kts):
                        kb.trans(ptp[:, j * 128:(j + 1) * 128],
                                 P[:, kt * 128:(kt + 1) * 128], ident)
                    for j, kt in enumerate(kts):
                        kb.copy(PT[:, kt * 128:(kt + 1) * 128],
                                ptp[:, j * 128:(j + 1) * 128])
                for oh in range(2):
                    ps = pav if oh == 0 else pav2
                    for kt in range(LT):
                        kb.mm(ps[:], PT[:, kt * 128:(kt + 1) * 128],
                              vtm[:, kt * 1024 + oh * 512: kt * 1024 + (oh + 1) * 512],
                              start=(kt == 0), stop=(kt == LT - 1))
                    kb.stt(otm[:, oh * 512:(oh + 1) * 512], ps[:], rrow[:],
                           bvB[:, oh * 512:(oh + 1) * 512], ALU.mult, ALU.add)
                ti = b * LT + qt
                kb.dma(xt2[:], x_src[ti * 128:(ti + 1) * 128, :], stream="xld2")
                kb.tadd(otm[:], otm[:], xt2[:], chan="pool")
                kb.dma(xres[ti * 128:(ti + 1) * 128, :], otm[:], stream="xst",
                       queue="gpsimd")
        kb.barrier()


def build_l0_ffn(kb, x_src, xres, consts, W, tag):
    nc = kb.nc
    ident = consts["ident"]
    T = B_LOC * L
    NT = T // 128
    TC = T // 512
    hT_d = nc.dram_tensor(f"moe_hT{tag}", [8, 128, T], BF16, kind="Internal")
    acc_d = nc.dram_tensor(f"moe_acc{tag}", [T, 1024], F32, kind="Internal")
    with kb.scope():
        with kb.scope():
            hT = kb.sbuf("moe_hT", [128, 8 * T], BF16)
            tp = kb.psum("moe_tp", [128, 512], BF16)
            ln_tiles_to_fm(kb, x_src, hT, list(range(NT)), T, ident,
                           prefix="moe_ln", tp=tp)
            for c in range(8):
                kb.dma(hT_d.ap()[c], hT[:, c * T:(c + 1) * T], stream="hst",
                       queue="gpsimd")
        kb.barrier()
        comb = kb.sbuf("moe_comb", [128, NT * 8], F32)
        with kb.scope():
            gstg = kb.sbuf("moe_gstg", [128, 8], F32)
            gate = kb.sbuf("moe_gate", [128, 8 * 8], BF16)
            for ci in range(8):
                kb.dma(gstg[:], W["L0_ffn_gate_w"].ap()[ci * 128:(ci + 1) * 128, :],
                       stream="wld")
                kb.copy(gate[:, ci * 8:(ci + 1) * 8], gstg[:], chan="pool")
            htc = kb.sbuf("moe_htcB", [128, 8 * 512], BF16)
            pg = kb.psum("moe_pg", [128, 8], F32)
            lg = kb.sbuf("moe_lg", [128, 8], F32)
            top = kb.sbuf("moe_top", [128, 8], F32)
            p1 = kb.sbuf("moe_p1", [128, 1], F32)
            p2 = kb.sbuf("moe_p2", [128, 1], F32)
            m1 = kb.sbuf("moe_m1", [128, 8], F32)
            m2 = kb.sbuf("moe_m2", [128, 8], F32)
            for tc in range(TC):
                for c in range(8):
                    kb.dma(htc[:, c * 512:(c + 1) * 512],
                           hT_d.ap()[c][:, tc * 512:(tc + 1) * 512], stream="hld")
                for tt in range(4):
                    ti = tc * 4 + tt
                    for ci in range(8):
                        kb.mm(pg[:], htc[:, ci * 512 + tt * 128: ci * 512 + (tt + 1) * 128],
                              gate[:, ci * 8:(ci + 1) * 8], start=(ci == 0), stop=(ci == 7))
                    kb.copy(lg[:], pg[:])
                    kb.max8(top[:], lg[:])
                    kb.tsub(p1[:], top[:, 0:1], top[:, 1:2])
                    kb.act(p1[:], p1[:], AF.Sigmoid)
                    kb.ts(p2[:], p1[:], -1.0, 1.0, ALU.mult, ALU.add)
                    kb.ts(m1[:], lg[:], top[:, 0:1], None, ALU.is_equal)
                    kb.ts(m2[:], lg[:], top[:, 1:2], None, ALU.is_equal)
                    kb.ts(m1[:], m1[:], p1[:], None, ALU.mult)
                    kb.stt(comb[:, ti * 8:(ti + 1) * 8], m2[:], p2[:], m1[:],
                           ALU.mult, ALU.add)
        kb.barrier()
        with kb.scope():
          experts = [("S", W["L0_ffn_Sg"], W["L0_ffn_Su"], W["L0_ffn_Sd"], 1408, None)]
          for e in range(8):
              experts.append((f"E{e}", W["L0_ffn_Wg"], W["L0_ffn_Wu"],
                              W["L0_ffn_Wd"], 2816, e))
          wg_sb = kb.sbuf("moe_wg", [128, 8 * 2816], BF16)
          wu_sb = kb.sbuf("moe_wu", [128, 8 * 2816], BF16)
          wd_sb = kb.sbuf("moe_wd", [128, 22 * 1024], BF16)
          wstgs = [kb.sbuf("moe_wstg", [128, 2816], F32) for _ in range(2)]
          htcs = [kb.sbuf(f"moe_htc{i}", [128, 8 * 512], BF16) for i in range(2)]
          sg = kb.sbuf("moe_sg", [128, 22 * 512], BF16)
          sgg = kb.sbuf("moe_sgg", [128, 512], BF16)
          pgas = [kb.psum("moe_pga", [128, 512], F32) for _ in range(2)]
          pgbs = [kb.psum("moe_pgb", [128, 512], F32) for _ in range(2)]
          pds = [kb.psum("moe_pd", [128, 512], F32) for _ in range(2)]
          sggs = [kb.sbuf("moe_sgg2", [128, 512], BF16) for _ in range(2)]
          atile = kb.sbuf("moe_at", [128, 1024], F32)
          otile = kb.sbuf("moe_ot", [128, 1024], F32)
          for name, Wg_h, Wu_h, Wd_h, I, e in experts:
              IC = I // 128
              for wi, (w_h, dst, O) in enumerate(((Wg_h, wg_sb, I), (Wu_h, wu_sb, I))):
                  ap = w_h.ap() if e is None else w_h.ap()[e]
                  for ci in range(8):
                      wstg = wstgs[ci % 2]
                      kb.dma(wstg[:, 0:O], ap[ci * 128:(ci + 1) * 128, :],
                             stream=f"wld{(wi * 8 + ci) % 4}")
                      kb.copy(dst[:, ci * I:ci * I + O], wstg[:, 0:O], chan="pool")
              ap = Wd_h.ap() if e is None else Wd_h.ap()[e]
              for oc in range(IC):
                  wstg = wstgs[oc % 2]
                  kb.dma(wstg[:, 0:1024], ap[oc * 128:(oc + 1) * 128, :],
                         stream=f"wld{oc % 4}")
                  kb.copy(wd_sb[:, oc * 1024:(oc + 1) * 1024], wstg[:, 0:1024], chan="pool")
              for tc in range(TC):
                  htc = htcs[tc % 2]
                  for c in range(8):
                      kb.dma(htc[:, c * 512:(c + 1) * 512],
                             hT_d.ap()[c][:, tc * 512:(tc + 1) * 512], stream="hld")
                  for oc in range(IC):
                      pga, pgb, sgg2 = pgas[oc % 2], pgbs[oc % 2], sggs[oc % 2]
                      for ci in range(8):
                          kb.mm(pga[:], wg_sb[:, ci * I + oc * 128: ci * I + (oc + 1) * 128],
                                htc[:, ci * 512:(ci + 1) * 512], start=(ci == 0), stop=(ci == 7))
                      for ci in range(8):
                          kb.mm(pgb[:], wu_sb[:, ci * I + oc * 128: ci * I + (oc + 1) * 128],
                                htc[:, ci * 512:(ci + 1) * 512], start=(ci == 0), stop=(ci == 7))
                      kb.act(sgg2[:], pga[:], AF.Silu)
                      kb.tmul(sg[:, oc * 512:(oc + 1) * 512], sgg2[:], pgb[:])
                  for tt in range(4):
                      ti = tc * 4 + tt
                      if e is not None:
                          kb.dma(atile[:], acc_d.ap()[ti * 128:(ti + 1) * 128, :],
                                 stream="ald")
                      for half in range(2):
                          pd = pds[(tt * 2 + half) % 2]
                          for oc in range(IC):
                              kb.mm(pd[:], sg[:, oc * 512 + tt * 128: oc * 512 + (tt + 1) * 128],
                                    wd_sb[:, oc * 1024 + half * 512: oc * 1024 + (half + 1) * 512],
                                    start=(oc == 0), stop=(oc == IC - 1))
                          if e is None:
                              kb.copy(otile[:, half * 512:(half + 1) * 512], pd[:])
                          else:
                              kb.stt(otile[:, half * 512:(half + 1) * 512], pd[:],
                                     comb[:, ti * 8 + e: ti * 8 + e + 1],
                                     atile[:, half * 512:(half + 1) * 512],
                                     ALU.mult, ALU.add)
                      kb.dma(acc_d.ap()[ti * 128:(ti + 1) * 128, :], otile[:],
                             stream="ast", queue="gpsimd")
        with kb.scope():
            xt = kb.sbuf("moe_xt", [128, 1024], F32)
            at2 = kb.sbuf("moe_at2", [128, 1024], F32)
            for ti in range(NT):
                kb.dma(xt[:], x_src[ti * 128:(ti + 1) * 128, :], stream="xld2")
                kb.dma(at2[:], acc_d.ap()[ti * 128:(ti + 1) * 128, :], stream="ald")
                kb.tadd(at2[:], at2[:], xt[:], chan="pool")
                kb.dma(xres[ti * 128:(ti + 1) * 128, :], at2[:], stream="xst",
                       queue="gpsimd")
        kb.barrier()


def build_l1_attn_b(kb, x_src, xres, consts, W):
    nc = kb.nc
    ident = consts["ident"]
    ones1 = consts["ones1"]
    tri = consts["tri"]
    LT = L // 128
    with kb.scope():
        bq = load_vec_chunks(kb, "mb_bq", W["L1_attn_b_b_q"], 1024)
        bk = load_vec_chunks(kb, "mb_bk", W["L1_attn_b_b_k"], 1024)
        brow = kb.sbuf("mb_brow", [1, 1024], BF16)
        bstg = kb.sbuf("mb_bstg", [1, 1024], F32)
        bp = kb.psum("mb_bp", [128, 512], F32)
        kb.dma(bstg[:], W["L1_attn_b_b_v"].ap().unsqueeze(0), stream="ld")
        kb.copy(brow[:], bstg[:])
        bvB = bcast_row(kb, "mb_bvB", brow, 1024, ones1, psum=bp)
        kb.dma(bstg[:], W["L1_attn_b_b_o"].ap().unsqueeze(0), stream="ld")
        kb.copy(brow[:], bstg[:])
        boB = bcast_row(kb, "mb_boB", brow, 1024, ones1, psum=bp)
        wsb = kb.sbuf("mb_w", [128, 8 * 1024], BF16)
        wstg = kb.sbuf("mb_wstg", [128, 1024], F32)
        QT = kb.sbuf("mb_QT", [128, 8 * L], BF16)
        KT = kb.sbuf("mb_KT", [128, 8 * L], BF16)
        Vtm = kb.sbuf("mb_V", [128, LT * 1024], BF16)
        pp = kb.psum("mb_pp", [128, 512], F32)
        pp2 = kb.psum("mb_pp2", [128, 512], F32)
        ptp = kb.psum("mb_ptp", [128, 512], BF16)
        pav = kb.psum("mb_pav", [64, 128], F32)
        for b in range(B_LOC):
            tiles = list(range(b * LT, (b + 1) * LT))
            with kb.scope():
                hT = kb.sbuf("mb_hT", [128, 8 * L], BF16)
                ln_tiles_to_fm(kb, x_src, hT, tiles, L, ident,
                               prefix=f"mb_ln{b}", tp=ptp)
                for wname, bias, dstT in (("L1_attn_b_W_q", bq, QT),
                                          ("L1_attn_b_W_k", bk, KT)):
                    load_w_direct(kb, "mb_wq", W[wname], 1024, 1024, wsb=wsb,
                                  stage=wstg)
                    for co in range(8):
                        for ti2, t0 in enumerate(range(0, L, 512)):
                            tw = min(512, L - t0)
                            ps = pp if ti2 % 2 == 0 else pp2
                            for ci in range(8):
                                kb.mm(ps[:, 0:tw],
                                      wsb[:, ci * 1024 + co * 128: ci * 1024 + (co + 1) * 128],
                                      hT[:, ci * L + t0: ci * L + t0 + tw],
                                      start=(ci == 0), stop=(ci == 7))
                            kb.ts(dstT[:, co * L + t0: co * L + t0 + tw],
                                  ps[:, 0:tw], bias[:, co:co + 1], None, ALU.add)
                load_w_direct(kb, "mb_wq", W["L1_attn_b_W_v"], 1024, 1024,
                              wsb=wsb, stage=wstg)
                for tt_ in range(LT):
                    for oh in range(2):
                        ps = pp if oh == 0 else pp2
                        for ci in range(8):
                            kb.mm(ps[:], hT[:, ci * L + tt_ * 128: ci * L + (tt_ + 1) * 128],
                                  wsb[:, ci * 1024 + oh * 512: ci * 1024 + (oh + 1) * 512],
                                  start=(ci == 0), stop=(ci == 7))
                        kb.stt(Vtm[:, tt_ * 1024 + oh * 512: tt_ * 1024 + (oh + 1) * 512],
                               ps[:], 1.0, bvB[:, oh * 512:(oh + 1) * 512],
                               ALU.mult, ALU.add)
            kb.barrier()
            ext = kb.scope()
            ext.__enter__()
            aT = kb.sbuf("mb_aT", [128, 8 * L], BF16)
            S = kb.sbuf("mb_S", [128, L], F32)
            Pf = kb.sbuf("mb_Pf", [128, L], F32)
            Pb = kb.sbuf("mb_Pb", [128, L], BF16)
            PT = kb.sbuf("mb_PT", [128, L], BF16)
            negm = kb.sbuf("mb_negm", [128, 1], F32)
            ssum = kb.sbuf("mb_ss", [128, 1], F32)
            rrow = kb.sbuf("mb_rr", [128, 1], F32)
            otm = kb.sbuf("mb_otm", [128, 1024], F32)
            xt2 = kb.sbuf("mb_xt2", [128, 1024], F32)
            for qt in range(LT):
                kspan = (qt + 1) * 128
                for h in range(16):
                    ci = h // 2
                    pb = (h % 2) * 64
                    for ki2, k0 in enumerate(range(0, kspan, 512)):
                        kw = min(512, kspan - k0)
                        ps = pp if ki2 % 2 == 0 else pp2
                        kb.mm(ps[:, 0:kw],
                              QT[:, ci * L + qt * 128: ci * L + (qt + 1) * 128][pb:pb + 64, :],
                              KT[:, ci * L + k0: ci * L + k0 + kw][pb:pb + 64, :],
                              start=True, stop=True)
                        kb.copy(S[:, k0:k0 + kw], ps[:, 0:kw])
                    kb.tadd(S[:, qt * 128:kspan], S[:, qt * 128:kspan], tri)
                    kb.red(negm[:], S[:, 0:kspan], ALU.max, negate=True)
                    kb.ts(negm[:], negm[:], 0.125, None, ALU.mult)
                    kb.act(Pf[:, 0:kspan], S[:, 0:kspan], AF.Exp, bias=negm[:],
                           scale=0.125, accum_out=ssum[:])
                    kb.recip(rrow[:], ssum[:])
                    kb.ts(Pb[:, 0:kspan], Pf[:, 0:kspan], rrow[:], None, ALU.mult)
                    for kt4 in range(0, qt + 1, 4):
                        kts = range(kt4, min(kt4 + 4, qt + 1))
                        for j, kt in enumerate(kts):
                            kb.trans(ptp[:, j * 128:(j + 1) * 128],
                                     Pb[:, kt * 128:(kt + 1) * 128], ident)
                        for j, kt in enumerate(kts):
                            kb.copy(PT[:, kt * 128:(kt + 1) * 128],
                                    ptp[:, j * 128:(j + 1) * 128])
                    for kt in range(qt + 1):
                        kb.mm(pav[:], Vtm[:, kt * 1024 + h * 64: kt * 1024 + (h + 1) * 64],
                              PT[:, kt * 128:(kt + 1) * 128],
                              start=(kt == 0), stop=(kt == qt))
                    kb.copy(aT[:, ci * L + qt * 128: ci * L + (qt + 1) * 128][pb:pb + 64, :],
                            pav[:])
            load_w_direct(kb, "mb_wq", W["L1_attn_b_W_o"], 1024, 1024,
                          wsb=wsb, stage=wstg)
            for qt in range(LT):
                for oh in range(2):
                    ps = pp if oh == 0 else pp2
                    for ci in range(8):
                        kb.mm(ps[:], aT[:, ci * L + qt * 128: ci * L + (qt + 1) * 128],
                              wsb[:, ci * 1024 + oh * 512: ci * 1024 + (oh + 1) * 512],
                              start=(ci == 0), stop=(ci == 7))
                    kb.stt(otm[:, oh * 512:(oh + 1) * 512], ps[:], 1.0,
                           boB[:, oh * 512:(oh + 1) * 512], ALU.mult, ALU.add)
                ti = b * LT + qt
                kb.dma(xt2[:], x_src[ti * 128:(ti + 1) * 128, :], stream="xld2")
                kb.tadd(otm[:], otm[:], xt2[:], chan="pool")
                kb.dma(xres[ti * 128:(ti + 1) * 128, :], otm[:], stream="xst",
                       queue="gpsimd")
            ext.__exit__(None, None, None)
        kb.barrier()


def _row_bcast_from_hbm(kb, name, v_hbm, n, ones1, psum, src_ap=None, repeat=1,
                        dtype=F32):
    stg = kb.sbuf(name + "_rs", [1, n], F32)
    rowb = kb.sbuf(name + "_rb", [1, n], BF16)
    ap = src_ap if src_ap is not None else v_hbm.ap()
    m = n // repeat
    for r in range(repeat):
        kb.dma(stg[:, r * m:(r + 1) * m], ap.unsqueeze(0), stream="ld")
    kb.copy(rowb[:], stg[:])
    return bcast_row(kb, name, rowb, n, ones1, psum=psum, dtype=dtype)


def _fm_mix(kb, dT, hT, mixc, outT, c, first_zero=True):
    kb.tsub(dT[:, 1:L], hT[:, c * L + 1: c * L + L], hT[:, c * L: c * L + L - 1])
    kb.stt(outT[:, c * L + 1: c * L + L], dT[:, 1:L], mixc,
           hT[:, c * L: c * L + L - 1], ALU.mult, ALU.add)
    kb.ts(outT[:, c * L: c * L + 1], hT[:, c * L: c * L + 1], mixc, None, ALU.mult)


def ln_plain_to_dram(kb, x_src, y_ap, NT, prefix):
    with kb.scope():
        xt = kb.sbuf(prefix + "_xt", [128, 1024], F32)
        yt = kb.sbuf(prefix + "_yt", [128, 1024], F32)
        mneg = kb.sbuf(prefix + "_mneg", [128, 1], F32)
        ssum = kb.sbuf(prefix + "_ss", [128, 1], F32)
        sq = kb.sbuf(prefix + "_sq", [128, 1024], F32)
        var = kb.sbuf(prefix + "_var", [128, 1], F32)
        rstd = kb.sbuf(prefix + "_rstd", [128, 1], F32)
        for ti in range(NT):
            kb.dma(xt[:], x_src[ti * 128:(ti + 1) * 128, :], stream="xld")
            kb.red(mneg[:], xt[:], ALU.add, negate=True)
            kb.ts(mneg[:], mneg[:], 1.0 / 1024, None, ALU.mult)
            kb.act(sq[:], xt[:], AF.Square, bias=mneg[:], accum_out=ssum[:])
            kb.ts(var[:], ssum[:], 1.0 / 1024, 1e-5, ALU.mult, ALU.add)
            kb.act(rstd[:], var[:], AF.Ln)
            kb.act(rstd[:], rstd[:], AF.Exp, scale=-0.5)
            kb.ts(yt[:], xt[:], mneg[:], rstd[:], ALU.add, ALU.mult)
            kb.dma(y_ap[ti * 128:(ti + 1) * 128, :], yt[:], stream="yst",
                   queue="gpsimd")


def final_add_xy(kb, x_src, y_ap, xres, NT, prefix):
    with kb.scope():
        xt = kb.sbuf(prefix + "_fx", [128, 1024], F32)
        yt = kb.sbuf(prefix + "_fy", [128, 1024], F32)
        for ti in range(NT):
            kb.dma(xt[:], x_src[ti * 128:(ti + 1) * 128, :], stream="xld")
            kb.dma(yt[:], y_ap[ti * 128:(ti + 1) * 128, :], stream="yld")
            kb.tadd(yt[:], yt[:], xt[:])
            kb.dma(xres[ti * 128:(ti + 1) * 128, :], yt[:], stream="xst",
                   queue="gpsimd")


def build_l0_attn_b(kb, x_orig, xres, consts, W):
    nc = kb.nc
    ident = consts["ident"]
    ones1 = consts["ones1"]
    LT = L // 128
    NT = B_LOC * LT
    y_d = nc.dram_tensor("rw_y", [B_LOC * L, D], F32, kind="Internal")
    ln_plain_to_dram(kb, x_orig, y_d.ap(), NT, "rwp")
    x_src = y_d.ap()
    with kb.scope():
        bp = kb.psum("rw_bp", [128, 512], F32)
        gA = _row_bcast_from_hbm(kb, "rw_gA", W["L0_attn_b_ln_att_g"], 1024, ones1, bp)
        bA = _row_bcast_from_hbm(kb, "rw_bA", W["L0_attn_b_ln_att_b"], 1024, ones1, bp)
        gF = _row_bcast_from_hbm(kb, "rw_gF", W["L0_attn_b_ln_ffn_g"], 1024, ones1, bp)
        bF = _row_bcast_from_hbm(kb, "rw_bF", W["L0_attn_b_ln_ffn_b"], 1024, ones1, bp)
        mixk = load_vec_chunks(kb, "rw_mk", W["L0_attn_b_att_mix_k"], 1024)
        mixv = load_vec_chunks(kb, "rw_mv", W["L0_attn_b_att_mix_v"], 1024)
        mixr = load_vec_chunks(kb, "rw_mr", W["L0_attn_b_att_mix_r"], 1024)
        fmixk = load_vec_chunks(kb, "rw_fmk", W["L0_attn_b_ffn_mix_k"], 1024)
        fmixr = load_vec_chunks(kb, "rw_fmr", W["L0_attn_b_ffn_mix_r"], 1024)
        td = load_vec_chunks(kb, "rw_td", W["L0_attn_b_att_time_decay"], 1024)
        tf = load_vec_chunks(kb, "rw_tf", W["L0_attn_b_att_time_first"], 1024)
        ew = kb.sbuf("rw_ew", [128, 8], F32)
        eu = kb.sbuf("rw_eu", [128, 8], F32)
        kb.act(ew[:], td[:], AF.Exp)
        kb.act(ew[:], ew[:], AF.Exp, scale=-1.0)
        kb.act(eu[:], tf[:], AF.Exp)
        wstg = kb.sbuf("rw_wstg", [128, 1024], F32)
        ptp = kb.psum("rw_ptp", [128, 512], BF16)
        pp = kb.psum("rw_pp", [128, 512], F32)
        pp2 = kb.psum("rw_pp2", [128, 512], F32)
        kx_d = nc.dram_tensor("rw_kx_d", [8, 128, L], BF16, kind="Internal")
        vx_d = nc.dram_tensor("rw_vx_d", [8, 128, L], BF16, kind="Internal")
        rx_d = nc.dram_tensor("rw_rx_d", [8, 128, L], BF16, kind="Internal")
        for b in range(B_LOC):
          with kb.scope():
            tiles = list(range(b * LT, (b + 1) * LT))
            with kb.scope():
                kxT = kb.sbuf("rw_kxT", [128, 8 * L], BF16)
                vxT = kb.sbuf("rw_vxT", [128, 8 * L], BF16)
                rxT = kb.sbuf("rw_rxT", [128, 8 * L], BF16)
                with kb.scope():
                    hT = kb.sbuf("rw_hT", [128, 8 * L], BF16)
                    dT = kb.sbuf("rw_dT", [128, L], BF16)
                    ln_tiles_to_fm(kb, x_src, hT, tiles, L, ident,
                                   g_bcast=gA, b_bcast=bA, prefix=f"rw_lnA{b}",
                                   tp=ptp)
                    for c in range(8):
                        _fm_mix(kb, dT, hT, mixk[:, c:c + 1], kxT, c)
                        _fm_mix(kb, dT, hT, mixv[:, c:c + 1], vxT, c)
                        _fm_mix(kb, dT, hT, mixr[:, c:c + 1], rxT, c)
                for c in range(8):
                    kb.dma(kx_d.ap()[c], kxT[:, c * L:(c + 1) * L], stream="rwst",
                           queue="gpsimd")
                    kb.dma(vx_d.ap()[c], vxT[:, c * L:(c + 1) * L], stream="rwst",
                           queue="gpsimd")
                    kb.dma(rx_d.ap()[c], rxT[:, c * L:(c + 1) * L], stream="rwst",
                           queue="gpsimd")
            with kb.scope():
              ekT = kb.sbuf("rw_ekT", [128, 8 * L], BF16)
              ekvT = kb.sbuf("rw_ekvT", [128, 8 * L], BF16)
              with kb.scope():
                  wsb = kb.sbuf("rw_wk", [128, 8 * 1024], BF16)
                  xin = kb.sbuf("rw_xin", [128, 8 * L], BF16)
                  for c in range(8):
                      kb.dma(xin[:, c * L:(c + 1) * L], kx_d.ap()[c], stream="rwld")
                  load_w_direct(kb, "rw_w", W["L0_attn_b_att_wk"], 1024, 1024,
                                wsb=wsb, stage=wstg)
                  for co in range(8):
                      for ti2, t0 in enumerate(range(0, L, 512)):
                          tw = min(512, L - t0)
                          ps = pp if ti2 % 2 == 0 else pp2
                          for ci in range(8):
                              kb.mm(ps[:, 0:tw],
                                    wsb[:, ci * 1024 + co * 128: ci * 1024 + (co + 1) * 128],
                                    xin[:, ci * L + t0: ci * L + t0 + tw],
                                    start=(ci == 0), stop=(ci == 7))
                          kb.act(ekT[:, co * L + t0: co * L + t0 + tw],
                                 ps[:, 0:tw], AF.Exp)
              with kb.scope():
                  wsb = kb.sbuf("rw_wv", [128, 8 * 1024], BF16)
                  xin = kb.sbuf("rw_xin2", [128, 8 * L], BF16)
                  vtmp = kb.sbuf("rw_vtmp", [128, L], BF16)
                  for c in range(8):
                      kb.dma(xin[:, c * L:(c + 1) * L], vx_d.ap()[c], stream="rwld")
                  load_w_direct(kb, "rw_w", W["L0_attn_b_att_wv"], 1024, 1024,
                                wsb=wsb, stage=wstg)
                  for co in range(8):
                      for ti2, t0 in enumerate(range(0, L, 512)):
                          tw = min(512, L - t0)
                          ps = pp if ti2 % 2 == 0 else pp2
                          for ci in range(8):
                              kb.mm(ps[:, 0:tw],
                                    wsb[:, ci * 1024 + co * 128: ci * 1024 + (co + 1) * 128],
                                    xin[:, ci * L + t0: ci * L + t0 + tw],
                                    start=(ci == 0), stop=(ci == 7))
                          kb.copy(vtmp[:, 0:tw], ps[:, 0:tw])
                          kb.tmul(ekvT[:, co * L + t0: co * L + t0 + tw],
                                  vtmp[:, 0:tw], ekT[:, co * L + t0: co * L + t0 + tw])
              rw = kb.sbuf("rw_rw", [128, 8 * L], BF16)
              with kb.scope():
                  ewb = kb.sbuf("rw_ewb", [128, L], F32)
                  pscan = kb.sbuf("rw_ps", [128, L], F32)
                  qscan = kb.sbuf("rw_qs", [128, L], F32)
                  num = kb.sbuf("rw_num", [128, L], F32)
                  den = kb.sbuf("rw_den", [128, L], F32)
                  tmp = kb.sbuf("rw_tmp", [128, L], F32)
                  for c in range(8):
                      kb.memset(ewb[:], 1.0)
                      kb.ts(ewb[:], ewb[:], ew[:, c:c + 1], None, ALU.mult)
                      kb.scan(pscan[:], ewb[:], ekvT[:, c * L:(c + 1) * L], 0.0)
                      kb.scan(qscan[:], ewb[:], ekT[:, c * L:(c + 1) * L], 0.0,
                              chan="pool")
                      kb.ts(tmp[:], ekvT[:, c * L:(c + 1) * L], eu[:, c:c + 1],
                            None, ALU.mult)
                      kb.copy(num[:, 0:1], tmp[:, 0:1])
                      kb.tadd(num[:, 1:L], pscan[:, 0:L - 1], tmp[:, 1:L])
                      kb.ts(tmp[:], ekT[:, c * L:(c + 1) * L], eu[:, c:c + 1],
                            None, ALU.mult, chan="pool")
                      kb.copy(den[:, 0:1], tmp[:, 0:1], chan="pool")
                      kb.tadd(den[:, 1:L], qscan[:, 0:L - 1], tmp[:, 1:L], chan="pool")
                      kb.recip(den[:], den[:])
                      kb.tmul(rw[:, c * L:(c + 1) * L], num[:], den[:])
              with kb.scope():
                  wsb = kb.sbuf("rw_wr", [128, 8 * 1024], BF16)
                  xin = kb.sbuf("rw_xin3", [128, 8 * L], BF16)
                  srf = kb.sbuf("rw_srf", [128, 512], BF16)
                  for c in range(8):
                      kb.dma(xin[:, c * L:(c + 1) * L], rx_d.ap()[c], stream="rwld")
                  load_w_direct(kb, "rw_w", W["L0_attn_b_att_wr"], 1024, 1024,
                                wsb=wsb, stage=wstg)
                  for co in range(8):
                      for ti2, t0 in enumerate(range(0, L, 512)):
                          tw = min(512, L - t0)
                          ps = pp if ti2 % 2 == 0 else pp2
                          for ci in range(8):
                              kb.mm(ps[:, 0:tw],
                                    wsb[:, ci * 1024 + co * 128: ci * 1024 + (co + 1) * 128],
                                    xin[:, ci * L + t0: ci * L + t0 + tw],
                                    start=(ci == 0), stop=(ci == 7))
                          kb.act(srf[:, 0:tw], ps[:, 0:tw], AF.Sigmoid)
                          kb.tmul(rw[:, co * L + t0: co * L + t0 + tw],
                                  rw[:, co * L + t0: co * L + t0 + tw], srf[:, 0:tw])
              with kb.scope():
                  wsb = kb.sbuf("rw_wo", [128, 8 * 1024], BF16)
                  otm = kb.sbuf("rw_otm", [128, 1024], F32)
                  xt2 = kb.sbuf("rw_xt2", [128, 1024], F32)
                  load_w_direct(kb, "rw_w", W["L0_attn_b_att_wo"], 1024, 1024,
                                wsb=wsb, stage=wstg)
                  for qt in range(LT):
                      for oh in range(2):
                          ps = pp if oh == 0 else pp2
                          for ci in range(8):
                              kb.mm(ps[:], rw[:, ci * L + qt * 128: ci * L + (qt + 1) * 128],
                                    wsb[:, ci * 1024 + oh * 512: ci * 1024 + (oh + 1) * 512],
                                    start=(ci == 0), stop=(ci == 7))
                          kb.copy(otm[:, oh * 512:(oh + 1) * 512], ps[:])
                      ti = b * LT + qt
                      kb.dma(xt2[:], x_src[ti * 128:(ti + 1) * 128, :], stream="xld2")
                      kb.tadd(otm[:], otm[:], xt2[:], chan="pool")
                      kb.dma(x_src[ti * 128:(ti + 1) * 128, :], otm[:], stream="xst",
                             queue="gpsimd")
              kb.barrier()

            with kb.scope():
                kxf = kb.sbuf("rw_kxf", [128, 8 * L], BF16)
                rxf = kb.sbuf("rw_rxf", [128, 8 * L], BF16)
                outT = kb.sbuf("rw_outT", [128, 8 * L], BF16)
                with kb.scope():
                    hT = kb.sbuf("rw_hT2", [128, 8 * L], BF16)
                    dT = kb.sbuf("rw_dT2", [128, L], BF16)
                    ln_tiles_to_fm(kb, x_src, hT, tiles, L, ident,
                                   g_bcast=gF, b_bcast=bF, prefix=f"rw_lnF{b}",
                                   tp=ptp)
                    for c in range(8):
                        _fm_mix(kb, dT, hT, fmixk[:, c:c + 1], kxf, c)
                        _fm_mix(kb, dT, hT, fmixr[:, c:c + 1], rxf, c)
                with kb.scope():
                    wkg = kb.sbuf("rw_wkg", [128, 8 * 1024], BF16)
                    wvg = kb.sbuf("rw_wvg", [128, 8 * 1024], BF16)
                    kfg = kb.sbuf("rw_kfg", [128, 8 * L], BF16)
                    rtmp = kb.sbuf("rw_rtmp", [128, 512], F32)
                    for g in range(4):
                        for ci in range(8):
                            kb.dma(wstg[:],
                                   W["L0_attn_b_ffn_wk"].ap()[ci * 128:(ci + 1) * 128,
                                                              g * 1024:(g + 1) * 1024],
                                   stream="wld")
                            kb.copy(wkg[:, ci * 1024:(ci + 1) * 1024], wstg[:],
                                    chan="pool")
                        load_w_direct(kb, "rw_w", W["L0_attn_b_ffn_wv"].ap()[g * 1024:(g + 1) * 1024, :],
                                      1024, 1024, wsb=wvg, stage=wstg)
                        for co in range(8):
                            for ti2, t0 in enumerate(range(0, L, 512)):
                                tw = min(512, L - t0)
                                ps = pp if ti2 % 2 == 0 else pp2
                                for ci in range(8):
                                    kb.mm(ps[:, 0:tw],
                                          wkg[:, ci * 1024 + co * 128: ci * 1024 + (co + 1) * 128],
                                          kxf[:, ci * L + t0: ci * L + t0 + tw],
                                          start=(ci == 0), stop=(ci == 7))
                                kb.ts(rtmp[:, 0:tw], ps[:, 0:tw], 0.0, None,
                                      ALU.max)
                                kb.act(kfg[:, co * L + t0: co * L + t0 + tw],
                                       rtmp[:, 0:tw], AF.Square)
                        for co in range(8):
                            for ti2, t0 in enumerate(range(0, L, 512)):
                                tw = min(512, L - t0)
                                ps = pp if ti2 % 2 == 0 else pp2
                                for ci in range(8):
                                    kb.mm(ps[:, 0:tw],
                                          wvg[:, ci * 1024 + co * 128: ci * 1024 + (co + 1) * 128],
                                          kfg[:, ci * L + t0: ci * L + t0 + tw],
                                          start=(ci == 0), stop=(ci == 7))
                                if g == 0:
                                    kb.copy(outT[:, co * L + t0: co * L + t0 + tw],
                                            ps[:, 0:tw])
                                else:
                                    kb.tadd(outT[:, co * L + t0: co * L + t0 + tw],
                                            outT[:, co * L + t0: co * L + t0 + tw],
                                            ps[:, 0:tw])
                with kb.scope():
                    wsb = kb.sbuf("rw_wr2", [128, 8 * 1024], BF16)
                    srf = kb.sbuf("rw_srf", [128, L], BF16)
                    bfprod = kb.sbuf("rw_bfp", [128, 8 * L], BF16)
                    otm = kb.sbuf("rw_otm2", [128, 1024], F32)
                    xt2 = kb.sbuf("rw_xt3", [128, 1024], F32)
                    load_w_direct(kb, "rw_w", W["L0_attn_b_ffn_wr"], 1024, 1024,
                                  wsb=wsb, stage=wstg)
                    for co in range(8):
                        for ti2, t0 in enumerate(range(0, L, 512)):
                            tw = min(512, L - t0)
                            ps = pp if ti2 % 2 == 0 else pp2
                            for ci in range(8):
                                kb.mm(ps[:, 0:tw],
                                      wsb[:, ci * 1024 + co * 128: ci * 1024 + (co + 1) * 128],
                                      rxf[:, ci * L + t0: ci * L + t0 + tw],
                                      start=(ci == 0), stop=(ci == 7))
                            kb.act(srf[:, 0:tw], ps[:, 0:tw], AF.Sigmoid)
                            kb.tmul(bfprod[:, co * L + t0: co * L + t0 + tw],
                                    srf[:, 0:tw], outT[:, co * L + t0: co * L + t0 + tw])
                    for qt in range(LT):
                        for c4 in range(2):
                            cs = range(c4 * 4, c4 * 4 + 4)
                            for j, c in enumerate(cs):
                                kb.trans(ptp[:, j * 128:(j + 1) * 128],
                                         bfprod[:, c * L + qt * 128: c * L + (qt + 1) * 128],
                                         ident)
                            for j, c in enumerate(cs):
                                kb.copy(otm[:, c * 128:(c + 1) * 128],
                                        ptp[:, j * 128:(j + 1) * 128])
                        ti = b * LT + qt
                        kb.dma(xt2[:], x_src[ti * 128:(ti + 1) * 128, :], stream="xld2")
                        kb.tadd(otm[:], otm[:], xt2[:], chan="pool")
                        kb.dma(x_src[ti * 128:(ti + 1) * 128, :], otm[:], stream="xst",
                               queue="gpsimd")
            kb.barrier()
        kb.barrier()
    final_add_xy(kb, x_orig, y_d.ap(), xres, NT, "rwf")
    kb.barrier()


def build_l1_attn_a(kb, x_orig, xres, consts, W):
    nc = kb.nc
    ident = consts["ident"]
    ones1 = consts["ones1"]
    LT = L // 128
    y_d = nc.dram_tensor("ma_y", [B_LOC * L, D], F32, kind="Internal")
    ln_plain_to_dram(kb, x_orig, y_d.ap(), B_LOC * LT, "map")
    x_src = y_d.ap()
    ED = 2048
    EC = ED // 128
    NS = 16
    z_d = nc.dram_tensor("ma_z", [EC, 128, L], BF16, kind="Internal")
    dl_d = nc.dram_tensor("ma_dl", [EC, 128, L], BF16, kind="Internal")
    dx_d = nc.dram_tensor("ma_dx", [EC, 128, L], BF16, kind="Internal")
    xc_d = nc.dram_tensor("ma_xc", [EC, 128, L], BF16, kind="Internal")
    pr_d = nc.dram_tensor("ma_pr", [EC, 128, L], BF16, kind="Internal")
    with kb.scope():
        bp = kb.psum("ma_bp", [128, 512], F32)
        gB = _row_bcast_from_hbm(kb, "ma_g", W["L1_attn_a_gamma"], 1024, ones1, bp)
        bB = _row_bcast_from_hbm(kb, "ma_b", W["L1_attn_a_beta"], 1024, ones1, bp)
        boB = _row_bcast_from_hbm(kb, "ma_bo", W["L1_attn_a_b_out"], 1024, ones1, bp)
        b_in = load_vec_chunks(kb, "ma_bin", W["L1_attn_a_b_in"], 4096)
        b_dt = load_vec_chunks(kb, "ma_bdt", W["L1_attn_a_b_dt"], 2048)
        Dp = load_vec_chunks(kb, "ma_D", W["L1_attn_a_D_param"], 2048)
        cw = kb.sbuf("ma_cw", [128, EC * 15], F32)
        for c in range(EC):
            kb.dma(cw[:, c * 15:(c + 1) * 15],
                   W["L1_attn_a_conv_w"].ap()[c * 128:(c + 1) * 128, :], stream="ld")
        cb = load_vec_chunks(kb, "ma_cb", W["L1_attn_a_conv_b"], 2048)
        A_sb = kb.sbuf("ma_A", [128, EC * NS], F32)
        astg = kb.sbuf("ma_astg", [128, NS], F32)
        for c in range(EC):
            kb.dma(astg[:], W["L1_attn_a_A_log"].ap()[c * 128:(c + 1) * 128, :],
                   stream="ld")
            kb.act(A_sb[:, c * NS:(c + 1) * NS], astg[:], AF.Exp)
            kb.ts(A_sb[:, c * NS:(c + 1) * NS], A_sb[:, c * NS:(c + 1) * NS],
                  -1.0, None, ALU.mult)
        wstg = kb.sbuf("ma_wstg", [128, 2048], F32)
        wstgb = kb.sbuf("ma_wstgb", [128, 2048], BF16)
        ptp = kb.psum("ma_ptp", [128, 512], BF16)
        pp = kb.psum("ma_pp", [128, 512], F32)
        pp2 = kb.psum("ma_pp2", [128, 512], F32)
        wstage = (wstg, wstgb, ptp)
        for b in range(B_LOC):
          with kb.scope():
            tiles = list(range(b * LT, (b + 1) * LT))
            Bt = kb.sbuf("ma_Bt", [16, L], F32)
            Ct = kb.sbuf("ma_Ct", [16, L], F32)
            with kb.scope():
              xcT = kb.sbuf("ma_xcT", [128, EC * L], BF16)
              with kb.scope():
                xnT = kb.sbuf("ma_xnT", [128, 8 * L], BF16)
                ln_tiles_to_fm(kb, x_src, xnT, tiles, L, ident, g_bcast=gB,
                               b_bcast=bB, prefix=f"ma_ln{b}", tp=ptp)
                ztmp = kb.sbuf("ma_ztmp", [128, 512], BF16)
                for half in range(2):
                  with kb.scope():
                    wT = kb.sbuf(f"ma_winT{half}", [128, 8 * 2048], BF16)
                    load_w_T(kb, "ma_win", W["L1_attn_a_W_in"].ap()
                             [half * 2048:(half + 1) * 2048, :],
                             2048, 1024, ident, wT=wT, stage=wstage)
                    for co in range(EC):
                        for ti2, t0 in enumerate(range(0, L, 512)):
                            tw = min(512, L - t0)
                            ps = pp if ti2 % 2 == 0 else pp2
                            for ci in range(8):
                                kb.mm(ps[:, 0:tw],
                                      wT[:, ci * 2048 + co * 128: ci * 2048 + (co + 1) * 128],
                                      xnT[:, ci * L + t0: ci * L + t0 + tw],
                                      start=(ci == 0), stop=(ci == 7))
                            if half == 0:
                                kb.ts(xcT[:, co * L + t0: co * L + t0 + tw],
                                      ps[:, 0:tw], b_in[:, co:co + 1], None, ALU.add)
                            else:
                                kb.ts(ztmp[:, 0:tw], ps[:, 0:tw],
                                      b_in[:, 16 + co:17 + co], None, ALU.add)
                                kb.dma(z_d.ap()[co][:, t0:t0 + tw], ztmp[:, 0:tw],
                                       stream="mzst", queue="gpsimd")
              with kb.scope():
                xpad = kb.sbuf("ma_xpad", [128, L + 14], BF16)
                cacc = kb.sbuf("ma_cacc", [128, L], F32)
                for c in range(EC):
                    kb.memset(xpad[:, 0:14], 0.0)
                    kb.copy(xpad[:, 14:L + 14], xcT[:, c * L:(c + 1) * L],
                            chan="pool")
                    kb.ts(cacc[:], xpad[:, 0:L], cw[:, c * 15:c * 15 + 1],
                          None, ALU.mult)
                    for j in range(1, 15):
                        kb.stt(cacc[:], xpad[:, j:j + L], cw[:, c * 15 + j:c * 15 + j + 1],
                               cacc[:], ALU.mult, ALU.add,
                               chan=("dve" if j % 2 else "pool"))
                    kb.act(xcT[:, c * L:(c + 1) * L], cacc[:], AF.Silu,
                           bias=cb[:, c:c + 1])
                    kb.dma(xc_d.ap()[c], xcT[:, c * L:(c + 1) * L],
                           stream="mxst", queue="gpsimd")
              with kb.scope():
                wxT = kb.sbuf("ma_wxT", [128, EC * 96], BF16)
                kb.dma(wstg[:96, 0:2048], W["L1_attn_a_W_x"].ap(), stream="wld")
                kb.copy(wstgb[:96, 0:2048], wstg[:96, 0:2048], chan="pool")
                for c in range(EC):
                    kb.trans(ptp[:, 0:96], wstgb[0:96, c * 128:(c + 1) * 128],
                             ident[0:96, 0:96])
                    kb.copy(wxT[:, c * 96:(c + 1) * 96], ptp[:, 0:96])
                dtl = kb.sbuf("ma_dtl", [64, L], F32)
                for ti2, t0 in enumerate(range(0, L, 512)):
                    tw = min(512, L - t0)
                    for lo, hi, dst in ((0, 64, dtl), (64, 80, Bt), (80, 96, Ct)):
                        ps = pp
                        for ci in range(EC):
                            kb.mm(ps[0:hi - lo, 0:tw],
                                  wxT[:, ci * 96 + lo: ci * 96 + hi],
                                  xcT[:, ci * L + t0: ci * L + t0 + tw],
                                  start=(ci == 0), stop=(ci == EC - 1))
                        kb.copy(dst[:, t0:t0 + tw], ps[0:hi - lo, 0:tw])
                wdtT = kb.sbuf("ma_wdtT", [64, EC * 128], BF16)
                for c4 in range(EC // 4):
                    for j in range(4):
                        c = c4 * 4 + j
                        kb.dma(wstg[:, 0:64],
                               W["L1_attn_a_W_dt"].ap()[c * 128:(c + 1) * 128, :],
                               stream="wld")
                        kb.copy(wstgb[:, j * 64:(j + 1) * 64], wstg[:, 0:64],
                                chan="pool")
                    for j in range(4):
                        c = c4 * 4 + j
                        kb.trans(ptp[:, 0:128][0:64, :] if False else ptp[0:64, j * 128:(j + 1) * 128],
                                 wstgb[:, j * 64:(j + 1) * 64], ident)
                    for j in range(4):
                        c = c4 * 4 + j
                        kb.copy(wdtT[:, c * 128:(c + 1) * 128],
                                ptp[0:64, j * 128:(j + 1) * 128])
                dtlb = kb.sbuf("ma_dtlb", [64, L], BF16)
                kb.copy(dtlb[:], dtl[:])
                spx = kb.sbuf("ma_spx", [128, 512], F32)
                spa = kb.sbuf("ma_spa", [128, 512], F32)
                dchunk = kb.sbuf("ma_dch", [128, L], BF16)
                dxchunk = kb.sbuf("ma_dxch", [128, L], BF16)
                for c in range(EC):
                    for ti2, t0 in enumerate(range(0, L, 512)):
                        tw = min(512, L - t0)
                        ps = pp if ti2 % 2 == 0 else pp2
                        kb.mm(ps[:, 0:tw], wdtT[:, c * 128:(c + 1) * 128],
                              dtlb[:, t0:t0 + tw], start=True, stop=True)
                        kb.ts(spx[:, 0:tw], ps[:, 0:tw], b_dt[:, c:c + 1],
                              None, ALU.add)
                        kb.act(spa[:, 0:tw], spx[:, 0:tw], AF.Abs)
                        kb.act(spa[:, 0:tw], spa[:, 0:tw], AF.Exp, scale=-1.0)
                        kb.act(spa[:, 0:tw], spa[:, 0:tw], AF.Ln, bias=1.0)
                        kb.ts(spx[:, 0:tw], spx[:, 0:tw], 0.0, None, ALU.max)
                        kb.tadd(dchunk[:, t0:t0 + tw], spa[:, 0:tw], spx[:, 0:tw])
                    kb.tmul(dxchunk[:], dchunk[:], xcT[:, c * L:(c + 1) * L],
                            chan="pool")
                    kb.dma(dl_d.ap()[c], dchunk[:], stream="mdst", queue="gpsimd")
                    kb.dma(dx_d.ap()[c], dxchunk[:], stream="mdst", queue="gpsimd")
            with kb.scope():
                yacc = kb.sbuf("ma_yacc", [128, EC * L], BF16)
                Brow = kb.sbuf("ma_Brow", [1, L], BF16)
                Crow = kb.sbuf("ma_Crow", [1, L], BF16)
                Bb = kb.sbuf("ma_Bb", [128, L], BF16)
                Cb = kb.sbuf("ma_Cb", [128, L], BF16)
                dl_c = [kb.sbuf(f"ma_dlc{i}", [128, L], BF16) for i in range(2)]
                dx_c = [kb.sbuf(f"ma_dxc{i}", [128, L], BF16) for i in range(2)]
                dA = kb.sbuf("ma_dA", [128, L], BF16)
                dxB = kb.sbuf("ma_dxB", [128, L], BF16)
                hsc = kb.sbuf("ma_h", [128, L], BF16)
                hC = kb.sbuf("ma_hC", [128, L], BF16)
                for n in range(NS):
                    kb.dma(Brow[:], Bt[n:n + 1, :], stream="mrow", queue="gpsimd")
                    kb.dma(Crow[:], Ct[n:n + 1, :], stream="mrow", queue="gpsimd")
                    for j, t0 in enumerate(range(0, L, 512)):
                        tw = min(512, L - t0)
                        kb.mm(pp[:, 0:tw], ones1, Brow[:, t0:t0 + tw],
                              start=True, stop=True)
                        kb.mm(pp2[:, 0:tw], ones1, Crow[:, t0:t0 + tw],
                              start=True, stop=True)
                        kb.copy(Bb[:, t0:t0 + tw], pp[:, 0:tw])
                        kb.copy(Cb[:, t0:t0 + tw], pp2[:, 0:tw], chan="act")
                    for c in range(EC):
                        dlc, dxc = dl_c[c % 2], dx_c[c % 2]
                        kb.dma(dlc[:], dl_d.ap()[c], stream="mld")
                        kb.dma(dxc[:], dx_d.ap()[c], stream="mld2", queue="gpsimd")
                        kb.act(dA[:], dlc[:], AF.Exp, scale=A_sb[:, c * NS + n: c * NS + n + 1])
                        kb.tmul(dxB[:], dxc[:], Bb[:])
                        kb.scan(hsc[:], dA[:], dxB[:], 0.0)
                        kb.tmul(hC[:], hsc[:], Cb[:], chan="pool")
                        if n == 0:
                            kb.copy(yacc[:, c * L:(c + 1) * L], hC[:], chan="pool")
                        else:
                            kb.tadd(yacc[:, c * L:(c + 1) * L],
                                    yacc[:, c * L:(c + 1) * L], hC[:], chan="pool")
                zc = kb.sbuf("ma_zc", [128, L], BF16)
                prc = kb.sbuf("ma_prc", [128, L], BF16)
                for c in range(EC):
                    kb.dma(dl_c[0][:], xc_d.ap()[c], stream="mld")
                    kb.stt(yacc[:, c * L:(c + 1) * L], dl_c[0][:], Dp[:, c:c + 1],
                           yacc[:, c * L:(c + 1) * L], ALU.mult, ALU.add)
                    kb.dma(zc[:], z_d.ap()[c], stream="mld")
                    kb.act(zc[:], zc[:], AF.Silu)
                    kb.tmul(prc[:], yacc[:, c * L:(c + 1) * L], zc[:])
                    kb.dma(pr_d.ap()[c], prc[:], stream="mpst", queue="gpsimd")
            with kb.scope():
                woT = kb.sbuf("ma_woT", [128, EC * 1024], BF16)
                load_w_T(kb, "ma_wo", W["L1_attn_a_W_out"], 1024, 2048, ident,
                         wT=woT, stage=wstage)
                prod = kb.sbuf("ma_prod", [128, EC * L], BF16)
                for c in range(EC):
                    kb.dma(prod[:, c * L:(c + 1) * L], pr_d.ap()[c], stream="mld")
                otm = kb.sbuf("ma_otm", [128, 1024], F32)
                xt2 = kb.sbuf("ma_xt2", [128, 1024], F32)
                for qt in range(LT):
                    for oh in range(2):
                        ps = pp if oh == 0 else pp2
                        for ci in range(EC):
                            kb.mm(ps[:], prod[:, ci * L + qt * 128: ci * L + (qt + 1) * 128],
                                  woT[:, ci * 1024 + oh * 512: ci * 1024 + (oh + 1) * 512],
                                  start=(ci == 0), stop=(ci == EC - 1))
                        kb.stt(otm[:, oh * 512:(oh + 1) * 512], ps[:], 1.0,
                               boB[:, oh * 512:(oh + 1) * 512], ALU.mult, ALU.add)
                    ti = b * LT + qt
                    kb.dma(xt2[:], x_src[ti * 128:(ti + 1) * 128, :], stream="xld2")
                    kb.tadd(otm[:], otm[:], xt2[:])
                    kb.dma(xt2[:], x_orig[ti * 128:(ti + 1) * 128, :], stream="xld2")
                    kb.tadd(otm[:], otm[:], xt2[:])
                    kb.dma(xres[ti * 128:(ti + 1) * 128, :], otm[:], stream="xst",
                           queue="gpsimd")
        kb.barrier()


def load_vec_chunks_pad(kb, name, src_ap, n, stream="ld"):
    nch = (n + 127) // 128
    t = kb.sbuf(name, [128, nch], F32)
    for c in range(nch):
        r = min(128, n - c * 128)
        kb.dma(t[0:r, c:c + 1], src_ap[c * 128:c * 128 + r].unsqueeze(1),
               stream=stream)
    return t


def shifted_tile_loader(kb, y_ap, xt, ti, first_of_batch, stream="xld"):
    t0 = ti * 128
    stream = stream + str(ti % 2)
    kb.dma(xt[:, 0:512], y_ap[t0:t0 + 128, 0:512], stream=stream)
    if first_of_batch:
        kb.memset(xt[0:1, 512:1024], 0.0, chan="pool")
        kb.dma(xt[1:128, 512:1024], y_ap[t0:t0 + 127, 512:1024], stream=stream)
    else:
        kb.dma(xt[:, 512:1024], y_ap[t0 - 1:t0 + 127, 512:1024], stream=stream)


def tiles_to_fm(kb, hT, tiles, Tcols, ident, loader, ln_affine=None,
                prefix="tf", tp=None, eps=1e-5):
    with kb.scope():
        bufs = []
        for _i in range(2):
            bufs.append([kb.sbuf(prefix + "_xt", [128, 1024], F32),
                         kb.sbuf(prefix + "_xn", [128, 1024], BF16),
                         kb.sbuf(prefix + "_mneg", [128, 1], F32),
                         kb.sbuf(prefix + "_ss", [128, 1], F32),
                         kb.sbuf(prefix + "_sq", [128, 1024], F32),
                         kb.sbuf(prefix + "_var", [128, 1], F32),
                         kb.sbuf(prefix + "_rstd", [128, 1], F32)])
        if tp is None:
            tps = [kb.psum(prefix + "_tp", [128, 512], BF16) for _ in range(2)]
        elif isinstance(tp, list):
            tps = tp
        else:
            tps = [tp, tp]
        if ln_affine is not None:
            g_bcast, b_bcast = ln_affine
        for j, ti in enumerate(tiles):
            xt, xn, mneg, ssum, sq, var, rstd = bufs[j % 2]
            tp = tps[j % 2]
            loader(kb, xt, ti, j)
            if ln_affine is not None:
                kb.red(mneg[:], xt[:], ALU.add, negate=True)
                kb.ts(mneg[:], mneg[:], 1.0 / 1024, None, ALU.mult)
                kb.act(sq[:], xt[:], AF.Square, bias=mneg[:], accum_out=ssum[:])
                kb.ts(var[:], ssum[:], 1.0 / 1024, eps, ALU.mult, ALU.add)
                kb.act(rstd[:], var[:], AF.Ln)
                kb.act(rstd[:], rstd[:], AF.Exp, scale=-0.5)
                kb.ts(sq[:], xt[:], mneg[:], rstd[:], ALU.add, ALU.mult)
                kb.tmul(sq[:], sq[:], g_bcast[:])
                kb.tadd(xn[:], sq[:], b_bcast[:])
            else:
                kb.copy(xn[:], xt[:])
            for c4 in range(2):
                for j2 in range(4):
                    c = c4 * 4 + j2
                    kb.trans(tp[:, j2 * 128:(j2 + 1) * 128],
                             xn[:, c * 128:(c + 1) * 128], ident)
                for j2 in range(4):
                    c = c4 * 4 + j2
                    kb.copy(hT[:, c * Tcols + j * 128: c * Tcols + (j + 1) * 128],
                            tp[:, j2 * 128:(j2 + 1) * 128])


def build_l1_ffn(kb, x_src, xres, consts, W, tag):
    nc = kb.nc
    ident = consts["ident"]
    ones1 = consts["ones1"]
    tri = consts["tri"]
    LT = L // 128
    T = B_LOC * L
    NT = T // 128
    y_d = nc.dram_tensor(f"lf_y{tag}", [T, D], F32, kind="Internal")
    qk_d = nc.dram_tensor(f"lf_qk{tag}", [3, L, D], BF16, kind="Internal")
    with kb.scope():
        pp = kb.psum("lf_pp", [128, 512], F32)
        pp2 = kb.psum("lf_pp2", [128, 512], F32)
        ptp = kb.psum("lf_ptp", [128, 512], BF16)
        bp = kb.psum("lf_bp", [128, 512], F32)
        with kb.scope():
            xt = kb.sbuf("lf_xt", [128, 1024], F32)
            yt = kb.sbuf("lf_yt", [128, 1024], F32)
            mneg = kb.sbuf("lf_mneg", [128, 1], F32)
            ssum = kb.sbuf("lf_ss", [128, 1], F32)
            sq = kb.sbuf("lf_sq", [128, 1024], F32)
            var = kb.sbuf("lf_var", [128, 1], F32)
            rstd = kb.sbuf("lf_rstd", [128, 1], F32)
            for ti in range(NT):
                kb.dma(xt[:], x_src[ti * 128:(ti + 1) * 128, :], stream="xld")
                kb.red(mneg[:], xt[:], ALU.add, negate=True)
                kb.ts(mneg[:], mneg[:], 1.0 / 1024, None, ALU.mult)
                kb.act(sq[:], xt[:], AF.Square, bias=mneg[:], accum_out=ssum[:])
                kb.ts(var[:], ssum[:], 1.0 / 1024, 1e-5, ALU.mult, ALU.add)
                kb.act(rstd[:], var[:], AF.Ln)
                kb.act(rstd[:], rstd[:], AF.Exp, scale=-0.5)
                kb.ts(yt[:], xt[:], mneg[:], rstd[:], ALU.add, ALU.mult)
                kb.dma(y_d.ap()[ti * 128:(ti + 1) * 128, :], yt[:], stream="yst",
                       queue="gpsimd")
        for li in range(6):
            with kb.scope():
                qsB = _row_bcast_from_hbm(kb, "lf_qs", None, 1024, ones1, bp, dtype=BF16,
                                          src_ap=W["L1_ffn_q_scale"].ap()[li],
                                          repeat=16)
                ksB = _row_bcast_from_hbm(kb, "lf_ks", None, 1024, ones1, bp, dtype=BF16,
                                          src_ap=W["L1_ffn_k_scale"].ap()[li],
                                          repeat=16)
                gB = _row_bcast_from_hbm(kb, "lf_g", None, 1024, ones1, bp, dtype=BF16,
                                         src_ap=W["L1_ffn_ln_g"].ap()[li])
                bB = _row_bcast_from_hbm(kb, "lf_b", None, 1024, ones1, bp, dtype=BF16,
                                         src_ap=W["L1_ffn_ln_b"].ap()[li])
                b2B = _row_bcast_from_hbm(kb, "lf_b2", None, 1024, ones1, bp, dtype=BF16,
                                          src_ap=W["L1_ffn_b2"].ap()[li])
                b1c = (load_vec_chunks_pad(kb, "lf_b1a",
                                           W["L1_ffn_b1"].ap()[li][0:2730], 2730),
                       load_vec_chunks_pad(kb, "lf_b1g",
                                           W["L1_ffn_b1"].ap()[li][2730:5460], 2730))
                for b in range(B_LOC):
                    tiles = list(range(b * LT, (b + 1) * LT))
                    _l1ffn_attn(kb, y_d, qk_d, consts, W, li, b, tiles, qsB,
                                ksB, pp, pp2, ptp)
                    _l1ffn_ffn(kb, y_d, consts, W, li, b, tiles, gB, bB, b1c,
                               b2B, pp, pp2, ptp)
        with kb.scope():
            xt = kb.sbuf("lf_ext", [128, 1024], F32)
            yt = kb.sbuf("lf_eyt", [128, 1024], F32)
            for ti in range(NT):
                kb.dma(xt[:], x_src[ti * 128:(ti + 1) * 128, :], stream="xld")
                kb.dma(yt[:], y_d.ap()[ti * 128:(ti + 1) * 128, :], stream="yld")
                kb.tadd(yt[:], yt[:], xt[:], chan="pool")
                kb.dma(xres[ti * 128:(ti + 1) * 128, :], yt[:], stream="xst",
                       queue="gpsimd")
        kb.barrier()


def _l1ffn_attn(kb, y_d, qk_d, consts, W, li, b, tiles, qsB, ksB, pp, pp2, ptp):
    ident = consts["ident"]
    tri = consts["tri"]
    LT = L // 128
    with kb.scope():
        QhT = kb.sbuf("lfa_QhT", [128, 8 * L], BF16)
        KhT = kb.sbuf("lfa_KhT", [128, 8 * L], BF16)
        with kb.scope():
            with kb.scope():
                xsT = kb.sbuf("lfa_xsT", [128, 8 * L], BF16)
                ld = lambda kb_, xt, ti, j: shifted_tile_loader(
                    kb_, y_d.ap(), xt, ti, j == 0)
                tiles_to_fm(kb, xsT, tiles, L, ident, ld, prefix=f"lfa_tf{li}{b}",
                            tp=ptp)
                wq = kb.sbuf("lfa_wq", [128, 8 * 1024], BF16)
                wstg = kb.sbuf("lfa_wstg", [128, 1024], F32)
                qtile = kb.sbuf("lfa_qtile", [128, 1024], BF16)
                for part in range(3):
                    load_w_direct(kb, "lfa_w",
                                  W["L1_ffn_Wqkv"].ap()[li][:, part * 1024:(part + 1) * 1024],
                                  1024, 1024, wsb=wq, stage=wstg)
                    for tt_ in range(LT):
                        for oh in range(2):
                            ps = pp if oh == 0 else pp2
                            for ci in range(8):
                                kb.mm(ps[:], xsT[:, ci * L + tt_ * 128: ci * L + (tt_ + 1) * 128],
                                      wq[:, ci * 1024 + oh * 512: ci * 1024 + (oh + 1) * 512],
                                      start=(ci == 0), stop=(ci == 7))
                            kb.copy(qtile[:, oh * 512:(oh + 1) * 512], ps[:])
                        kb.dma(qk_d.ap()[part][tt_ * 128:(tt_ + 1) * 128, :],
                               qtile[:], stream="qkst", queue="gpsimd")
            with kb.scope():
                cs = kb.sbuf("lfa_cs", [128, 1024], BF16)
                ss = kb.sbuf("lfa_ss", [128, 1024], BF16)
                sq = kb.sbuf("lfa_sq", [128, 1024], F32)
                ssq = kb.sbuf("lfa_ssq", [128, 16], F32)
                rn = kb.sbuf("lfa_rn", [128, 16], F32)
                sw = kb.sbuf("lfa_sw", [128, 1024], F32)
                t1 = kb.sbuf("lfa_t1", [128, 1024], F32)
                qb = kb.sbuf("lfa_qb", [128, 1024], BF16)
                qld = kb.sbuf("lfa_qld", [128, 1024], BF16)
                for j, ti in enumerate(tiles):
                    for which, sc, tabc, tabs, hdst in (
                            (0, qsB, "c_qcs", "c_qss", QhT),
                            (1, ksB, "c_kcs", "c_kss", KhT)):
                        kb.dma(qld[:], qk_d.ap()[which][j * 128:(j + 1) * 128, :],
                               stream="qkld")
                        q = qld[:, :]
                        kb.dma(cs[:], consts[tabc][j * 128:(j + 1) * 128, :],
                               stream="rtl")
                        kb.dma(ss[:], consts[tabs][j * 128:(j + 1) * 128, :],
                               stream="rtl")
                        kb.act(sq[:], q, AF.Square)
                        kb.red(ssq[:], sq[:, :].rearrange("p (h e) -> p h e", h=16),
                               ALU.add)
                        kb.act(rn[:], ssq[:], AF.Sqrt)
                        kb.ts(rn[:], rn[:], 1e-12, None, ALU.max)
                        kb.recip(rn[:], rn[:])
                        rnb = rn[:, :].unsqueeze(2).broadcast_to((128, 16, 64))
                        kb.op("dve", lambda e: e.tensor_tensor(
                            t1[:, :].rearrange("p (h e) -> p h e", h=16),
                            q.rearrange("p (h e) -> p h e", h=16), rnb, ALU.mult),
                            [q, rn[:]], [t1[:]])
                        kb.tmul(t1[:], t1[:], sc[:])
                        swv = sw[:, :].rearrange("p (h v e) -> p h v e", h=16, v=2)
                        t1v = t1[:, :].rearrange("p (h v e) -> p h v e", h=16, v=2)
                        kb.op("dve", lambda e: e.tensor_copy(swv[:, :, 0, :], t1v[:, :, 1, :]),
                              [t1[:]], [sw[:]])
                        kb.op("dve", lambda e: e.tensor_copy(swv[:, :, 1, :], t1v[:, :, 0, :]),
                              [t1[:]], [sw[:]])
                        kb.tmul(sw[:], sw[:], ss[:])
                        kb.tt(qb[:], t1[:], cs[:], ALU.mult)
                        kb.op("dve", lambda e: e.scalar_tensor_tensor(
                            qb[:], sw[:], 1.0, qb[:], ALU.mult, ALU.add),
                            [sw[:], qb[:]], [qb[:]])
                        for c4 in range(2):
                            for j2 in range(4):
                                c = c4 * 4 + j2
                                kb.trans(ptp[:, j2 * 128:(j2 + 1) * 128],
                                         qb[:, c * 128:(c + 1) * 128], ident)
                            for j2 in range(4):
                                c = c4 * 4 + j2
                                kb.copy(hdst[:, c * L + j * 128: c * L + (j + 1) * 128],
                                        ptp[:, j2 * 128:(j2 + 1) * 128])
        aT = kb.sbuf("lfa_aT", [128, 8 * L], BF16)
        with kb.scope():
            vwin = kb.sbuf("lfa_vwin", [128, 4 * 1024], BF16)
            Sall = kb.sbuf("lfa_S", [128, 16 * 512], F32)
            Pall = kb.sbuf("lfa_P", [128, 16 * 512], BF16)
            negm = kb.sbuf("lfa_negm", [128, 16], F32)
            rs = kb.sbuf("lfa_rs", [128, 16], F32)
            PT = kb.sbuf("lfa_PT", [128, 512], BF16)
            pav = kb.psum("lfa_pav", [64, 128], F32)
            vloaded = set()
            for qt in range(LT):
                w = qt // 2
                kt0 = 0 if w == 0 else (w - 1) * 2
                nkt = qt - kt0 + 1
                kw = nkt * 128
                for kt in range(kt0, qt + 1):
                    if kt not in vloaded:
                        kb.dma(vwin[:, (kt % 4) * 1024:(kt % 4 + 1) * 1024],
                               qk_d.ap()[2][kt * 128:(kt + 1) * 128, :],
                               stream="vld")
                        vloaded.add(kt)
                for h in range(16):
                    ci, pb = h // 2, (h % 2) * 64
                    kb.mm(pp[:, 0:kw],
                          QhT[:, ci * L + qt * 128: ci * L + (qt + 1) * 128][pb:pb + 64, :],
                          KhT[:, ci * L + kt0 * 128: ci * L + kt0 * 128 + kw][pb:pb + 64, :],
                          start=True, stop=True)
                    kb.copy(Sall[:, h * 512:h * 512 + kw], pp[:, 0:kw])
                sv = Sall[:, :].rearrange("p (h k) -> p h k", h=16)[:, :, 0:kw]
                dv = Sall[:, :].rearrange("p (h k) -> p h k", h=16)[:, :, kw - 128:kw]
                trib = tri.unsqueeze(1).broadcast_to((128, 16, 128))
                kb.op("dve", lambda e: e.tensor_tensor(dv, dv, trib, ALU.add),
                      [Sall[:], tri], [Sall[:]])
                kb.red(negm[:], sv, ALU.max, negate=True)
                nmb = negm[:, :].unsqueeze(2).broadcast_to((128, 16, kw))
                kb.op("dve", lambda e: e.tensor_tensor(sv, sv, nmb, ALU.add),
                      [Sall[:], negm[:]], [Sall[:]])
                pv = Pall[:, :].rearrange("p (h k) -> p h k", h=16)[:, :, 0:kw]
                kb.act(pv, sv, AF.Exp, scale=8.0)
                kb.red(rs[:], pv, ALU.add)
                kb.recip(rs[:], rs[:])
                rsb = rs[:, :].unsqueeze(2).broadcast_to((128, 16, kw))
                kb.op("dve", lambda e: e.tensor_tensor(pv, pv, rsb, ALU.mult),
                      [Pall[:], rs[:]], [Pall[:]])
                for h in range(16):
                    ci, pb = h // 2, (h % 2) * 64
                    for j2 in range(nkt):
                        kb.trans(ptp[:, j2 * 128:(j2 + 1) * 128],
                                 Pall[:, h * 512 + j2 * 128: h * 512 + (j2 + 1) * 128],
                                 ident)
                    for j2 in range(nkt):
                        kb.copy(PT[:, j2 * 128:(j2 + 1) * 128],
                                ptp[:, j2 * 128:(j2 + 1) * 128])
                    for j2 in range(nkt):
                        kt = kt0 + j2
                        kb.mm(pav[:], vwin[:, (kt % 4) * 1024 + h * 64: (kt % 4) * 1024 + (h + 1) * 64],
                              PT[:, j2 * 128:(j2 + 1) * 128],
                              start=(j2 == 0), stop=(j2 == nkt - 1))
                    kb.copy(aT[:, ci * L + qt * 128: ci * L + (qt + 1) * 128][pb:pb + 64, :],
                            pav[:])
        with kb.scope():
            wo = kb.sbuf("lfa_wo", [128, 8 * 1024], BF16)
            wstg = kb.sbuf("lfa_wstg2", [128, 1024], F32)
            otm = kb.sbuf("lfa_otm", [128, 1024], F32)
            yt2 = kb.sbuf("lfa_yt2", [128, 1024], F32)
            load_w_direct(kb, "lfa_w2", W["L1_ffn_Wo"].ap()[li], 1024, 1024,
                          wsb=wo, stage=wstg)
            for j, ti in enumerate(tiles):
                for oh in range(2):
                    ps = pp if oh == 0 else pp2
                    for ci in range(8):
                        kb.mm(ps[:], aT[:, ci * L + j * 128: ci * L + (j + 1) * 128],
                              wo[:, ci * 1024 + oh * 512: ci * 1024 + (oh + 1) * 512],
                              start=(ci == 0), stop=(ci == 7))
                    kb.copy(otm[:, oh * 512:(oh + 1) * 512], ps[:])
                kb.dma(yt2[:], y_d.ap()[ti * 128:(ti + 1) * 128, :], stream="yld")
                kb.tadd(otm[:], otm[:], yt2[:], chan="pool")
                kb.dma(y_d.ap()[ti * 128:(ti + 1) * 128, :], otm[:], stream="yst",
                       queue="gpsimd")
    kb.barrier()


def _l1ffn_ffn(kb, y_d, consts, W, li, b, tiles, gB, bB, b1c, b2B, pp, pp2, ptp):
    ident = consts["ident"]
    LT = L // 128
    NI = 2730
    OC = 22
    b1a, b1g = b1c
    with kb.scope():
        xnT = kb.sbuf("lff_xnT", [128, 8 * L], BF16)
        ld = lambda kb_, xt, ti, j: shifted_tile_loader(kb_, y_d.ap(), xt, ti,
                                                        j == 0)
        with kb.scope():
            tiles_to_fm(kb, xnT, tiles, L, ident, ld, ln_affine=(gB, bB),
                        prefix=f"lff_tf{li}{b}", tp=ptp)
        otacc = kb.sbuf("lff_otacc", [128, (L // 128) * 1024], BF16)
        otm = kb.sbuf("lff_otm", [128, 1024], F32)
        yt2 = kb.sbuf("lff_yt2", [128, 1024], F32)
        pd = kb.psum("lff_pd", [128, 512], F32)
        for hf in range(2):
            oc0 = hf * 11
            nmm = sum(1 for oc in range(11) if NI - (oc0 + oc) * 128 > 0)
            with kb.scope():
                w1h = kb.sbuf("lff_w1h", [128, 8 * 2816], BF16)
                w2h = kb.sbuf("lff_w2h", [128, 11 * 1024], BF16)
                with kb.scope():
                    wstg = kb.sbuf("lff_wstg", [128, 1408], F32)
                    wid = min(1408, NI - oc0 * 128)
                    for ci in range(8):
                        for gpart, off in ((0, oc0 * 128), (1, NI + oc0 * 128)):
                            kb.dma(wstg[:, 0:wid],
                                   W["L1_ffn_W1"].ap()[li][ci * 128:(ci + 1) * 128,
                                                           off:off + wid],
                                   stream="wld")
                            kb.copy(w1h[:, ci * 2816 + gpart * 1408:
                                        ci * 2816 + gpart * 1408 + wid],
                                    wstg[:, 0:wid], chan="pool")
                    for oc in range(nmm):
                        g_oc = oc0 + oc
                        r = min(128, NI - g_oc * 128)
                        kb.dma(wstg[0:r, 0:1024],
                               W["L1_ffn_W2"].ap()[li][g_oc * 128:g_oc * 128 + r, :],
                               stream="wld")
                        kb.copy(w2h[0:r, oc * 1024:(oc + 1) * 1024],
                                wstg[0:r, 0:1024], chan="pool")
                prodT = kb.sbuf("lff_prod", [128, 11 * 512], BF16)
                ga = kb.sbuf("lff_ga", [128, 512], BF16)
                for t0 in range(0, L, 512):
                    tc = t0 // 512
                    tw = min(512, L - t0)
                    for oc in range(nmm):
                        g_oc = oc0 + oc
                        r = min(128, NI - g_oc * 128)
                        for ci in range(8):
                            kb.mm(pp[0:r, 0:tw],
                                  w1h[:, ci * 2816 + 1408 + oc * 128: ci * 2816 + 1408 + oc * 128 + r],
                                  xnT[:, ci * L + t0: ci * L + t0 + tw],
                                  start=(ci == 0), stop=(ci == 7))
                        for ci in range(8):
                            kb.mm(pp2[0:r, 0:tw],
                                  w1h[:, ci * 2816 + oc * 128: ci * 2816 + oc * 128 + r],
                                  xnT[:, ci * L + t0: ci * L + t0 + tw],
                                  start=(ci == 0), stop=(ci == 7))
                        kb.op("act", lambda e: e.activation(
                            ga[0:r, 0:tw], pp[0:r, 0:tw], AF.Gelu,
                            bias=b1g[0:r, g_oc:g_oc + 1]), [pp[:], b1g[:]], [ga[:]])
                        kb.op("dve", lambda e: e.scalar_tensor_tensor(
                            prodT[0:r, oc * 512:oc * 512 + tw], pp2[0:r, 0:tw],
                            b1a[0:r, g_oc:g_oc + 1], ga[0:r, 0:tw], ALU.add, ALU.mult),
                            [pp2[:], ga[:], b1a[:]], [prodT[:]])
                    for tt_ in range(tw // 128):
                        gt = tc * 4 + tt_
                        for oh in range(2):
                            for oc in range(nmm):
                                g_oc = oc0 + oc
                                r = min(128, NI - g_oc * 128)
                                kb.mm(pd[:],
                                      prodT[0:r, oc * 512 + tt_ * 128: oc * 512 + (tt_ + 1) * 128],
                                      w2h[0:r, oc * 1024 + oh * 512: oc * 1024 + (oh + 1) * 512],
                                      start=(oc == 0), stop=(oc == nmm - 1))
                            if hf == 0:
                                kb.copy(otacc[:, gt * 1024 + oh * 512: gt * 1024 + (oh + 1) * 512],
                                        pd[:])
                            else:
                                kb.tadd(otacc[:, gt * 1024 + oh * 512: gt * 1024 + (oh + 1) * 512],
                                        otacc[:, gt * 1024 + oh * 512: gt * 1024 + (oh + 1) * 512],
                                        pd[:])
        for gt in range(L // 128):
            ti = tiles[gt]
            kb.dma(yt2[:], y_d.ap()[ti * 128:(ti + 1) * 128, :], stream="yld")
            kb.tadd(otm[:], otacc[:, gt * 1024:(gt + 1) * 1024], b2B[:])
            kb.tadd(otm[:], otm[:], yt2[:], chan="pool")
            kb.dma(y_d.ap()[ti * 128:(ti + 1) * 128, :], otm[:], stream="yst",
                   queue="gpsimd")
    kb.barrier()



def input_specs():
    return [
        ("L0_attn_a_x", [B_LOC, L, D]), ("L0_attn_a_wq", [D, D]), ("L0_attn_a_bq", [D]),
        ("L0_attn_a_wk", [D, D]), ("L0_attn_a_bk", [D]), ("L0_attn_a_wv", [D, D]),
        ("L0_attn_a_bv", [D]),
        ("L0_attn_b_ln_att_g", [D]), ("L0_attn_b_ln_att_b", [D]),
        ("L0_attn_b_ln_ffn_g", [D]), ("L0_attn_b_ln_ffn_b", [D]),
        ("L0_attn_b_att_time_decay", [D]), ("L0_attn_b_att_time_first", [D]),
        ("L0_attn_b_att_mix_k", [D]), ("L0_attn_b_att_mix_v", [D]),
        ("L0_attn_b_att_mix_r", [D]),
        ("L0_attn_b_att_wk", [D, D]), ("L0_attn_b_att_wv", [D, D]),
        ("L0_attn_b_att_wr", [D, D]), ("L0_attn_b_att_wo", [D, D]),
        ("L0_attn_b_ffn_mix_k", [D]), ("L0_attn_b_ffn_mix_r", [D]),
        ("L0_attn_b_ffn_wk", [D, 4096]), ("L0_attn_b_ffn_wv", [4096, D]),
        ("L0_attn_b_ffn_wr", [D, D]),
        ("L0_ffn_gate_w", [D, 8]), ("L0_ffn_Wg", [8, D, 2816]),
        ("L0_ffn_Wu", [8, D, 2816]), ("L0_ffn_Wd", [8, 2816, D]),
        ("L0_ffn_Sg", [D, 1408]), ("L0_ffn_Su", [D, 1408]), ("L0_ffn_Sd", [1408, D]),
        ("L1_attn_a_gamma", [D]), ("L1_attn_a_beta", [D]),
        ("L1_attn_a_W_in", [4096, D]), ("L1_attn_a_b_in", [4096]),
        ("L1_attn_a_conv_w", [2048, 15]), ("L1_attn_a_conv_b", [2048]),
        ("L1_attn_a_W_x", [96, 2048]), ("L1_attn_a_W_dt", [2048, 64]),
        ("L1_attn_a_b_dt", [2048]), ("L1_attn_a_A_log", [2048, 16]),
        ("L1_attn_a_D_param", [2048]), ("L1_attn_a_W_out", [D, 2048]),
        ("L1_attn_a_b_out", [D]),
        ("L1_attn_b_W_q", [D, D]), ("L1_attn_b_b_q", [D]),
        ("L1_attn_b_W_k", [D, D]), ("L1_attn_b_b_k", [D]),
        ("L1_attn_b_W_v", [D, D]), ("L1_attn_b_b_v", [D]),
        ("L1_attn_b_W_o", [D, D]), ("L1_attn_b_b_o", [D]),
        ("L1_ffn_Wqkv", [6, D, 3072]), ("L1_ffn_q_scale", [6, 64]),
        ("L1_ffn_k_scale", [6, 64]), ("L1_ffn_Wo", [6, 1024, D]),
        ("L1_ffn_ln_g", [6, D]), ("L1_ffn_ln_b", [6, D]),
        ("L1_ffn_W1", [6, D, 5460]), ("L1_ffn_b1", [6, 5460]),
        ("L1_ffn_W2", [6, 2730, D]), ("L1_ffn_b2", [6, D]),
    ]


def declare_inputs(nc):
    W = {}
    for name, shape in input_specs():
        W[name] = nc.dram_tensor(name, shape, F32, kind="ExternalInput")
    return W


def build_consts(kb):
    nc = kb.nc
    c = {}
    ident_h = nc.dram_tensor("c_ident", [128, 128], BF16, kind="ExternalInput")
    ident = kb.sbuf("ident", [128, 128], BF16)
    kb.dma(ident[:], ident_h.ap(), stream="ld")
    ones1 = kb.sbuf("ones1", [1, 128], BF16)
    kb.memset(ones1[:], 1.0)
    tri_h = nc.dram_tensor("c_tri", [128, 128], F32, kind="ExternalInput")
    tri = kb.sbuf("tri", [128, 128], F32)
    kb.dma(tri[:], tri_h.ap(), stream="ld")
    c["ident"] = ident[:]
    c["ones1"] = ones1[:]
    c["tri"] = tri[:]
    for name in ("c_qcs", "c_qss", "c_kcs", "c_kss"):
        c[name] = nc.dram_tensor(name, [L, 1024], BF16, kind="ExternalInput").ap()
    return c


def all_stages():
    return [
        ("l0_attn_a", lambda kb, s, d, c, W: build_l0_attn_a(kb, s, d, c, W)),
        ("l0_ffn_0", lambda kb, s, d, c, W: build_l0_ffn(kb, s, d, c, W, "0")),
        ("l0_attn_b", lambda kb, s, d, c, W: build_l0_attn_b(kb, s, d, c, W)),
        ("l0_ffn_1", lambda kb, s, d, c, W: build_l0_ffn(kb, s, d, c, W, "1")),
        ("l1_attn_a", lambda kb, s, d, c, W: build_l1_attn_a(kb, s, d, c, W)),
        ("l1_ffn_0", lambda kb, s, d, c, W: build_l1_ffn(kb, s, d, c, W, "0")),
        ("l1_attn_b", lambda kb, s, d, c, W: build_l1_attn_b(kb, s, d, c, W)),
        ("l1_ffn_1", lambda kb, s, d, c, W: build_l1_ffn(kb, s, d, c, W, "1")),
    ]


def build_program(stages=None):
    nc = bass.Bass("TRN2", target_bir_lowering=False)
    es = contextlib.ExitStack()
    kb = KB(nc, es)
    W = declare_inputs(nc)
    T = B_LOC * L
    out = nc.dram_tensor("out", [T, D], F32, kind="ExternalOutput")
    xres = nc.dram_tensor("xres", [T, D], F32, kind="Internal")
    consts = build_consts(kb)
    x_in = W["L0_attn_a_x"].ap().rearrange("b l d -> (b l) d")
    stg = all_stages()
    if stages is not None:
        stg = [s for s in stg if s[0] in stages]
    src = x_in
    for i, (name, fn) in enumerate(stg):
        dst = out.ap() if i == len(stg) - 1 else xres.ap()
        fn(kb, src, dst, consts, W)
        src = dst
    es.close()
    return nc


def host_consts():
    import ml_dtypes
    c = {}
    c["c_ident"] = np.eye(128, dtype=np.float32).astype(ml_dtypes.bfloat16)
    ii = np.arange(128)
    c["c_tri"] = np.where(ii[None, :] <= ii[:, None], 0.0, -1e30).astype(np.float32)
    dh = 64
    half = np.arange(0, dh, 2, dtype=np.float64)
    inv_freq = 1.0 / 10000.0 ** (half / dh)
    t = np.arange(L, dtype=np.float64)
    freqs = np.concatenate([t[:, None] * inv_freq] * 2, -1)
    base_scale = (half + 0.4 * dh) / (1.4 * dh)
    power = (t - L // 2) / 128.0
    scale = np.concatenate([base_scale[None, :] ** power[:, None]] * 2, -1)
    sign = np.where(np.arange(dh) < 32, -1.0, 1.0)
    co, si = np.cos(freqs), np.sin(freqs)
    tile16 = lambda a: np.tile(a.astype(np.float32), (1, 16)).astype(ml_dtypes.bfloat16)
    c["c_qcs"] = tile16(co * scale)
    c["c_qss"] = tile16(si * scale * sign)
    c["c_kcs"] = tile16(co / scale)
    c["c_kss"] = tile16(si / scale * sign)
    return c


def kernel(**inputs):
    nc = build_program()
    consts = host_consts()
    in_maps = []
    for core in range(N_CORES):
        m = {}
        for k, v in inputs.items():
            if k == "L0_attn_a_x":
                m[k] = np.ascontiguousarray(v[core * B_LOC:(core + 1) * B_LOC])
            else:
                m[k] = np.ascontiguousarray(v)
        m.update(consts)
        in_maps.append(m)
    res = run_bass_kernel_spmd(nc, in_maps, core_ids=list(range(N_CORES)))
    out = np.concatenate([np.asarray(r["out"]).reshape(B_LOC, L, D)
                          for r in res.results], axis=0)
    return out
